# version 01
# baseline (speedup 1.0000x reference)
import os
import numpy as np
from contextlib import ExitStack

import concourse.bass as bass
import concourse.mybir as mybir
from concourse.bass_utils import run_bass_kernel_spmd
from concourse.bass import IndirectOffsetOnAxis

import ml_dtypes

FP32 = mybir.dt.float32
BF16 = mybir.dt.bfloat16
I16 = mybir.dt.int16
I32 = mybir.dt.int32
U32 = mybir.dt.uint32
AF = mybir.ActivationFunctionType
ALU = mybir.AluOpType
AX = mybir.AxisListType.X

NB = int(os.environ.get("KNB", "2"))
T = 2048
D = 1024
NT = T // 128
ND = D // 128
EPS_LN = 1e-5

ALL_BLOCKS = ("l0a", "l0f", "l1a", "l1f")


class KB:
    def __init__(self):
        import os
        self.nc = bass.Bass("TRN2", target_bir_lowering=False, debug=bool(os.environ.get("KDBG")))
        self.ctx = ExitStack()
        self.sems = {}
        self.dram = {}

    def dram_in(self, name, shape, dt=FP32):
        ap = self.nc.dram_tensor(name, list(shape), dt, kind="ExternalInput").ap()
        self.dram[name] = ap
        return ap

    def dram_out(self, name, shape, dt=FP32):
        ap = self.nc.dram_tensor(name, list(shape), dt, kind="ExternalOutput").ap()
        self.dram[name] = ap
        return ap

    def dram_tmp(self, name, shape, dt=FP32):
        ap = self.nc.dram_tensor(name, list(shape), dt).ap()
        self.dram[name] = ap
        return ap

    def sb(self, name, shape, dt=FP32):
        return self.ctx.enter_context(self.nc.sbuf_tensor(name, list(shape), dt))

    def ps(self, name, shape, dt=FP32):
        return self.ctx.enter_context(self.nc.psum_tensor(name, list(shape), dt))

    def sem(self, name):
        h = self.ctx.enter_context(self.nc.semaphore(name))
        self.sems[name] = [h, 0]
        return h

    def inc(self, instr, name, amt=1):
        h, c = self.sems[name]
        instr.then_inc(h, amt)
        self.sems[name][1] = c + amt
        return self.sems[name][1]

    def wait(self, eng, name, v=None):
        h, c = self.sems[name]
        eng.wait_ge(h, c if v is None else v)



def build_program(blocks=ALL_BLOCKS, dbg_taps=()):
    k = KB()
    nc = k.nc
    taps = {}

    x_in = k.dram_in("x_in", [NB, T, D], FP32)
    y_out = k.dram_out("y_out", [NB, T, D], FP32)

    wdecl = []
    def win(name, shape, dt=FP32):
        wdecl.append((name, shape, dt))
        return k.dram_in(name, shape, dt)

    c_ident = win("c_ident", [128, 128], FP32)
    c_identb = win("c_identb", [128, 128], BF16)
    c_ones_b = win("c_ones_b", [128, 512], BF16)
    c_iota_col = win("c_iota_col", [128, 1], FP32)
    c_eps = win("c_eps", [128, 1], FP32)

    if "l0f" in blocks or True:
        w_sW1 = win("w_sW1", [ND, 128, 1024])
        w_sW2 = win("w_sW2", [ND, 128, 1024])
        w_eW1 = win("w_eW1", [15, ND, 128, 1024])
        w_eW2 = win("w_eW2", [15, ND, 128, 1024])
        w_routerW = win("w_routerW", [ND, 128, 15])
        w_routerB = win("w_routerB", [128, 15])
        w_sb1c = win("w_sb1c", [128, ND])
        w_sb2r = win("w_sb2r", [1, D])
        w_eb1c = win("w_eb1c", [15, 128, ND])
        w_eb2 = win("w_eb2", [15, 1024])

    w_l0a_WinT = win("w_l0a_WinT", [ND, 128, 1024])
    w_l0a_Win = win("w_l0a_Win", [ND, 128, 1024])
    w_l0a_WoutT = win("w_l0a_WoutT", [ND, 128, 1024])
    w_l0a_Wc = win("w_l0a_Wc", [8, 128, 4])
    w_l0a_binc = win("w_l0a_binc", [128, ND])
    w_l0a_binsl = win("w_l0a_binsl", [ND, 128, 1])
    w_l0a_bout = win("w_l0a_bout", [1, D])
    w_l0a_A_rep = win("w_l0a_A_rep", [128, 672])
    w_l0a_base_cols = win("w_l0a_base_cols", [128, 168])
    w_l0a_alphas = win("w_l0a_alphas", [1, 8])
    w_l0a_Vcols = win("w_l0a_Vcols", [128, 32])
    c_diag = win("c_diag", [128, 41 * 128], BF16)
    o_dram = k.dram_tmp("o_dram", [8, 128, T], BF16)
    h1_dram = k.dram_tmp("h1_dram", [32, 128, T], BF16)

    w_l1f_W1T = win("w_l1f_W1T", [ND, 128, 4096])
    w_l1f_W2T = win("w_l1f_W2T", [32, 128, 1024])
    w_l1f_routerWT = win("w_l1f_routerWT", [ND, 128, 512])
    w_l1f_routerBr = win("w_l1f_routerBr", [1, 512])
    w_l1f_rawUT = win("w_l1f_rawUT", [ND, 128, 512])
    w_l1f_rawU_rows = win("w_l1f_rawU_rows", [4, 128, 1024])
    w_l1f_rawV_rows = win("w_l1f_rawV_rows", [4, 128, 1024])
    w_l1f_lngc = win("w_l1f_lngc", [128, ND])
    w_l1f_lnbc = win("w_l1f_lnbc", [128, ND])
    w_l1f_gammac = win("w_l1f_gammac", [128, ND])

    w_l1a_dkv = win("w_l1a_dkv", [ND, 128, 256])
    w_l1a_uk = win("w_l1a_uk", [2, 128, 512])
    w_l1a_uv = win("w_l1a_uv", [2, 128, 512])
    w_l1a_dq = win("w_l1a_dq", [ND, 128, 512])
    w_l1a_uq = win("w_l1a_uq", [4, 128, 512])
    w_l1a_qrR = win("w_l1a_qrR", [4, 128, 256])
    w_l1a_qr = win("w_l1a_qr", [4, 128, 256])
    w_l1a_kr = win("w_l1a_kr", [ND, 128, 32])
    w_l1a_wout = win("w_l1a_wout", [4, 128, 1024])
    w_l1a_iwq = win("w_l1a_iwq", [ND, 128, 256])
    w_l1a_iwk = win("w_l1a_iwk", [ND, 128, 64])
    w_l1a_iw = win("w_l1a_iw", [1, 4])
    g_csband_c = win("g_csband_c", [128, T])
    g_csband_s = win("g_csband_s", [128, T])
    g_csq2_c = win("g_csq2_c", [128, 128])
    g_csq2_s = win("g_csq2_s", [128, 128])
    g_cswbar = win("g_cswbar", [16, 2 * T])
    g_csrot = win("g_csrot", [128, 2048])
    c_iota_s = win("c_iota_s", [128, T])
    mu_dram = k.dram_tmp("mu_dram", [T, 32], FP32)
    srd_dram = k.dram_tmp("srd_dram", [128 * T + 4096, 8], BF16)
    qb_dram = k.dram_tmp("qb_dram", [8, 128, T], BF16)
    kc_dram = k.dram_tmp("kc_dram", [8, 64, T], BF16)
    qc_dram = k.dram_tmp("qc_dram", [4, 128, T], BF16)
    qh_dram = k.dram_tmp("qh_dram", [NT, 128, 256], BF16)

    for s in ("dq", "dw", "pe", "dve", "act", "pool", "dst",
              "dsc0", "dsc1", "dsb0", "dsb1", "dob"):
        k.sem(s)

    xT = k.sb("xT", [128, ND * T], FP32)
    ylnT = k.sb("ylnT", [128, ND * T], BF16)
    ident = k.sb("ident", [128, 128], FP32)
    identb = k.sb("identb", [128, 128], BF16)
    ones_b = k.sb("ones_b", [128, 512], BF16)
    iota_col = k.sb("iota_col", [128, 1], FP32)
    eps_col = k.sb("eps_col", [128, 1], FP32)

    nat_tmp = k.sb("nat_tmp", [128, D], FP32)
    PSA = k.ps("PSA", [128, 4096], FP32)
    ps_tp = PSA[:, 3584:4096]

    ln = None

    routerW_sb = k.sb("routerW_sb", [128, ND * 16], BF16)
    routerB_sb = k.sb("routerB_sb", [128, 15], FP32)
    sb1c_sb = k.sb("sb1c_sb", [128, ND], FP32)
    sb2r_sb = k.sb("sb2r_sb", [1, D], BF16)
    eb1c_sb = k.sb("eb1c_sb", [128, 15 * ND], FP32)
    eb2_sb = k.sb("eb2_sb", [15, 1024], BF16)

    POOL_BF = k.sb("POOL_BF", [128, 33792], BF16)
    POOL_F = k.sb("POOL_F", [128, 8192], FP32)

    ew_sb = POOL_BF[:, 0:16384]
    s1_bf = POOL_BF[:, 16384:20480]
    he_bf = s1_bf
    cwb_bf = POOL_BF[:, 20480:20992]
    ps_mm = PSA[:, 2560:3584]
    ps_big = PSA[:, 0:2048]
    lg_nat = k.sb("lg_nat", [128, 16], FP32)
    prob = k.sb("prob", [128, 16], FP32)
    r1 = k.sb("r1", [128, 16], FP32)
    rmax = k.sb("rmax", [128, 1], FP32)
    rsum = k.sb("rsum", [128, 1], FP32)
    rrec = k.sb("rrec", [128, 1], FP32)
    m8 = k.sb("m8_l0f", [128, 8], FP32)
    cw_nat = k.sb("cw_nat", [128, NT * 16], FP32)
    cwT_bf = POOL_BF[0:16, 20992:23040]
    cw_row = POOL_BF[0:1, 23040:25088]

    a_WinT = POOL_BF[:, 0:8192]
    a_WoutT = POOL_BF[:, 8192:16384]
    a_xpT_bf = POOL_BF[:, 16384:18432]
    a_xh_nat = POOL_BF[:, 18432:20480]
    a_T = POOL_BF[:, 20480:22016]
    a_diag = POOL_BF[:, 22016:24704]
    a_otc = POOL_BF[:, 24704:25216]
    a_S = POOL_BF[:, 25216:27264]
    a_cbsb = POOL_BF[:, 27264:29312]
    a_tmp = POOL_BF[:, 29312:31360]
    a_crow = POOL_BF[0:1, 31360:33408]
    a_winnat = POOL_BF[:, 31360:32384]
    a_wch = POOL_BF[:, 32384:32416]
    a_binsl = POOL_BF[:, 32416:32424]
    a_wceff = POOL_BF[:, 33408:33664]
    a_cbrow = POOL_BF[0:1, 33664:33696]
    a_obuf = POOL_BF[:, 16384:20480]
    a_dgc = POOL_BF[:, 8192:13440]

    ln0 = dict(
        ps_sx=PSA[0:1, 0:T],
        ps_sq=PSA[0:1, T:2 * T],
        ps_bc_a=PSA[:, 0:T],
        ps_bc_b=PSA[:, T:2 * T],
        row_m=POOL_F[0:1, 2048:4096],
        row_r=POOL_F[0:1, 4096:6144],
        row_t=POOL_F[0:1, 6144:8192],
        t1=POOL_F[:, 0:2048],
    )
    ln0["xbf"] = POOL_BF[:, 0:2048]
    ln0["sq"] = POOL_BF[:, 2048:4096]
    ln0["row_r_bf"] = POOL_BF[0:1, 4096:6144]
    ln0["row_m_bf"] = POOL_BF[0:1, 6144:8192]
    ln = ln0

    def dma(eng, out, in_, sname="dq"):
        ins = eng.dma_start(out=out, in_=in_)
        return k.inc(ins, sname, 16)

    def edge(ins, sname, *engs):
        k.inc(ins, sname)
        for e in engs:
            k.wait(e, sname)

    def V(op, *a, **kw):
        k.wait(nc.vector, "dve")
        ins = getattr(nc.vector, op)(*a, **kw)
        k.inc(ins, "dve")
        return ins

    def A(op, *a, **kw):
        k.wait(nc.scalar, "act")
        ins = getattr(nc.scalar, op)(*a, **kw)
        k.inc(ins, "act")
        return ins

    def barrier():
        for eng in (nc.tensor, nc.vector, nc.scalar, nc.gpsimd, nc.sync):
            for sn in list(k.sems):
                k.wait(eng, sn)

    dma(nc.sync, ident[:], c_ident[:, :])
    dma(nc.sync, identb[:], c_identb[:, :])
    dma(nc.sync, ones_b[:], c_ones_b[:, :])
    dma(nc.sync, iota_col[:], c_iota_col[:, :])
    dma(nc.sync, eps_col[:], c_eps[:, :])
    for c in range(ND):
        dma(nc.gpsimd, routerW_sb[:, c * 16:c * 16 + 15], w_routerW[c], "dw")
    dma(nc.sync, routerB_sb[:], w_routerB[:, :])
    dma(nc.sync, sb1c_sb[:], w_sb1c[:, :])
    dma(nc.gpsimd, sb2r_sb[:], w_sb2r[:, :], "dw")
    for e in range(15):
        dma(nc.sync, eb1c_sb[:, e * ND:(e + 1) * ND], w_eb1c[e], "dq")
    dma(nc.gpsimd, eb2_sb[:], w_eb2[:, :], "dw")
    W_LOADED = (k.sems["dq"][1], k.sems["dw"][1])

    for eng in (nc.vector, nc.scalar, nc.tensor, nc.gpsimd, nc.sync):
        k.wait(eng, "dq", W_LOADED[0])
        k.wait(eng, "dw", W_LOADED[1])


    def transpose_in(b):
        for tt in range(NT):
            k.wait(nc.sync, "pe")
            k.wait(nc.sync, "dq")
            dma(nc.sync, nat_tmp[:], x_in[b, tt * 128:(tt + 1) * 128, :])
            k.wait(nc.tensor, "dq")
            for c in range(ND):
                ins = nc.tensor.transpose(ps_tp[:, (c % 4) * 128:(c % 4) * 128 + 128],
                                          nat_tmp[:, c * 128:(c + 1) * 128], ident[:])
                if c % 4 == 3:
                    edge(ins, "pe", nc.vector)
                    for cc in range(c - 3, c + 1):
                        ins2 = V("tensor_copy", 
                            xT[:, cc * T + tt * 128: cc * T + tt * 128 + 128],
                            ps_tp[:, (cc % 4) * 128:(cc % 4) * 128 + 128])
                    k.wait(nc.tensor, "dve")

    def transpose_out(b):
        for tt in range(NT):
            k.wait(nc.tensor, "dq")
            k.wait(nc.vector, "dq")
            k.wait(nc.sync, "dq")
            for c in range(ND):
                ins = nc.tensor.matmul(ps_tp[:, (c % 4) * 128:(c % 4) * 128 + 128],
                                       xT[:, c * T + tt * 128: c * T + tt * 128 + 128],
                                       ident[:], start=True, stop=True)
                if c % 4 == 3:
                    edge(ins, "pe", nc.vector)
                    for cc in range(c - 3, c + 1):
                        ins2 = V("tensor_copy", 
                            nat_tmp[:, cc * 128:(cc + 1) * 128],
                            ps_tp[:, (cc % 4) * 128:(cc % 4) * 128 + 128])
                    k.wait(nc.tensor, "dve")
            k.wait(nc.sync, "dve")
            dma(nc.sync, y_out[b, tt * 128:(tt + 1) * 128, :], nat_tmp[:])

    def layer_norm():
        for c in range(ND):
            k.wait(nc.vector, "pe")
            ins = V("tensor_copy", ln["xbf"][:], xT[:, c * T:(c + 1) * T])
            k.wait(nc.scalar, "dve"); k.wait(nc.tensor, "dve")
            ins = A("activation", ln["sq"][:], ln["xbf"][:], AF.Square)
            k.wait(nc.tensor, "act")
            for tc in range(T // 512):
                nc.tensor.matmul(ln["ps_sx"][0:1, tc * 512:(tc + 1) * 512],
                                 ones_b[:, 0:1],
                                 ln["xbf"][:, tc * 512:(tc + 1) * 512],
                                 start=(c == 0), stop=(c == ND - 1))
                ins = nc.tensor.matmul(ln["ps_sq"][0:1, tc * 512:(tc + 1) * 512],
                                       ones_b[:, 0:1],
                                       ln["sq"][:, tc * 512:(tc + 1) * 512],
                                       start=(c == 0), stop=(c == ND - 1))
            k.inc(ins, "pe")
        k.wait(nc.vector, "pe")
        V("tensor_scalar", ln["row_m"][0:1, :], ln["ps_sx"][0:1, :], 1.0 / D, None, op0=ALU.mult)
        V("tensor_scalar", ln["row_r"][0:1, :], ln["ps_sq"][0:1, :], 1.0 / D, None, op0=ALU.mult)
        V("tensor_tensor", ln["row_t"][0:1, :], ln["row_m"][0:1, :], ln["row_m"][0:1, :], ALU.mult)
        ins = V("tensor_sub", ln["row_r"][0:1, :], ln["row_r"][0:1, :], ln["row_t"][0:1, :])
        k.wait(nc.scalar, "dve")
        ins = A("activation", ln["row_t"][0:1, :], ln["row_r"][0:1, :], AF.Sqrt, bias=eps_col[0:1, 0:1])
        k.wait(nc.vector, "act")
        V("reciprocal", ln["row_r"][0:1, :], ln["row_t"][0:1, :])
        V("tensor_mul", ln["row_m"][0:1, :], ln["row_m"][0:1, :], ln["row_r"][0:1, :])
        V("tensor_scalar", ln["row_m"][0:1, :], ln["row_m"][0:1, :], -1.0, None, op0=ALU.mult)
        V("tensor_copy", ln["row_r_bf"][0:1, :], ln["row_r"][0:1, :])
        ins = V("tensor_copy", ln["row_m_bf"][0:1, :], ln["row_m"][0:1, :])
        k.wait(nc.tensor, "dve")
        for tc in range(T // 512):
            nc.tensor.matmul(ln["ps_bc_a"][:, tc * 512:(tc + 1) * 512], ones_b[0:1, 0:128],
                             ln["row_r_bf"][0:1, tc * 512:(tc + 1) * 512], start=True, stop=True)
            ins = nc.tensor.matmul(ln["ps_bc_b"][:, tc * 512:(tc + 1) * 512], ones_b[0:1, 0:128],
                                   ln["row_m_bf"][0:1, tc * 512:(tc + 1) * 512], start=True, stop=True)
        edge(ins, "pe", nc.vector)
        for c in range(ND):
            V("tensor_tensor", ln["t1"][:], xT[:, c * T:(c + 1) * T], ln["ps_bc_a"][:], ALU.mult)
            ins = V("tensor_tensor", ylnT[:, c * T:(c + 1) * T], ln["t1"][:], ln["ps_bc_b"][:], ALU.add)
        k.wait(nc.tensor, "dve")

    def l0_ffn():
        for tt in range(NT):
            for c in range(ND):
                ins = nc.tensor.matmul(ps_mm[:, 0:15],
                                       ylnT[:, c * T + tt * 128: c * T + tt * 128 + 128],
                                       routerW_sb[:, c * 16: c * 16 + 15],
                                       start=(c == 0), stop=(c == ND - 1))
            edge(ins, "pe", nc.vector)
            ins = V("tensor_tensor", lg_nat[:, 0:15], ps_mm[:, 0:15], routerB_sb[:, 0:15], ALU.add)
            k.wait(nc.tensor, "dve")
            ins = V("reduce_max", rmax[:, 0:1], lg_nat[:, 0:15], axis=AX, negate=True)
            k.wait(nc.scalar, "dve")
            ins = A("activation", prob[:, 0:15], lg_nat[:, 0:15], AF.Exp,
                                       bias=rmax[:, 0:1], scale=1.0, accum_out=rsum[:, 0:1])
            k.wait(nc.vector, "act")
            V("reciprocal", rrec[:, 0:1], rsum[:, 0:1])
            ins = V("tensor_scalar", prob[:, 0:15], prob[:, 0:15], rrec[:, 0:1], None, op0=ALU.mult)
            ins = V("max", out=m8[:, 0:8], in_=prob[:, 0:15])
            
            V("tensor_scalar", r1[:, 0:15], prob[:, 0:15], m8[:, 2:3], None, op0=ALU.is_ge)
            ins = V("tensor_tensor", cw_nat[:, tt * 16: tt * 16 + 15], r1[:, 0:15], prob[:, 0:15], ALU.mult)
            if tt == NT - 1:
                k.wait(nc.tensor, "dve")
        for tt in range(NT):
            ins = nc.tensor.transpose(ps_tp[0:15, 0:128],
                                      cw_nat[:, tt * 16: tt * 16 + 15], ident[:])
            edge(ins, "pe", nc.vector)
            ins = V("tensor_copy", cwT_bf[0:15, tt * 128:(tt + 1) * 128], ps_tp[0:15, 0:128])
            k.wait(nc.tensor, "dve")

        k.wait(nc.gpsimd, "pe")
        for c in range(ND):
            dma(nc.gpsimd, ew_sb[:, c * 1024:(c + 1) * 1024], w_sW1[c], "dw")
            dma(nc.gpsimd, ew_sb[:, (ND + c) * 1024:(ND + c + 1) * 1024], w_sW2[c], "dw")
        k.wait(nc.tensor, "dw")
        for tc in range(T // 512):
            t0 = tc * 512
            for ht in range(ND):
                pp = ps_mm[:, (ht % 2) * 512:(ht % 2) * 512 + 512]
                k.wait(nc.tensor, "act")
                for c in range(ND):
                    ins = nc.tensor.matmul(pp,
                                           ew_sb[:, c * 1024 + ht * 128: c * 1024 + ht * 128 + 128],
                                           ylnT[:, c * T + t0: c * T + t0 + 512],
                                           start=(c == 0), stop=(c == ND - 1))
                edge(ins, "pe", nc.scalar)
                ins = A("activation", s1_bf[:, ht * 512:(ht + 1) * 512], pp, AF.Relu,
                                           bias=sb1c_sb[:, ht:ht + 1], scale=1.0)
            k.wait(nc.tensor, "act")
            for half in range(2):
                for dt2 in range(4):
                    dt = half * 4 + dt2
                    pb = ps_big[:, dt2 * 512:(dt2 + 1) * 512]
                    mms = []
                    for hc in range(ND):
                        mms.append((ew_sb[:, (ND + hc) * 1024 + dt * 128: (ND + hc) * 1024 + dt * 128 + 128],
                                    s1_bf[:, hc * 512:(hc + 1) * 512]))
                    mms.append((sb2r_sb[0:1, dt * 128:(dt + 1) * 128], ones_b[0:1, 0:512]))
                    mms.append((eb2_sb[0:15, dt * 128:(dt + 1) * 128], cwT_bf[0:15, t0:t0 + 512]))
                    for i, (lh, rh) in enumerate(mms):
                        ins = nc.tensor.matmul(pb, lh, rh, start=(i == 0), stop=(i == len(mms) - 1))
                    k.inc(ins, "pe")
                k.wait(nc.vector, "pe")
                for dt2 in range(4):
                    dt = half * 4 + dt2
                    pb = ps_big[:, dt2 * 512:(dt2 + 1) * 512]
                    sl = slice(dt * T + t0, dt * T + t0 + 512)
                    ins = V("tensor_tensor", xT[:, sl], xT[:, sl], pb, ALU.add)
                k.wait(nc.tensor, "dve")

        for e in range(15):
            k.wait(nc.gpsimd, "pe")
            k.wait(nc.gpsimd, "dve")
            dma(nc.gpsimd, cw_row[0:1, :], cwT_bf[e:e + 1, :], "dw")
            for c in range(ND):
                dma(nc.gpsimd, ew_sb[:, c * 1024:(c + 1) * 1024], w_eW1[e, c], "dw")
                dma(nc.gpsimd, ew_sb[:, (ND + c) * 1024:(ND + c + 1) * 1024], w_eW2[e, c], "dw")
            k.wait(nc.tensor, "dw")
            for tc in range(T // 512):
                t0 = tc * 512
                for ht in range(ND):
                    pp = ps_mm[:, (ht % 2) * 512:(ht % 2) * 512 + 512]
                    k.wait(nc.tensor, "act")
                    for c in range(ND):
                        ins = nc.tensor.matmul(pp,
                                               ew_sb[:, c * 1024 + ht * 128: c * 1024 + ht * 128 + 128],
                                               ylnT[:, c * T + t0: c * T + t0 + 512],
                                               start=(c == 0), stop=(c == ND - 1))
                    edge(ins, "pe", nc.scalar)
                    ins = A("activation", s1_bf[:, ht * 512:(ht + 1) * 512], pp, AF.Relu,
                                               bias=eb1c_sb[:, e * ND + ht: e * ND + ht + 1], scale=1.0)
                pp = ps_mm[:, 0:512]
                k.wait(nc.tensor, "act")
                ins = nc.tensor.matmul(pp, ones_b[0:1, 0:128], cw_row[0:1, t0:t0 + 512], start=True, stop=True)
                edge(ins, "pe", nc.scalar)
                ins = A("activation", cwb_bf[:, 0:512], pp, AF.Copy)
                k.wait(nc.vector, "act"); k.wait(nc.tensor, "act")
                for ht in range(ND):
                    sl = slice(ht * 512, (ht + 1) * 512)
                    ins = V("tensor_tensor", s1_bf[:, sl], s1_bf[:, sl], cwb_bf[:, 0:512], ALU.mult)
                k.wait(nc.tensor, "dve")
                for half in range(2):
                    for dt2 in range(4):
                        dt = half * 4 + dt2
                        pb = ps_big[:, dt2 * 512:(dt2 + 1) * 512]
                        for hc in range(ND):
                            ins = nc.tensor.matmul(pb,
                                                   ew_sb[:, (ND + hc) * 1024 + dt * 128:(ND + hc) * 1024 + dt * 128 + 128],
                                                   s1_bf[:, hc * 512:(hc + 1) * 512],
                                                   start=(hc == 0), stop=(hc == ND - 1))
                        k.inc(ins, "pe")
                    k.wait(nc.vector, "pe")
                    for dt2 in range(4):
                        dt = half * 4 + dt2
                        pb = ps_big[:, dt2 * 512:(dt2 + 1) * 512]
                        sl = slice(dt * T + t0, dt * T + t0 + 512)
                        ins = V("tensor_tensor", xT[:, sl], xT[:, sl], pb, ALU.add)
                    k.wait(nc.tensor, "dve")
        return

    def dump(name, sb_ap, shape, dt=FP32):
        if name not in dbg_taps:
            return
        d = k.dram_out("tap_" + name, shape, dt)
        taps[name] = d
        for eng in (nc.sync,):
            k.wait(eng, "dve"); k.wait(eng, "act"); k.wait(eng, "pe"); k.wait(eng, "dq")
        dma(nc.sync, d[:, :] if len(shape) == 2 else d[:], sb_ap, "dq")
        k.wait(nc.vector, "dq"); k.wait(nc.tensor, "dq"); k.wait(nc.scalar, "dq")


    KS = [3, 3, 7, 7, 11, 11, 21, 21]
    OFFS = list(range(-10, 11)) + list(range(-127, -117)) + list(range(118, 128))

    def l0_attn():
        barrier()
        for c in range(ND):
            dma(nc.gpsimd, a_WinT[:, c * 1024:(c + 1) * 1024], w_l0a_WinT[c], "dw")
        dma(nc.sync, a_dgc[:], c_diag[:, :])
        dma(nc.sync, POOL_F[:, 2048:2720], w_l0a_A_rep[:, :])
        dma(nc.sync, POOL_F[:, 2720:2888], w_l0a_base_cols[:, :])
        dma(nc.sync, POOL_F[0:1, 2888:2896], w_l0a_alphas[:, :])
        dma(nc.sync, POOL_F[:, 2896:2928], w_l0a_Vcols[:, :])
        dma(nc.gpsimd, a_binsl[:, 0:8], w_l0a_binc[:, :], "dw")
        dma(nc.sync, POOL_F[:, 2928:2936], w_l0a_binc[:, :])
        A_rep = POOL_F[:, 2048:2720]
        base_cols = POOL_F[:, 2720:2888]
        alp_row = POOL_F[0:1, 2888:2896]
        Vcols = POOL_F[:, 2896:2928]
        binc_f = POOL_F[:, 2928:2936]
        c_nat = POOL_F[:, 2936:3448]
        alpb = POOL_F[:, 3448:3464]
        alp_bf = POOL_BF[0:1, 33696:33712]
        cT = POOL_F[0:32, 3584:5632]
        barrier()

        ps_wc = PSA[:, 2560:2816]
        ps_cb = PSA[0:1, 2816:2848]
        for h in range(8):
            k.wait(nc.gpsimd, "pe")
            dma(nc.gpsimd, a_winnat[:, 0:1024], w_l0a_Win[h], "dw")
            dma(nc.gpsimd, a_wch[:, h * 4:(h + 1) * 4], w_l0a_Wc[h], "dw")
            k.wait(nc.tensor, "dw")
            for dc in range(ND):
                ins = nc.tensor.matmul(ps_wc[:, dc * 32 + h * 4: dc * 32 + h * 4 + 4],
                                       a_winnat[:, dc * 128:(dc + 1) * 128],
                                       a_wch[:, h * 4:(h + 1) * 4], start=True, stop=True)
            ins = nc.tensor.matmul(ps_cb[0:1, h * 4:(h + 1) * 4],
                                   a_binsl[:, h:h + 1],
                                   a_wch[:, h * 4:(h + 1) * 4], start=True, stop=True)
            k.inc(ins, "pe")
        k.wait(nc.vector, "pe")
        V("tensor_copy", a_wceff[:, 0:256], ps_wc[:, 0:256])
        ins = V("tensor_copy", a_cbrow[0:1, 0:32], ps_cb[0:1, 0:32])
        k.wait(nc.tensor, "dve")

        ins = A("activation", alpb[0:1, 0:8], alp_row[0:1, 0:8], AF.Sigmoid)
        k.wait(nc.vector, "act")
        V("tensor_scalar", alpb[0:1, 8:16], alpb[0:1, 0:8], -1.0, 1.0, op0=ALU.mult, op1=ALU.add)
        ins = V("tensor_copy", alp_bf[0:1, 0:16], alpb[0:1, 0:16])
        k.wait(nc.tensor, "dve")
        ps_al = PSA[:, 2848:2864]
        ins = nc.tensor.matmul(ps_al[:, 0:16], ones_b[0:1, 0:128], alp_bf[0:1, 0:16], start=True, stop=True)
        edge(ins, "pe", nc.vector)
        ins = V("tensor_copy", alpb[:, 0:16], ps_al[:, 0:16])
        k.wait(nc.tensor, "dve")
        v_all = POOL_F[:, 5632:5800]
        for h in range(8):
            ins = V("tensor_scalar", v_all[:, h * 21:(h + 1) * 21],
                    base_cols[:, h * 21:(h + 1) * 21], alpb[:, 8 + h:9 + h], None, op0=ALU.mult)
        k.wait(nc.tensor, "dve")

        ps_c = PSA[:, 3072:3104]
        for tt in range(NT):
            for c in range(ND):
                nc.tensor.matmul(ps_c[:, 0:32],
                                 ylnT[:, c * T + tt * 128: c * T + tt * 128 + 128],
                                 a_wceff[:, c * 32:(c + 1) * 32],
                                 start=(c == 0), stop=False)
            ins = nc.tensor.matmul(ps_c[:, 0:32], ones_b[0:1, 0:128], a_cbrow[0:1, 0:32],
                                   start=False, stop=True)
            edge(ins, "pe", nc.vector)
            ins = V("tensor_copy", c_nat[:, tt * 32:(tt + 1) * 32], ps_c[:, 0:32])
            k.wait(nc.tensor, "dve")
        al_ap = bass.AP(alpb.tensor, alpb.offset, [[alpb.ap[0][0], 128], [0, NT], [1, 8], [0, 4]])
        ins = V("tensor_tensor", c_nat[:, 0:512], c_nat[:, 0:512], al_ap, ALU.mult)
        k.wait(nc.tensor, "dve")
        for tt in range(NT):
            ins = nc.tensor.transpose(ps_tp[0:32, 0:128], c_nat[:, tt * 32:(tt + 1) * 32], ident[:])
            edge(ins, "pe", nc.vector)
            ins = V("tensor_copy", cT[0:32, tt * 128:(tt + 1) * 128], ps_tp[0:32, 0:128])
            k.wait(nc.tensor, "dve")

        for h in range(8):
            kh = KS[h]; pad = kh // 2
            for c in range(ND):
                for tq in range(4):
                    ins = nc.tensor.matmul(PSA[:, tq * 512:(tq + 1) * 512],
                                           a_WinT[:, c * 1024 + h * 128: c * 1024 + h * 128 + 128],
                                           ylnT[:, c * T + tq * 512: c * T + tq * 512 + 512],
                                           start=(c == 0), stop=(c == ND - 1))
            edge(ins, "pe", nc.scalar, nc.vector)
            xpT_f = POOL_F[:, 0:2048]
            for tq in range(4):
                V("tensor_scalar", xpT_f[:, tq * 512:(tq + 1) * 512], PSA[:, tq * 512:(tq + 1) * 512],
                  binc_f[:, h:h + 1], None, op0=ALU.add)
            ins = V("tensor_copy", a_xpT_bf[:, 0:2048], xpT_f[:, 0:2048])
            k.wait(nc.tensor, "dve")
            for i in range(NT):
                ins = nc.tensor.transpose(ps_tp[:, (i % 4) * 128:(i % 4) * 128 + 128],
                                          xpT_f[:, i * 128:(i + 1) * 128], ident[:])
                if i % 4 == 3:
                    edge(ins, "pe", nc.vector)
                    for ii in range(i - 3, i + 1):
                        ins2 = V("tensor_copy", a_xh_nat[:, ii * 128:(ii + 1) * 128],
                                 ps_tp[:, (ii % 4) * 128:(ii % 4) * 128 + 128])
                    k.wait(nc.tensor, "dve")
            for r in range(4):
                for kind in range(3):
                    sl = a_T[:, (r * 3 + kind) * 128:(r * 3 + kind + 1) * 128]
                    taps = []
                    for kk in range(kh):
                        o = pad - kk + (0 if kind == 1 else (-128 if kind == 0 else 128))
                        if -127 <= o <= 127 and o in OFFS:
                            taps.append((kk, OFFS.index(o)))
                    first = True
                    for kk, oi in taps:
                        acol = A_rep[:, (h * 4 + r) * 21 + kk: (h * 4 + r) * 21 + kk + 1]
                        dg = a_dgc[:, oi * 128:(oi + 1) * 128]
                        if first:
                            V("tensor_scalar", sl, dg, acol, None, op0=ALU.mult)
                            first = False
                        else:
                            V("scalar_tensor_tensor", sl, dg, acol, sl, op0=ALU.mult, op1=ALU.add)
                    if first:
                        V("memset", sl, 0.0)
            zi = OFFS.index(0)
            for kk in range(kh):
                ins = V("tensor_scalar", a_diag[:, kk * 128:(kk + 1) * 128],
                        a_dgc[:, zi * 128:(zi + 1) * 128],
                        v_all[:, h * 21 + kk: h * 21 + kk + 1], None, op0=ALU.mult)
            k.wait(nc.tensor, "dve")

            for tq in range(4):
                t0 = tq * 512
                ps_o = PSA[:, 2048:2560]
                ps_S = PSA[:, 2560:3072]
                ps_cbb = PSA[:, 3072:3584]
                order = [pad] + [kk for kk in range(kh) if kk != pad]
                for idx_k, kk in enumerate(order):
                    src0 = t0 + kk - pad
                    lo = max(0, -src0); hi = min(512, 2048 - src0)
                    ins = nc.tensor.matmul(ps_o[:, lo:hi],
                                           a_diag[:, kk * 128:(kk + 1) * 128],
                                           a_xpT_bf[:, src0 + lo: src0 + hi],
                                           start=(idx_k == 0), stop=(idx_k == len(order) - 1))
                k.inc(ins, "pe")
                for nt2 in range(4):
                    i = tq * 4 + nt2
                    for r in range(4):
                        mms = []
                        if i > 0:
                            mms.append((a_xh_nat[:, (i - 1) * 128: i * 128], a_T[:, (r * 3 + 0) * 128:(r * 3 + 1) * 128]))
                        mms.append((a_xh_nat[:, i * 128:(i + 1) * 128], a_T[:, (r * 3 + 1) * 128:(r * 3 + 2) * 128]))
                        if i < NT - 1:
                            mms.append((a_xh_nat[:, (i + 1) * 128:(i + 2) * 128], a_T[:, (r * 3 + 2) * 128:(r * 3 + 3) * 128]))
                        for mi, (lh, rh) in enumerate(mms):
                            ins = nc.tensor.matmul(ps_S[:, r * 128:(r + 1) * 128], lh, rh,
                                                   start=(mi == 0), stop=(mi == len(mms) - 1))
                    edge(ins, "pe", nc.vector)
                    ins = V("tensor_copy", a_S[:, nt2 * 512:(nt2 + 1) * 512], ps_S[:, 0:512])
                    k.wait(nc.tensor, "dve")
                for r in range(4):
                    k.wait(nc.gpsimd, "pe"); k.wait(nc.gpsimd, "dve"); k.wait(nc.gpsimd, "act")
                    dma(nc.gpsimd, a_crow[0:1, 0:2048], cT[h * 4 + r: h * 4 + r + 1, 0:2048], "dw")
                    k.wait(nc.tensor, "dw")
                    ins = nc.tensor.matmul(ps_cbb[:, 0:512], ones_b[0:1, 0:128],
                                           a_crow[0:1, t0:t0 + 512], start=True, stop=True)
                    edge(ins, "pe", nc.scalar)
                    ins = A("activation", a_cbsb[:, r * 512:(r + 1) * 512], ps_cbb[:, 0:512], AF.Copy)
                k.wait(nc.vector, "act")
                cb_ap = bass.AP(a_cbsb.tensor, a_cbsb.offset,
                                [[a_cbsb.ap[0][0], 128], [128, 4], [512, 4], [1, 128]])
                ins = V("tensor_tensor", a_tmp[:, 0:2048], a_S[:, 0:2048], cb_ap, ALU.mult)
                k.wait(nc.tensor, "dve")
                for nt2 in range(4):
                    for r in range(4):
                        ins = V("scalar_tensor_tensor", ps_o[:, nt2 * 128:(nt2 + 1) * 128],
                                a_tmp[:, nt2 * 512 + r * 128: nt2 * 512 + r * 128 + 128],
                                Vcols[:, h * 4 + r: h * 4 + r + 1],
                                ps_o[:, nt2 * 128:(nt2 + 1) * 128], op0=ALU.mult, op1=ALU.add)
                k.wait(nc.scalar, "dve")
                k.wait(nc.scalar, "dq")
                ins = A("activation", a_otc[:, 0:512], ps_o[:, 0:512], AF.Copy)
                k.wait(nc.sync, "act"); k.wait(nc.tensor, "act")
                k.wait(nc.sync, "dq")
                dma(nc.sync, o_dram[h][:, t0:t0 + 512], a_otc[:, 0:512])
        barrier()
        for c in range(ND):
            dma(nc.gpsimd, a_WoutT[:, c * 1024:(c + 1) * 1024], w_l0a_WoutT[c], "dw")
        dma(nc.gpsimd, POOL_BF[0:1, 31360:32384], w_l0a_bout[:, :], "dw")
        bout_bf = POOL_BF[0:1, 31360:32384]
        barrier()
        for tq in range(4):
            t0 = tq * 512
            k.wait(nc.sync, "pe")
            for hc in range(ND):
                dma(nc.sync, a_obuf[:, hc * 512:(hc + 1) * 512], o_dram[hc][:, t0:t0 + 512])
            k.wait(nc.tensor, "dq")
            for half in range(2):
                for dt2 in range(4):
                    dt = half * 4 + dt2
                    pb = ps_big[:, dt2 * 512:(dt2 + 1) * 512]
                    for hc in range(ND):
                        nc.tensor.matmul(pb,
                                         a_WoutT[:, hc * 1024 + dt * 128: hc * 1024 + dt * 128 + 128],
                                         a_obuf[:, hc * 512:(hc + 1) * 512],
                                         start=(hc == 0), stop=False)
                    ins = nc.tensor.matmul(pb, bout_bf[0:1, dt * 128:(dt + 1) * 128],
                                           ones_b[0:1, 0:512], start=False, stop=True)
                    k.inc(ins, "pe")
                k.wait(nc.vector, "pe")
                for dt2 in range(4):
                    dt = half * 4 + dt2
                    pb = ps_big[:, dt2 * 512:(dt2 + 1) * 512]
                    sl = slice(dt * T + t0, dt * T + t0 + 512)
                    ins = V("tensor_tensor", xT[:, sl], xT[:, sl], pb, ALU.add)
                k.wait(nc.tensor, "dve")
        barrier()


    def l1_ffn():
        barrier()
        f_W1c = POOL_BF[:, 0:1024]
        f_W2c = POOL_BF[:, 1024:2048]
        f_h1slot = POOL_BF[:, 2048:2560]
        f_h1in = POOL_BF[:, 2560:3584]
        f_wT = POOL_BF[:, 3584:11776]
        f_yl2 = POOL_BF[:, 11776:12800]
        f_rBrow = POOL_BF[0:1, 24704:25216]
        f_rUc = POOL_BF[:, 12416:16512]
        f_rWc = POOL_BF[:, 16512:20608]
        f_rVc = POOL_BF[:, 20608:24704]
        f_urows = POOL_F[:, 4096:5120]
        g_alpha = POOL_F[:, 0:512]
        g_tmp = POOL_F[:, 512:1024]
        g_w = POOL_F[:, 1024:1536]
        g_m8 = POOL_F[:, 1536:1544]
        g_tau = POOL_F[:, 1544:1545]
        g_S = POOL_F[:, 1545:1546]
        g_zf = POOL_F[:, 1546:1547]
        g_nrm = POOL_F[:, 1552:1560]
        g_inv = POOL_F[:, 1560:2072]
        g_lngc = POOL_F[:, 2072:2080]
        g_lnbc = POOL_F[:, 2080:2088]
        g_gmc = POOL_F[:, 2088:2096]
        inv_row = POOL_F[0:4, 2112:2240]
        inv_row1 = POOL_F[0:1, 2240:2752]
        inv_rowb = POOL_BF[0:1, 26240:26752]

        for c in range(ND):
            dma(nc.gpsimd, f_rUc[:, c * 512:(c + 1) * 512], w_l1f_rawUT[c], "dw")
            dma(nc.gpsimd, f_rWc[:, c * 512:(c + 1) * 512], w_l1f_routerWT[c], "dw")
        for ec in range(4):
            dma(nc.gpsimd, f_rVc[:, ec * 1024:(ec + 1) * 1024], w_l1f_rawV_rows[ec], "dw")
        dma(nc.gpsimd, f_rBrow[0:1, :], w_l1f_routerBr[:, :], "dw")
        dma(nc.sync, g_lngc[:, 0:8], w_l1f_lngc[:, :])
        dma(nc.sync, g_lnbc[:, 0:8], w_l1f_lnbc[:, :])
        dma(nc.sync, g_gmc[:, 0:8], w_l1f_gammac[:, :])
        barrier()

        for which in range(2):
            src = w_l1f_rawU_rows if which == 0 else w_l1f_rawV_rows
            for ec in range(4):
                k.wait(nc.sync, "dve"); k.wait(nc.sync, "act")
                dma(nc.sync, f_urows[:, 0:1024], src[ec])
                k.wait(nc.scalar, "dq")
                ins = A("activation", POOL_F[:, 5120:6144], f_urows[:, 0:1024], AF.Square)
                k.wait(nc.vector, "act")
                ins = V("reduce_sum", g_nrm[:, which * 4 + ec: which * 4 + ec + 1],
                        POOL_F[:, 5120:6144], axis=AX)
        k.wait(nc.scalar, "dve")
        ins = A("activation", g_nrm[:, 0:8], g_nrm[:, 0:8], AF.Sqrt)
        k.wait(nc.vector, "act")
        V("tensor_scalar", g_nrm[:, 0:8], g_nrm[:, 0:8], 1e-6, None, op0=ALU.max)
        V("tensor_tensor", g_nrm[:, 0:4], g_nrm[:, 0:4], g_nrm[:, 4:8], ALU.mult)
        ins = V("reciprocal", g_nrm[:, 0:4], g_nrm[:, 0:4])
        k.wait(nc.tensor, "dve")
        ins = nc.tensor.transpose(ps_tp[0:4, 0:128], g_nrm[:, 0:4], ident[:])
        edge(ins, "pe", nc.vector)
        ins = V("tensor_copy", inv_row[0:4, 0:128], ps_tp[0:4, 0:128])
        k.wait(nc.sync, "dve")
        for ec in range(4):
            dma(nc.sync, inv_row1[0:1, ec * 128:(ec + 1) * 128], inv_row[ec:ec + 1, 0:128])
        k.wait(nc.vector, "dq")
        ins = V("tensor_copy", inv_rowb[0:1, 0:512], inv_row1[0:1, 0:512])
        k.wait(nc.tensor, "dve")
        ins = nc.tensor.matmul(PSA[:, 3072:3584], ones_b[0:1, 0:128], inv_rowb[0:1, 0:512], start=True, stop=True)
        edge(ins, "pe", nc.vector)
        ins = V("tensor_copy", g_inv[:, 0:512], PSA[:, 3072:3584])
        k.wait(nc.tensor, "dve")

        for c in range(ND):
            k.wait(nc.scalar, "pe")
            ins = A("activation", ln["sq"], ylnT[:, c * T:(c + 1) * T], AF.Square)
            k.wait(nc.tensor, "act")
            for tc in range(T // 512):
                nc.tensor.matmul(ln["ps_sx"][0:1, tc * 512:(tc + 1) * 512], ones_b[:, 0:1],
                                 ylnT[:, c * T + tc * 512: c * T + tc * 512 + 512],
                                 start=(c == 0), stop=(c == ND - 1))
                ins = nc.tensor.matmul(ln["ps_sq"][0:1, tc * 512:(tc + 1) * 512], ones_b[:, 0:1],
                                       ln["sq"][:, tc * 512:(tc + 1) * 512],
                                       start=(c == 0), stop=(c == ND - 1))
            k.inc(ins, "pe")
        k.wait(nc.vector, "pe")
        V("tensor_scalar", ln["row_m"][0:1, :], ln["ps_sx"][0:1, :], 1.0 / D, None, op0=ALU.mult)
        V("tensor_scalar", ln["row_r"][0:1, :], ln["ps_sq"][0:1, :], 1.0 / D, None, op0=ALU.mult)
        V("tensor_tensor", ln["row_t"][0:1, :], ln["row_m"][0:1, :], ln["row_m"][0:1, :], ALU.mult)
        ins = V("tensor_sub", ln["row_r"][0:1, :], ln["row_r"][0:1, :], ln["row_t"][0:1, :])
        k.wait(nc.scalar, "dve")
        ins = A("activation", ln["row_t"][0:1, :], ln["row_r"][0:1, :], AF.Sqrt, bias=eps_col[0:1, 0:1])
        k.wait(nc.vector, "act")
        V("reciprocal", ln["row_r"][0:1, :], ln["row_t"][0:1, :])
        V("tensor_scalar", ln["row_m"][0:1, :], ln["row_m"][0:1, :], -1.0, None, op0=ALU.mult)
        V("tensor_copy", ln["row_r_bf"][0:1, :], ln["row_r"][0:1, :])
        ins = V("tensor_copy", ln["row_m_bf"][0:1, :], ln["row_m"][0:1, :])
        k.wait(nc.tensor, "dve")

        ps_r = PSA[:, 2048:2560]
        ps_h = PSA[:, 2560:3072]
        ps_bc2 = PSA[:, 3072:3328]
        for tt in range(NT):
            t0 = tt * 128
            nc.tensor.matmul(ps_bc2[:, 0:128], ones_b[0:1, 0:128],
                             ln["row_m_bf"][0:1, t0:t0 + 128], start=True, stop=True)
            ins = nc.tensor.matmul(ps_bc2[:, 128:256], ones_b[0:1, 0:128],
                                   ln["row_r_bf"][0:1, t0:t0 + 128], start=True, stop=True)
            edge(ins, "pe", nc.vector)
            for c in range(ND):
                V("tensor_tensor", g_tmp[:, 0:128], ylnT[:, c * T + t0: c * T + t0 + 128],
                  ps_bc2[:, 0:128], ALU.add)
                V("tensor_tensor", g_tmp[:, 128:256], g_tmp[:, 0:128],
                  ps_bc2[:, 128:256], ALU.mult)
                ins = V("tensor_scalar", f_yl2[:, c * 128:(c + 1) * 128], g_tmp[:, 128:256],
                        g_lngc[:, c:c + 1], g_lnbc[:, c:c + 1], op0=ALU.mult, op1=ALU.add)
            k.wait(nc.tensor, "dve")
            for c in range(ND):
                nc.tensor.matmul(ps_r[:, 0:512], f_yl2[:, c * 128:(c + 1) * 128],
                                 f_rWc[:, c * 512:(c + 1) * 512],
                                 start=(c == 0), stop=False)
            ins = nc.tensor.matmul(ps_r[:, 0:512], ones_b[0:1, 0:128], f_rBrow[0:1, 0:512],
                                   start=False, stop=True)
            edge(ins, "pe", nc.vector)
            for c in range(ND):
                ins = nc.tensor.matmul(ps_h[:, 0:512],
                                       ylnT[:, c * T + t0: c * T + t0 + 128],
                                       f_rUc[:, c * 512:(c + 1) * 512],
                                       start=(c == 0), stop=(c == ND - 1))
            edge(ins, "pe", nc.vector)
            V("tensor_scalar", g_alpha[:, 0:512], ps_r[:, 0:512], 10.0, -10.0, op0=ALU.min, op1=ALU.max)
            ins = V("tensor_scalar", g_tmp[:, 0:512], g_alpha[:, 0:512], 0.0, None, op0=ALU.max)
            k.wait(nc.scalar, "dve")
            ins = A("activation", g_w[:, 0:512], g_alpha[:, 0:512], AF.Abs)
            ins = A("activation", g_w[:, 0:512], g_w[:, 0:512], AF.Exp, scale=-1.0)
            ins = A("activation", g_w[:, 0:512], g_w[:, 0:512], AF.Ln, bias=1.0)
            k.wait(nc.vector, "act")
            ins = V("tensor_tensor", g_alpha[:, 0:512], g_tmp[:, 0:512], g_w[:, 0:512], ALU.add)
            ins = V("max", out=g_m8[:, 0:8], in_=g_alpha[:, 0:512])
            V("reduce_sum", g_S[:, 0:1], g_m8[:, 0:8], axis=AX)
            k.wait(nc.scalar, "dve")
            ins = A("activation", g_zf[:, 0:1], g_S[:, 0:1], AF.Tanh)
            k.wait(nc.vector, "act")
            V("tensor_scalar", g_tmp[:, 0:1], g_S[:, 0:1], 1e-6, None, op0=ALU.add)
            V("reciprocal", g_tmp[:, 0:1], g_tmp[:, 0:1])
            V("tensor_mul", g_zf[:, 0:1], g_zf[:, 0:1], g_tmp[:, 0:1])
            V("tensor_scalar", g_tmp[:, 0:512], g_alpha[:, 0:512], g_m8[:, 7:8], None, op0=ALU.is_ge)
            V("tensor_tensor", g_w[:, 0:512], g_tmp[:, 0:512], g_alpha[:, 0:512], ALU.mult)
            V("tensor_scalar", g_w[:, 0:512], g_w[:, 0:512], g_zf[:, 0:1], None, op0=ALU.mult)
            V("tensor_tensor", g_w[:, 0:512], g_w[:, 0:512], ps_h[:, 0:512], ALU.mult)
            V("tensor_tensor", g_w[:, 0:512], g_w[:, 0:512], g_inv[:, 0:512], ALU.mult)
            k.wait(nc.tensor, "dve")
            for ec in range(4):
                ins = nc.tensor.transpose(ps_tp[:, (ec % 4) * 128:(ec % 4) * 128 + 128],
                                          g_w[:, ec * 128:(ec + 1) * 128], ident[:])
                edge(ins, "pe", nc.vector)
                ins = V("tensor_copy", f_wT[:, ec * T + t0: ec * T + t0 + 128],
                        ps_tp[:, (ec % 4) * 128:(ec % 4) * 128 + 128])
                k.wait(nc.tensor, "dve")
        barrier()

        for dt in range(ND):
            for tq in range(4):
                pb = PSA[:, 2048:2560]
                for ec in range(4):
                    ins = nc.tensor.matmul(pb,
                                           f_rVc[:, ec * 1024 + dt * 128: ec * 1024 + dt * 128 + 128],
                                           f_wT[:, ec * T + tq * 512: ec * T + tq * 512 + 512],
                                           start=(ec == 0), stop=(ec == 3))
                edge(ins, "pe", nc.vector)
                sl = slice(dt * T + tq * 512, dt * T + tq * 512 + 512)
                ins = V("scalar_tensor_tensor", xT[:, sl], pb, g_gmc[:, dt:dt + 1], xT[:, sl],
                        op0=ALU.mult, op1=ALU.add)
                k.wait(nc.tensor, "dve")
        barrier()

        for ht in range(32):
            k.wait(nc.gpsimd, "pe")
            for c in range(ND):
                dma(nc.gpsimd, f_W1c[:, c * 128:(c + 1) * 128],
                    w_l1f_W1T[c][:, ht * 128:(ht + 1) * 128], "dw")
            k.wait(nc.tensor, "dw")
            for tq in range(4):
                pb = PSA[:, 2048 + (tq % 2) * 512: 2048 + (tq % 2) * 512 + 512]
                k.wait(nc.tensor, "act"); k.wait(nc.tensor, "dq")
                for c in range(ND):
                    ins = nc.tensor.matmul(pb,
                                           f_W1c[:, c * 128:(c + 1) * 128],
                                           ylnT[:, c * T + tq * 512: c * T + tq * 512 + 512],
                                           start=(c == 0), stop=(c == ND - 1))
                edge(ins, "pe", nc.scalar)
                k.wait(nc.scalar, "dq")
                ins = A("activation", f_h1slot[:, 0:512], pb, AF.Gelu)
                k.wait(nc.sync, "act")
                dma(nc.sync, h1_dram[ht][:, tq * 512:(tq + 1) * 512], f_h1slot[:, 0:512])
        barrier()
        for tq in range(4):
            t0 = tq * 512
            for hc in range(32):
                buf = f_h1in[:, (hc % 2) * 512:(hc % 2) * 512 + 512]
                k.wait(nc.sync, "pe")
                dma(nc.sync, buf, h1_dram[hc][:, t0:t0 + 512])
                k.wait(nc.gpsimd, "pe")
                dma(nc.gpsimd, f_W2c[:, 0:1024], w_l1f_W2T[hc], "dw")
                k.wait(nc.tensor, "dq"); k.wait(nc.tensor, "dw")
                for dt in range(ND):
                    ins = nc.tensor.matmul(PSA[:, dt * 512:(dt + 1) * 512],
                                           f_W2c[:, dt * 128:(dt + 1) * 128], buf,
                                           start=(hc == 0), stop=(hc == 31))
                k.inc(ins, "pe")
            k.wait(nc.vector, "pe")
            for dt in range(ND):
                sl = slice(dt * T + t0, dt * T + t0 + 512)
                ins = V("tensor_tensor", xT[:, sl], xT[:, sl], PSA[:, dt * 512:(dt + 1) * 512], ALU.add)
            k.wait(nc.tensor, "dve")
        barrier()




    s_m8 = k.sb("s_m8", [128, 8], FP32)
    s_v32 = k.sb("s_v32", [128, 32], FP32)
    s_iu = k.sb("s_iu", [128, 32], U32)
    s_ii = k.sb("s_ii", [128, 32], I32)
    s_ig = k.sb("s_ig", [128, 32], I32)
    s_isc = k.sb("s_isc", [128, 32], I32)
    s_icol = k.sb("s_icol", [128, 1], I32)
    s_tc = k.sb("s_tc", [128, 2], I32)
    s_iosm = k.sb("s_iosm", [128, 256], FP32)
    s_th = k.sb("s_th", [128, 1], FP32)
    s_iwb = k.sb("s_iwb", [128, 4], FP32)
    s_iwrow = k.sb("s_iwrow", [1, 4], BF16)
    s_z = k.sb("s_z", [128, 8], FP32)
    s_mx = k.sb("s_mx", [128, 16], FP32)
    s_cm = s_v32
    c_icol_i = win("c_icol_i", [128, 1], I32)
    c_iota_sm = win("c_iota_sm", [128, 256], FP32)
    g_csq64hi_c = win("g_csq64hi_c", [128, T])
    g_csq64hi_s = win("g_csq64hi_s", [128, T])
    c_zeros2k = win("c_zeros2k", [128, 2048], BF16)
    c_t94 = win("c_t94", [128, 1], FP32)
    c_t63 = win("c_t63", [128, 1], FP32)
    c_zone = win("c_zone", [128, 128], BF16)
    g_csq64n = win("g_csq64n", [T, 32], FP32)
    qi_dram = k.dram_tmp("qi_dram", [2, 128, T], BF16)
    ol1a_dram = k.dram_tmp("ol1a_dram", [8, 64, T], BF16)
    sc_dram = k.dram_tmp("sc_dram", [8, 128, 2048], BF16)

    def l1_attn(b):
        barrier()
        SCALE = float((64 + 32) ** -0.5)
        vnat = POOL_BF[:, 0:8192]
        qbar_tt = POOL_BF[:, 8192:8576]
        q2p_tt = POOL_BF[:, 8576:8960]
        kcb = POOL_BF[0:64, 12288:14336]
        wbar = POOL_BF[:, 16384:18432]
        kiT = POOL_BF[0:64, 18688:20736]
        srdq = [POOL_BF[:, 22784:24832], POOL_BF[:, 24832:26880]]
        maskb = POOL_BF[:, 26880:28928]
        stripb = POOL_BF[:, 28928:29184]
        zoneb = POOL_BF[:, 29184:29312]
        asc = POOL_BF[:, 29952:30208]
        slabT = POOL_BF[:, 30208:32256]
        qctt = POOL_BF[0:64, 32256:33280]
        qitt = POOL_BF[0:64, 33280:33792]
        qhtt = POOL_BF[:, 20736:20992]
        obh = POOL_BF[:, 20992:21248]
        fI = POOL_F[:, 0:2048]
        f_relu = POOL_F[:, 2048:4096]
        f_sc = POOL_F[:, 4096:4352]

        W_ = {}
        off = 0
        for nm, src, n in (("dkv", w_l1a_dkv, ND), ("uk", w_l1a_uk, 2), ("uv", w_l1a_uv, 2),
                           ("dq", w_l1a_dq, ND), ("uq", w_l1a_uq, 4), ("qrR", w_l1a_qrR, 4),
                           ("qr", w_l1a_qr, 4), ("kr", w_l1a_kr, ND), ("iwq", w_l1a_iwq, ND),
                           ("iwk", w_l1a_iwk, ND)):
            per = src.shape[2]
            W_[nm] = (off, per)
            for c in range(n):
                dma(nc.gpsimd, POOL_BF[:, off + c * per: off + (c + 1) * per], src[c], "dw")
            off += n * per
        for c in range(2):
            dma(nc.gpsimd, POOL_BF[:, 29440 + c * 512: 29440 + (c + 1) * 512], w_l1a_uv[c], "dw")
        dma(nc.gpsimd, s_iwrow[0:1, :], w_l1a_iw[:, :], "dw")
        dma(nc.sync, s_icol[:], c_icol_i[:, :])
        dma(nc.sync, s_iosm[:], c_iota_sm[:, :])
        dma(nc.sync, maskb[:, 0:2048], c_zeros2k[:, :])
        k.wait(nc.sync, "dq")
        for zi in range(8):
            zap = bass.AP(srd_dram.tensor, zi * 262144, [[2048, 128], [1, 2048]])
            dma(nc.sync, zap, maskb[:, 0:2048])
        barrier()

        def wsl(nm, c, o, width=128):
            o0, per = W_[nm]
            return POOL_BF[:, o0 + c * per + o: o0 + c * per + o + width]

        cqT = POOL_BF[:, 15104:23296]
        ckvT = POOL_BF[:, 23296:27392]
        ebuf2 = POOL_BF[:, 27392:29440]
        uv2 = POOL_BF[:, 29440:30464]

        def wstat_to(dst_fn, nm, nk, rhs, ochunks, scale=None):
            for oc in range(ochunks):
                for tq in range(4):
                    for c in range(nk):
                        ins = nc.tensor.matmul(PSA[:, tq * 512:(tq + 1) * 512],
                                               wsl(nm, c, oc * 128),
                                               rhs[:, c * T + tq * 512: c * T + tq * 512 + 512],
                                               start=(c == 0), stop=(c == nk - 1))
                edge(ins, "pe", nc.vector)
                k.wait(nc.vector, "dq")
                for tq in range(4):
                    if scale is None:
                        ins = V("tensor_copy", ebuf2[:, tq * 512:(tq + 1) * 512],
                                PSA[:, tq * 512:(tq + 1) * 512])
                    else:
                        ins = V("tensor_scalar", ebuf2[:, tq * 512:(tq + 1) * 512],
                                PSA[:, tq * 512:(tq + 1) * 512], scale, None, op0=ALU.mult)
                k.wait(nc.tensor, "dve")
                dst_fn(oc)

        for oc in range(2):
            for tq in range(4):
                for c in range(ND):
                    ins = nc.tensor.matmul(PSA[:, tq * 512:(tq + 1) * 512],
                                           wsl("dkv", c, oc * 128),
                                           ylnT[:, c * T + tq * 512: c * T + tq * 512 + 512],
                                           start=(c == 0), stop=(c == ND - 1))
            edge(ins, "pe", nc.vector)
            for tq in range(4):
                ins = V("tensor_copy", ckvT[:, oc * T + tq * 512: oc * T + tq * 512 + 512],
                        PSA[:, tq * 512:(tq + 1) * 512])
            k.wait(nc.tensor, "dve")
        def kc_out(oc):
            k.wait(nc.sync, "dve")
            dma(nc.sync, kc_dram[2 * oc], ebuf2[0:64, 0:2048])
            dma(nc.sync, kc_dram[2 * oc + 1], POOL_BF[64:128, 27392:29440])
        wstat_to(kc_out, "uk", 2, ckvT, 4)
        for oc in range(4):
            for tq in range(4):
                for c in range(ND):
                    ins = nc.tensor.matmul(PSA[:, tq * 512:(tq + 1) * 512],
                                           wsl("dq", c, oc * 128),
                                           ylnT[:, c * T + tq * 512: c * T + tq * 512 + 512],
                                           start=(c == 0), stop=(c == ND - 1))
            edge(ins, "pe", nc.vector)
            for tq in range(4):
                ins = V("tensor_copy", cqT[:, oc * T + tq * 512: oc * T + tq * 512 + 512],
                        PSA[:, tq * 512:(tq + 1) * 512])
            k.wait(nc.tensor, "dve")
        for st in range(NT):
            for c in range(2):
                ins = nc.tensor.matmul(PSA[:, 2048:2560],
                                       ckvT[:, c * T + st * 128: c * T + st * 128 + 128],
                                       uv2[:, c * 512:(c + 1) * 512],
                                       start=(c == 0), stop=(c == 1))
            edge(ins, "pe", nc.vector)
            ins = V("tensor_copy", vnat[:, st * 512:(st + 1) * 512], PSA[:, 2048:2560])
            k.wait(nc.tensor, "dve")
        def qc_out(oc):
            k.wait(nc.sync, "dve")
            dma(nc.sync, qc_dram[oc], ebuf2[:, 0:2048])
        wstat_to(qc_out, "uq", 4, cqT, 4, scale=SCALE)
        qrT = [POOL_BF[:, 23296:25344], POOL_BF[:, 25344:27392]]
        for oc in range(2):
            for tq in range(4):
                for c in range(4):
                    ins = nc.tensor.matmul(PSA[:, tq * 512:(tq + 1) * 512],
                                           wsl("qrR", c, oc * 128),
                                           cqT[:, c * T + tq * 512: c * T + tq * 512 + 512],
                                           start=(c == 0), stop=(c == 3))
            edge(ins, "pe", nc.vector)
            for tq in range(4):
                ins = V("tensor_copy", qrT[oc][:, tq * 512:(tq + 1) * 512],
                        PSA[:, tq * 512:(tq + 1) * 512])
            k.wait(nc.tensor, "dve")
        for tt in range(NT):
            t0 = tt * 128
            for c in range(4):
                ins = nc.tensor.matmul(PSA[:, 2048:2304],
                                       cqT[:, c * T + t0: c * T + t0 + 128],
                                       POOL_BF[:, W_["qr"][0] + c * 256: W_["qr"][0] + (c + 1) * 256],
                                       start=(c == 0), stop=(c == 3))
            edge(ins, "pe", nc.vector)
            V("tensor_scalar", POOL_F[:, 4608:4864], PSA[:, 2048:2304], 0.0, None, op0=ALU.max)
            ins = V("tensor_copy", POOL_F[:, 4864:5120], PSA[:, 2048:2304])
            k.wait(nc.scalar, "dve"); k.wait(nc.tensor, "dve")
            ins = A("activation", POOL_F[:, 5120:5376], POOL_F[:, 4864:5120], AF.Abs)
            ins = A("activation", POOL_F[:, 5120:5376], POOL_F[:, 5120:5376], AF.Exp, scale=-1.0)
            ins = A("activation", POOL_F[:, 5120:5376], POOL_F[:, 5120:5376], AF.Ln, bias=1.0)
            k.wait(nc.vector, "act"); k.wait(nc.vector, "dq")
            ins = V("tensor_tensor", ebuf2[:, 0:256], POOL_F[:, 4608:4864], POOL_F[:, 5120:5376], ALU.add)
            k.wait(nc.sync, "dve")
            dma(nc.sync, qh_dram[tt], ebuf2[:, 0:256])
        barrier()
        for st in range(NT):
            s0 = st * 128
            for c in range(ND):
                ins = nc.tensor.matmul(PSA[:, 2048:2080],
                                       ylnT[:, c * T + s0: c * T + s0 + 128],
                                       wsl("kr", c, 0, 32),
                                       start=(c == 0), stop=(c == ND - 1))
            edge(ins, "pe", nc.vector)
            V("tensor_scalar", POOL_F[:, 0:32], PSA[:, 2048:2080], 0.0, None, op0=ALU.max)
            ins = V("tensor_copy", POOL_F[:, 32:64], PSA[:, 2048:2080])
            k.wait(nc.scalar, "dve"); k.wait(nc.tensor, "dve")
            ins = A("activation", POOL_F[:, 64:96], POOL_F[:, 32:64], AF.Abs)
            ins = A("activation", POOL_F[:, 64:96], POOL_F[:, 64:96], AF.Exp, scale=-1.0)
            ins = A("activation", POOL_F[:, 64:96], POOL_F[:, 64:96], AF.Ln, bias=1.0)
            k.wait(nc.vector, "act"); k.wait(nc.vector, "dq")
            ins = V("tensor_tensor", POOL_F[:, 96:128], POOL_F[:, 0:32], POOL_F[:, 64:96], ALU.add)
            k.wait(nc.sync, "dve")
            dma(nc.sync, mu_dram[s0:s0 + 128, :], POOL_F[:, 96:128])

        barrier()
        what = [POOL_F[0:16, 4096:6144], POOL_F[0:16, 6144:8192]]
        for comp in range(2):
            for tq in range(4):
                for c in range(ND):
                    ins = nc.tensor.matmul(PSA[0:16, 2048 + tq * 512: 2048 + tq * 512 + 512],
                                           wsl("kr", c, comp * 16, 16),
                                           ylnT[:, c * T + tq * 512: c * T + tq * 512 + 512],
                                           start=(c == 0), stop=(c == ND - 1))
            edge(ins, "pe", nc.vector)
            V("tensor_scalar", POOL_F[0:16, 0:2048], PSA[0:16, 2048:4096], 0.0, None, op0=ALU.max)
            ins = V("tensor_copy", POOL_F[0:16, 2048:4096], PSA[0:16, 2048:4096])
            k.wait(nc.scalar, "dve"); k.wait(nc.tensor, "dve")
            ins = A("activation", POOL_F[0:16, 2048:4096], POOL_F[0:16, 2048:4096], AF.Abs)
            ins = A("activation", POOL_F[0:16, 2048:4096], POOL_F[0:16, 2048:4096], AF.Exp, scale=-1.0)
            ins = A("activation", POOL_F[0:16, 2048:4096], POOL_F[0:16, 2048:4096], AF.Ln, bias=1.0)
            k.wait(nc.vector, "act")
            ins = V("tensor_tensor", what[comp], POOL_F[0:16, 0:2048], POOL_F[0:16, 2048:4096], ALU.add)
            k.wait(nc.tensor, "dve")
        barrier()
        gstage = POOL_F[0:16, 2048:4096]
        wtmp = POOL_F[0:16, 0:2048]
        for comp in range(2):
            k.wait(nc.sync, "dve"); k.wait(nc.sync, "act"); k.wait(nc.sync, "dq")
            k.wait(nc.vector, "dw")
            dma(nc.sync, gstage, g_cswbar[0:16, 0:T] if comp == 0 else g_cswbar[0:16, T:2 * T])
            k.wait(nc.vector, "dq")
            V("tensor_tensor", wtmp, what[0], gstage, ALU.mult)
            k.wait(nc.sync, "dve")
            dma(nc.sync, gstage, g_cswbar[0:16, T:2 * T] if comp == 0 else g_cswbar[0:16, 0:T])
            k.wait(nc.vector, "dq")
            V("tensor_tensor", gstage, what[1], gstage, ALU.mult)
            if comp == 0:
                ins = V("tensor_sub", wtmp, wtmp, gstage)
            else:
                ins = V("tensor_add", wtmp, wtmp, gstage)
            k.wait(nc.gpsimd, "dve")
            for a4 in range(3):
                dma(nc.gpsimd, POOL_BF[a4 * 32 + comp * 16:a4 * 32 + comp * 16 + 16, 16384:18432], wtmp, "dw")

        barrier()
        for oc in range(2):
            for tq in range(4):
                for c in range(ND):
                    ins = nc.tensor.matmul(PSA[:, tq * 512:(tq + 1) * 512],
                                           wsl("iwq", c, oc * 128),
                                           ylnT[:, c * T + tq * 512: c * T + tq * 512 + 512],
                                           start=(c == 0), stop=(c == ND - 1))
            edge(ins, "pe", nc.vector)
            k.wait(nc.vector, "dq")
            for tq in range(4):
                ins = V("tensor_copy", ebuf2[:, tq * 512:(tq + 1) * 512],
                        PSA[:, tq * 512:(tq + 1) * 512])
            k.wait(nc.tensor, "dve"); k.wait(nc.sync, "dve")
            dma(nc.sync, qi_dram[oc], ebuf2[:, 0:2048])
        kiT = POOL_BF[0:64, 18688:20736]
        for tq in range(4):
            for c in range(ND):
                ins = nc.tensor.matmul(PSA[0:64, tq * 512:(tq + 1) * 512],
                                       wsl("iwk", c, 0, 64),
                                       ylnT[:, c * T + tq * 512: c * T + tq * 512 + 512],
                                       start=(c == 0), stop=(c == ND - 1))
        edge(ins, "pe", nc.vector)
        for tq in range(4):
            ins = V("tensor_copy", kiT[:, tq * 512:(tq + 1) * 512], PSA[0:64, tq * 512:(tq + 1) * 512])
        k.wait(nc.tensor, "dve")

        barrier()
        qsp = [POOL_F[:, 0:2048], POOL_F[:, 2048:4096]]
        for comp in range(2):
            src = POOL_BF[:, 23296 + comp * 2048: 23296 + (comp + 1) * 2048]
            V("tensor_scalar", qsp[comp], src, 0.0, None, op0=ALU.max)
            ins = V("tensor_copy", POOL_F[:, 4096:6144], src)
            k.wait(nc.scalar, "dve")
            ins = A("activation", POOL_F[:, 4096:6144], POOL_F[:, 4096:6144], AF.Abs)
            ins = A("activation", POOL_F[:, 4096:6144], POOL_F[:, 4096:6144], AF.Exp, scale=-1.0)
            ins = A("activation", POOL_F[:, 4096:6144], POOL_F[:, 4096:6144], AF.Ln, bias=1.0)
            k.wait(nc.vector, "act")
            ins = V("tensor_tensor", qsp[comp], qsp[comp], POOL_F[:, 4096:6144], ALU.add)
        k.wait(nc.sync, "dve")
        gc = POOL_F[:, 4096:6144]
        gs = POOL_F[:, 6144:8192]
        rot_out = [POOL_BF[:, 20736:22784], POOL_BF[:, 22784:24832]]
        rot_tmp = POOL_BF[:, 24832:26880]
        k.wait(nc.sync, "dve"); k.wait(nc.sync, "dq")
        dma(nc.sync, gc, g_csband_c[:, :])
        dma(nc.sync, gs, g_csband_s[:, :])
        k.wait(nc.vector, "dq")
        V("tensor_tensor", rot_out[0], qsp[0], gc, ALU.mult)
        V("tensor_tensor", rot_tmp, qsp[1], gs, ALU.mult)
        V("tensor_sub", rot_out[0], rot_out[0], rot_tmp)
        V("tensor_tensor", rot_out[1], qsp[0], gs, ALU.mult)
        V("tensor_tensor", rot_tmp, qsp[1], gc, ALU.mult)
        ins = V("tensor_add", rot_out[1], rot_out[1], rot_tmp)
        k.wait(nc.sync, "dve")
        dma(nc.sync, qb_dram[2], rot_out[0])
        dma(nc.sync, qb_dram[3], rot_out[1])
        k.wait(nc.vector, "dq")
        k.wait(nc.sync, "dve")
        k.wait(nc.sync, "dq")
        dma(nc.sync, gc[:, 0:128], g_csq2_c[:, :])
        dma(nc.sync, gs[:, 0:128], g_csq2_s[:, :])
        k.wait(nc.vector, "dq")
        V("tensor_tensor", rot_out[0][:, 0:128], qsp[0][:, 0:128], gc[:, 0:128], ALU.mult)
        V("tensor_tensor", rot_tmp[:, 0:128], qsp[1][:, 0:128], gs[:, 0:128], ALU.mult)
        V("tensor_sub", rot_out[0][:, 0:128], rot_out[0][:, 0:128], rot_tmp[:, 0:128])
        V("tensor_tensor", rot_out[1][:, 0:128], qsp[0][:, 0:128], gs[:, 0:128], ALU.mult)
        V("tensor_tensor", rot_tmp[:, 0:128], qsp[1][:, 0:128], gc[:, 0:128], ALU.mult)
        ins = V("tensor_add", rot_out[1][:, 0:128], rot_out[1][:, 0:128], rot_tmp[:, 0:128])
        k.wait(nc.sync, "dve")
        dma(nc.sync, qb_dram[4][:, 0:128], rot_out[0][:, 0:128])
        dma(nc.sync, qb_dram[5][:, 0:128], rot_out[1][:, 0:128])
        k.wait(nc.tensor, "dw"); k.wait(nc.tensor, "dq")
        ins = nc.tensor.matmul(PSA[:, 2048:2052], ones_b[0:1, 0:128], s_iwrow[0:1, 0:4], start=True, stop=True)
        edge(ins, "pe", nc.vector)
        ins = V("tensor_copy", s_iwb[:], PSA[:, 2048:2052])
        k.wait(nc.sync, "dve"); k.wait(nc.sync, "dq")
        dma(nc.sync, POOL_F[:, 5120:6144], g_csrot[:, 0:1024])
        dma(nc.sync, POOL_F[:, 6144:7168], g_csrot[:, 1024:2048])
        dma(nc.sync, zoneb[:, 0:128], c_zone[:, :])
        grc = POOL_F[:, 5120:6144]
        grs = POOL_F[:, 6144:7168]
        k.wait(nc.vector, "dq")
        V("tensor_scalar", s_tc[:, 0:1], s_icol[:], 2048, None, op0=ALU.mult)
        ins = V("tensor_scalar", s_tc[:, 1:2], s_icol[:], 32 - 2048, None, op0=ALU.mult)
        barrier()

        mug = POOL_F[:, 7168:8192]
        NBIG = 128 * 2048
        TT_DBG = 8

        def FA(col, *dims):
            a0 = POOL_F[:, col:col + 1]
            return bass.AP(a0.tensor, a0.offset, [[a0.ap[0][0], 128]] + [list(d) for d in dims])
        for tt in range(NT):
            t0 = tt * 128
            Wk = 128 * (tt + 1)
            nq = (Wk + 511) // 512
            k.wait(nc.sync, "pe"); k.wait(nc.sync, "dve"); k.wait(nc.sync, "dq")
            for h8 in range(8):
                qdst = POOL_BF[0:64, 32256 + h8 * 128:32256 + (h8 + 1) * 128] if h8 < 4 \
                    else POOL_BF[64:128, 32256 + (h8 - 4) * 128:32256 + (h8 - 3) * 128]
                dma(nc.sync, qdst,
                    qc_dram[h8 // 2][(h8 % 2) * 64:(h8 % 2) * 64 + 64, t0:t0 + 128])
            for hi in range(4):
                dma(nc.sync, POOL_BF[0:64, 33280 + hi * 128:33280 + (hi + 1) * 128],
                    qi_dram[hi // 2][(hi % 2) * 64:(hi % 2) * 64 + 64, t0:t0 + 128])
            dma(nc.sync, qhtt[:, 0:256], qh_dram[tt])
            for h in range(8):
                blk, ab = h // 3, h % 3
                for comp in range(2):
                    dma(nc.sync, POOL_BF[ab * 32 + comp * 16:ab * 32 + comp * 16 + 16,
                                         8192 + blk * 128:8192 + (blk + 1) * 128],
                        qb_dram[2 + comp][h * 16:(h + 1) * 16, t0:t0 + 128])
            if tt == 0:
                for h in range(8):
                    blk, ab = h // 3, h % 3
                    for comp in range(2):
                        dma(nc.sync, POOL_BF[ab * 32 + comp * 16:ab * 32 + comp * 16 + 16,
                                             8576 + blk * 128:8576 + (blk + 1) * 128],
                            qb_dram[4 + comp][h * 16:(h + 1) * 16, 0:128])
            k.wait(nc.tensor, "dq")
            for hi in range(4):
                lh = POOL_BF[0:64, 33280 + hi * 128:33280 + (hi + 1) * 128]
                for ch in range(nq):
                    w = min(512, Wk - ch * 512)
                    ins = nc.tensor.matmul(PSA[:, ch * 512:ch * 512 + w], lh, kiT[:, ch * 512:ch * 512 + w],
                                           start=True, stop=True)
                edge(ins, "pe", nc.scalar)
                k.wait(nc.scalar, "dve")
                for ch in range(nq):
                    w = min(512, Wk - ch * 512)
                    ins = A("activation", f_relu[:, ch * 512:ch * 512 + w], PSA[:, ch * 512:ch * 512 + w], AF.Relu)
                k.wait(nc.vector, "act"); k.wait(nc.tensor, "act")
                if hi == 0:
                    ins = V("tensor_scalar", fI[:, 0:Wk], f_relu[:, 0:Wk], s_iwb[:, 0:1], None, op0=ALU.mult)
                else:
                    ins = V("scalar_tensor_tensor", fI[:, 0:Wk], f_relu[:, 0:Wk], s_iwb[:, hi:hi + 1],
                            fI[:, 0:Wk], op0=ALU.mult, op1=ALU.add)
            ins = V("tensor_scalar", fI[:, t0 + 64:Wk], fI[:, t0 + 64:Wk], -2e30, None, op0=ALU.add)
            B0 = max(0, t0 - 64)
            bw = (t0 + 64) - B0
            V("tensor_scalar", s_th[:], iota_col[:], float(t0 - 63 - B0), None, op0=ALU.add)
            V("tensor_scalar", f_relu[:, 0:bw], s_iosm[:, 0:bw], s_th[:, 0:1], None, op0=ALU.is_ge)
            ins = V("scalar_tensor_tensor", fI[:, B0:B0 + bw], f_relu[:, 0:bw], -2e30, fI[:, B0:B0 + bw],
                    op0=ALU.mult, op1=ALU.add)
            if b == 0 and tt == TT_DBG:
                dump("selI", fI[:, 0:2048], [128, 2048])
            for r in range(4):
                V("max", out=s_m8[:, 0:8], in_=fI[:, 0:Wk])
                V("max_index", out=s_iu[:, r * 8:(r + 1) * 8], in_max=s_m8[:, 0:8], in_values=fI[:, 0:Wk])
                V("tensor_copy", s_v32[:, r * 8:(r + 1) * 8], s_m8[:, 0:8])
                V("match_replace", out=fI[:, 0:Wk], in_to_replace=s_m8[:, 0:8], in_values=fI[:, 0:Wk],
                  imm_value=-3e30)
            V("tensor_copy", s_ii[:], s_iu[:])
            V("tensor_scalar", s_ig[:], s_ii[:], 0, 2047, op0=ALU.max, op1=ALU.min)
            V("tensor_scalar", f_sc[:, 0:32], s_v32[:], -1e29, None, op0=ALU.is_le)
            V("tensor_copy", s_isc[:], f_sc[:, 0:32])
            V("tensor_copy", s_ii[:], s_iosm[:, 0:32])
            V("tensor_scalar", s_ii[:], s_ii[:], NBIG, None, op0=ALU.add)
            V("tensor_tensor", s_ii[:], s_ii[:], s_ig[:], ALU.subtract)
            V("tensor_tensor", s_ii[:], s_ii[:], s_tc[:, 1:2].to_broadcast([128, 32]), ALU.add)
            V("tensor_tensor", s_isc[:], s_isc[:], s_ii[:], ALU.mult)
            V("tensor_tensor", s_isc[:], s_isc[:], s_ig[:], ALU.add)
            ins = V("tensor_tensor", s_isc[:], s_isc[:], s_tc[:, 0:1].to_broadcast([128, 32]), ALU.add)
            if b == 0 and tt == TT_DBG:
                dump("selidx", s_iu[:], [128, 32], U32)
                dump("selv", s_v32[:], [128, 32])

            V("tensor_scalar", f_relu[:, 0:Wk], fI[:, 0:Wk], -2.5e30, None, op0=ALU.is_le)
            tA = POOL_F[:, 4096:4352]; tB = POOL_F[:, 4352:4608]
            tC = POOL_F[:, 4608:4864]
            if tt >= 1:
                B0m = t0 - 128
                if B0m > 0:
                    V("tensor_scalar", maskb[:, 0:B0m], f_relu[:, 0:B0m], 2e30, -2e30,
                      op0=ALU.mult, op1=ALU.add)
                V("tensor_scalar", s_th[:], iota_col[:], 65.0, None, op0=ALU.add)
                V("tensor_scalar", tA[:, 0:256], s_iosm[:, 0:256], s_th[:, 0:1], None, op0=ALU.is_ge)
                V("tensor_scalar", s_th[:], iota_col[:], 128.0, None, op0=ALU.add)
                V("tensor_scalar", tB[:, 0:256], s_iosm[:, 0:256], s_th[:, 0:1], None, op0=ALU.is_le)
                V("tensor_tensor", stripb[:, 0:256], tA[:, 0:256], tB[:, 0:256], ALU.mult)
                V("tensor_tensor", tC[:, 0:256], f_relu[:, B0m:B0m + 256], tA[:, 0:256], ALU.mult)
                V("tensor_sub", tC[:, 0:256], f_relu[:, B0m:B0m + 256], tC[:, 0:256])
                V("tensor_tensor", tC[:, 0:256], tC[:, 0:256], stripb[:, 0:256], ALU.add)
                V("tensor_scalar", maskb[:, B0m:B0m + 256], tC[:, 0:256], 2e30, -2e30,
                  op0=ALU.mult, op1=ALU.add)
            else:
                V("tensor_scalar", s_th[:], iota_col[:], -63.0, None, op0=ALU.add)
                V("tensor_scalar", tA[:, 0:128], s_iosm[:, 0:128], s_th[:, 0:1], None, op0=ALU.is_ge)
                V("tensor_scalar", tB[:, 0:128], s_iosm[:, 0:128], iota_col[:, 0:1], None, op0=ALU.is_le)
                V("tensor_tensor", stripb[:, 0:128], tA[:, 0:128], tB[:, 0:128], ALU.mult)
                V("tensor_tensor", tC[:, 0:128], f_relu[:, 0:128], tA[:, 0:128], ALU.mult)
                V("tensor_sub", tC[:, 0:128], f_relu[:, 0:128], tC[:, 0:128])
                V("tensor_tensor", tC[:, 0:128], tC[:, 0:128], stripb[:, 0:128], ALU.add)
                V("tensor_tensor", tC[:, 0:128], tC[:, 0:128], zoneb[:, 0:128], ALU.add)
                V("tensor_scalar", tC[:, 0:128], tC[:, 0:128], 1.0, None, op0=ALU.min)
                V("tensor_scalar", maskb[:, 0:128], tC[:, 0:128], 2e30, -2e30,
                  op0=ALU.mult, op1=ALU.add)
            if b == 0 and tt == TT_DBG:
                dump("maskb", maskb[:, 0:2048], [128, 2048], BF16)

            k.wait(nc.gpsimd, "dve")
            for m in range(32):
                ins = nc.gpsimd.indirect_dma_start(
                    out=mug[:, m * 32:(m + 1) * 32], out_offset=None,
                    in_=mu_dram[:, :],
                    in_offset=IndirectOffsetOnAxis(s_ig[:, m:m + 1], 0))
                k.inc(ins, "dw", 16)
            if b == 0 and tt == TT_DBG:
                k.wait(nc.sync, "dw")
                dump("mug", mug[:, 0:1024], [128, 1024])

            k.wait(nc.vector, "dw")
            GT = 4096
            V("tensor_tensor", POOL_F[:, 2048:3072], mug[:, 0:1024], grc[:, 0:1024], ALU.mult)
            V("tensor_tensor", POOL_F[:, 3072:4096], mug[:, 0:1024], grs[:, 0:1024], ALU.mult)
            V("tensor_tensor", FA(GT, [32, 32], [1, 16]), FA(2048, [32, 32], [1, 16]),
              FA(3072 + 16, [32, 32], [1, 16]), ALU.subtract)
            V("tensor_tensor", FA(GT + 16, [32, 32], [1, 16]), FA(3072, [32, 32], [1, 16]),
              FA(2048 + 16, [32, 32], [1, 16]), ALU.add)

            k.wait(nc.sync, "dve"); k.wait(nc.sync, "dq")
            dma(nc.sync, POOL_F[:, 512:544], g_csq64n[t0:t0 + 128, :])
            k.wait(nc.vector, "dq")
            cosb = FA(512, [0, 8], [1, 16]); sinb = FA(528, [0, 8], [1, 16])
            qh1 = bass.AP(qhtt.tensor, qhtt.offset, [[qhtt.ap[0][0], 128], [32, 8], [1, 16]])
            qh2 = bass.AP(qhtt.tensor, qhtt.offset + 16, [[qhtt.ap[0][0], 128], [32, 8], [1, 16]])
            V("tensor_tensor", FA(256, [32, 8], [1, 16]), qh1, cosb, ALU.mult)
            V("tensor_tensor", FA(0, [32, 8], [1, 16]), qh2, sinb, ALU.mult)
            V("tensor_tensor", FA(0, [32, 8], [1, 16]), FA(256, [32, 8], [1, 16]),
              FA(0, [32, 8], [1, 16]), ALU.subtract)
            V("tensor_tensor", FA(256, [32, 8], [1, 16]), qh1, sinb, ALU.mult)
            V("tensor_tensor", FA(16, [32, 8], [1, 16]), qh2, cosb, ALU.mult)
            V("tensor_tensor", FA(16, [32, 8], [1, 16]), FA(256, [32, 8], [1, 16]),
              FA(16, [32, 8], [1, 16]), ALU.add)

            for h in range(8):
                V("tensor_tensor", POOL_F[:, 2048:3072], POOL_F[:, GT:GT + 1024],
                  FA(h * 32, [0, 32], [1, 32]), ALU.mult)
                V("tensor_tensor", FA(3072, [16, 32], [1, 16]), FA(2048, [32, 32], [1, 16]),
                  FA(2048 + 16, [32, 32], [1, 16]), ALU.add)
                V("tensor_tensor", FA(3584, [8, 32], [1, 8]), FA(3072, [16, 32], [1, 8]),
                  FA(3072 + 8, [16, 32], [1, 8]), ALU.add)
                V("tensor_tensor", FA(3840, [4, 32], [1, 4]), FA(3584, [8, 32], [1, 4]),
                  FA(3584 + 4, [8, 32], [1, 4]), ALU.add)
                V("tensor_tensor", FA(3968, [2, 32], [1, 2]), FA(3840, [4, 32], [1, 2]),
                  FA(3840 + 2, [4, 32], [1, 2]), ALU.add)
                V("tensor_tensor", FA(544 + h, [8, 32]), FA(3968, [2, 32]),
                  FA(3968 + 1, [2, 32]), ALU.add)
            ins = V("tensor_copy", asc[:, 0:256], POOL_F[:, 544:800])
            if b == 0 and tt == TT_DBG:
                dump("pack", POOL_F[:, 544:800], [128, 256])

            k.wait(nc.gpsimd, "dve"); k.wait(nc.gpsimd, "dq")
            for m in range(32):
                ins = nc.gpsimd.indirect_dma_start(
                    out=srd_dram[:, :],
                    out_offset=IndirectOffsetOnAxis(s_isc[:, m:m + 1], 0),
                    in_=asc[:, m * 8:(m + 1) * 8], in_offset=None)
                k.inc(ins, "dw", 16)

            bdm_all = POOL_BF[:, 14336:16384]
            srdq_full = POOL_BF[:, 22784:26880]
            sc_bf = [POOL_BF[:, 32768:33280], POOL_BF[:, 33280:33792]]
            PST = 33792

            if tt == 0:
                k.wait(nc.sync, "dw"); k.wait(nc.sync, "dve")
                srd_ap0 = bass.AP(srd_dram.tensor, 0, [[16384, 128], [1, 1024]])
                dma(nc.sync, srdq_full[:, 0:1024], srd_ap0)
                for h in range(8):
                    blk, ab = h // 3, h % 3
                    base = 0 if h < 4 else 64
                    kc_h = POOL_BF[base:base + 64, 12288:12416]
                    k.wait(nc.sync, "dve"); k.wait(nc.sync, "pe")
                    dma(nc.sync, kc_h, kc_dram[h][:, 0:128])
                    k.wait(nc.tensor, "dq"); k.wait(nc.tensor, "dve")
                    qc_h = POOL_BF[0:64, 32256 + h * 128:32256 + (h + 1) * 128] if h < 4 \
                        else POOL_BF[64:128, 32256 + (h - 4) * 128:32256 + (h - 3) * 128]
                    nc.tensor.matmul(PSA[:, 0:128], qc_h, kc_h, start=True, stop=True)
                    ps_band = PSA[:, 2048:2304]
                    ps_r2 = PSA[:, 2560:2688]
                    nc.tensor.matmul(ps_band[:, 0:128],
                                     qbar_tt[ab * 32:(ab + 1) * 32, blk * 128:blk * 128 + 128],
                                     wbar[ab * 32:(ab + 1) * 32, 0:128], start=True, stop=True)
                    ins = nc.tensor.matmul(ps_r2[:, 0:128],
                                           q2p_tt[ab * 32:(ab + 1) * 32, blk * 128:blk * 128 + 128],
                                           wbar[ab * 32:(ab + 1) * 32, 0:128], start=True, stop=True)
                    edge(ins, "pe", nc.vector)
                    k.wait(nc.vector, "dq")
                    sA = bass.AP(srdq_full.tensor, srdq_full.offset + h, [[PST, 128], [8, 128]])
                    V("tensor_tensor", fI[:, 0:128], PSA[:, 0:128], sA, ALU.add)
                    bdm = POOL_F[:, 4096:4352]
                    V("tensor_tensor", bdm[:, 0:128], ps_band[:, 0:128], stripb[:, 0:128], ALU.mult)
                    V("tensor_tensor", fI[:, 0:128], fI[:, 0:128], bdm[:, 0:128], ALU.add)
                    r1 = POOL_F[:, 4352:4480]
                    V("tensor_tensor", r1[:, 0:128], ps_r2[:, 0:128], sA, ALU.subtract)
                    V("tensor_sub", r1[:, 0:128], r1[:, 0:128], bdm[:, 0:128])
                    V("tensor_tensor", r1[:, 0:128], r1[:, 0:128], zoneb[:, 0:128], ALU.mult)
                    V("tensor_tensor", fI[:, 0:128], fI[:, 0:128], r1[:, 0:128], ALU.add)
                    V("tensor_tensor", fI[:, 0:128], fI[:, 0:128], maskb[:, 0:128], ALU.add)
                    V("reduce_max", out=s_mx[:, 0:1], in_=fI[:, 0:128], axis=AX)
                    if b == 0 and tt == TT_DBG and h == 0:
                        dump("slab", fI[:, 0:2048], [128, 2048])
                    V("tensor_scalar", s_mx[:, 1:2], s_mx[:, 0:1], -1.0, None, op0=ALU.mult)
                    k.wait(nc.scalar, "dve")
                    A("activation", fI[:, 0:128], fI[:, 0:128], AF.Exp, scale=1.0,
                      bias=s_mx[:, 1:2], accum_out=s_mx[:, 2:3])
                    k.wait(nc.vector, "act")
                    V("reciprocal", s_mx[:, 3:4], s_mx[:, 2:3])
                    V("tensor_scalar", fI[:, 0:128], fI[:, 0:128], s_mx[:, 3:4], None, op0=ALU.mult)
                    k.wait(nc.tensor, "dve")
                    ins = nc.tensor.transpose(PSA[:, 3584:3712], fI[:, 0:128], ident[:])
                    edge(ins, "pe", nc.vector)
                    V("tensor_copy", slabT[:, 0:128], PSA[:, 3584:3712])
                    k.wait(nc.tensor, "dve")
                    ps_o = PSA[0:64, 3072:3200]
                    ins = nc.tensor.matmul(ps_o, vnat[:, h * 64:h * 64 + 64],
                                           slabT[:, 0:128], start=True, stop=True)
                    edge(ins, "pe", nc.vector)
                    k.wait(nc.vector, "dob")
                    ins = V("tensor_copy", obh[0:64, 0:128], ps_o)
                    k.wait(nc.sync, "dve")
                    dma(nc.sync, ol1a_dram[h][0:64, t0:t0 + 128], obh[0:64, 0:128], "dob")
            else:
                ps_band = PSA[:, 2048:2304]
                for h in range(8):
                    blk, ab = h // 3, h % 3
                    k.wait(nc.tensor, "dve")
                    nc.tensor.matmul(ps_band[:, 0:128],
                                     qbar_tt[ab * 32:(ab + 1) * 32, blk * 128:blk * 128 + 128],
                                     wbar[ab * 32:(ab + 1) * 32, t0 - 128:t0], start=True, stop=True)
                    ins = nc.tensor.matmul(ps_band[:, 128:256],
                                           qbar_tt[ab * 32:(ab + 1) * 32, blk * 128:blk * 128 + 128],
                                           wbar[ab * 32:(ab + 1) * 32, t0:t0 + 128], start=True, stop=True)
                    edge(ins, "pe", nc.vector)
                    V("tensor_tensor", bdm_all[:, h * 256:(h + 1) * 256],
                      ps_band[:, 0:256], stripb[:, 0:256], ALU.mult)
                k.wait(nc.sync, "dw")
                k.wait(nc.sync, "dsb0"); k.wait(nc.sync, "dsb1")
                for ch in range(nq):
                    c0 = ch * 512
                    cw = min(512, Wk - c0)
                    k.wait(nc.sync, "dve"); k.wait(nc.sync, "pe"); k.wait(nc.sync, "dq")
                    srd_apc = bass.AP(srd_dram.tensor, c0 * 8, [[16384, 128], [1, cw * 8]])
                    dma(nc.sync, srdq_full[:, 0:cw * 8], srd_apc)
                    for h in range(8):
                        base = 0 if h < 4 else 64
                        dma(nc.sync, POOL_BF[base:base + 64, 12288 + (h % 4) * 512:12288 + (h % 4) * 512 + cw],
                            kc_dram[h][:, c0:c0 + cw])
                    k.wait(nc.tensor, "dq"); k.wait(nc.tensor, "dve")
                    for h in range(8):
                        qc_h = POOL_BF[0:64, 32256 + h * 128:32256 + (h + 1) * 128] if h < 4 \
                            else POOL_BF[64:128, 32256 + (h - 4) * 128:32256 + (h - 3) * 128]
                        kc_h = POOL_BF[0:64, 12288 + h * 512:12288 + h * 512 + cw] if h < 4 \
                            else POOL_BF[64:128, 12288 + (h - 4) * 512:12288 + (h - 4) * 512 + cw]
                        ins = nc.tensor.matmul(PSA[:, h * 512:h * 512 + cw], qc_h, kc_h,
                                               start=True, stop=True)
                    edge(ins, "pe", nc.vector)
                    k.wait(nc.vector, "dq")
                    d0 = t0 - 128
                    for h in range(8):
                        sA = bass.AP(srdq_full.tensor, srdq_full.offset + h, [[PST, 128], [8, cw]])
                        V("tensor_tensor", fI[:, 0:cw], PSA[:, h * 512:h * 512 + cw], sA, ALU.add)
                        o1 = max(d0, c0); o2 = min(t0 + 128, c0 + cw)
                        if o1 < o2:
                            V("tensor_tensor", fI[:, o1 - c0:o2 - c0], fI[:, o1 - c0:o2 - c0],
                              bdm_all[:, h * 256 + (o1 - d0):h * 256 + (o2 - d0)], ALU.add)
                        k.wait(nc.vector, "dsc%d" % (h % 2))
                        V("tensor_tensor", sc_bf[h % 2][:, 0:cw], fI[:, 0:cw],
                          maskb[:, c0:c0 + cw], ALU.add)
                        V("reduce_max", out=s_cm[:, h * 4 + ch:h * 4 + ch + 1],
                          in_=sc_bf[h % 2][:, 0:cw], axis=AX)
                        k.wait(nc.sync, "dve")
                        dma(nc.sync, sc_dram[h][:, c0:c0 + cw], sc_bf[h % 2][:, 0:cw], "dsc%d" % (h % 2))
                for h in range(8):
                    k.wait(nc.sync, "dve"); k.wait(nc.sync, "act")
                    k.wait(nc.sync, "dsc0"); k.wait(nc.sync, "dsc1")
                    sch = srdq[h % 2]
                    dma(nc.sync, sch[:, 0:Wk], sc_dram[h][:, 0:Wk], "dsb%d" % (h % 2))
                    if b == 0 and tt == TT_DBG and h == 0:
                        k.wait(nc.sync, "dsb0")
                        dump("slab", sch[:, 0:2048], [128, 2048], BF16)
                    V("reduce_max", out=s_mx[:, 0:1], in_=s_cm[:, h * 4:h * 4 + nq], axis=AX)
                    V("tensor_scalar", s_mx[:, 1:2], s_mx[:, 0:1], -1.0, None, op0=ALU.mult)
                    k.wait(nc.scalar, "dve"); k.wait(nc.scalar, "dsb%d" % (h % 2))
                    A("activation", fI[:, 0:Wk], sch[:, 0:Wk], AF.Exp, scale=1.0,
                      bias=s_mx[:, 1:2], accum_out=s_mx[:, 2:3])
                    k.wait(nc.vector, "act")
                    V("reciprocal", s_mx[:, 3:4], s_mx[:, 2:3])
                    V("tensor_scalar", fI[:, 0:Wk], fI[:, 0:Wk], s_mx[:, 3:4], None, op0=ALU.mult)
                    k.wait(nc.tensor, "dve")
                    for j in range(tt + 1):
                        ins = nc.tensor.transpose(PSA[:, 3584:3712], fI[:, j * 128:(j + 1) * 128], ident[:])
                        edge(ins, "pe", nc.vector)
                        V("tensor_copy", slabT[:, j * 128:(j + 1) * 128], PSA[:, 3584:3712])
                        k.wait(nc.tensor, "dve")
                    ps_o = PSA[0:64, 3072:3200]
                    for j in range(tt + 1):
                        ins = nc.tensor.matmul(ps_o, vnat[:, j * 512 + h * 64:j * 512 + h * 64 + 64],
                                               slabT[:, j * 128:(j + 1) * 128],
                                               start=(j == 0), stop=(j == tt))
                    edge(ins, "pe", nc.vector)
                    k.wait(nc.vector, "dob")
                    ins = V("tensor_copy", obh[0:64, 0:128], ps_o)
                    k.wait(nc.sync, "dve")
                    dma(nc.sync, ol1a_dram[h][0:64, t0:t0 + 128], obh[0:64, 0:128], "dob")
                    if b == 0 and tt == TT_DBG and h == 0:
                        dump("oh", obh[0:64, 0:128], [64, 128], BF16)

        barrier()
        wout_sb = POOL_BF[:, 12288:16384]
        for c4 in range(4):
            dma(nc.gpsimd, wout_sb[:, c4 * 1024:(c4 + 1) * 1024], w_l1a_wout[c4], "dw")
        obuf = POOL_BF[:, 18688:20736]
        k.wait(nc.tensor, "dw")
        for tq5 in range(4):
            k.wait(nc.sync, "pe"); k.wait(nc.sync, "dve")
            for c4 in range(4):
                dma(nc.sync, obuf[0:64, c4 * 512:(c4 + 1) * 512],
                    ol1a_dram[2 * c4][0:64, tq5 * 512:(tq5 + 1) * 512])
                dma(nc.sync, obuf[64:128, c4 * 512:(c4 + 1) * 512],
                    ol1a_dram[2 * c4 + 1][0:64, tq5 * 512:(tq5 + 1) * 512])
            k.wait(nc.tensor, "dq"); k.wait(nc.tensor, "dve")
            for dt in range(ND):
                for c4 in range(4):
                    ins = nc.tensor.matmul(PSA[:, dt * 512:(dt + 1) * 512],
                                           wout_sb[:, c4 * 1024 + dt * 128:c4 * 1024 + (dt + 1) * 128],
                                           obuf[:, c4 * 512:(c4 + 1) * 512],
                                           start=(c4 == 0), stop=(c4 == 3))
            edge(ins, "pe", nc.vector)
            for dt in range(ND):
                V("tensor_tensor", xT[:, dt * T + tq5 * 512:dt * T + (tq5 + 1) * 512],
                  xT[:, dt * T + tq5 * 512:dt * T + (tq5 + 1) * 512],
                  PSA[:, dt * 512:(dt + 1) * 512], ALU.add)
            k.wait(nc.tensor, "dve")
        barrier()
    for b in range(NB):
        transpose_in(b)
        if b == 0:
            dump("xT", xT[:], [128, ND * T])
        if "l0a" in blocks:
            layer_norm()
            l0_attn()
        if "l0f" in blocks:
            layer_norm()
            if b == 0:
                dump("ylnT", ylnT[:], [128, ND * T], BF16)
            l0_ffn()
            if b == 0:
                dump("cw", cw_nat[:], [128, NT * 16])
                dump("xT2", xT[:], [128, ND * T])
        if "l1a" in blocks:
            layer_norm()
            l1_attn(b)
        if "l1f" in blocks:
            layer_norm()
            l1_ffn()
        transpose_out(b)

    k.wait(nc.sync, "dq")

    return k, wdecl, taps



def make_consts():
    c = {}
    c["c_ident"] = np.eye(128, dtype=np.float32)
    c["c_identb"] = np.eye(128, dtype=np.float32).astype(ml_dtypes.bfloat16)
    c["c_ones_b"] = np.ones((128, 512), dtype=np.float32).astype(ml_dtypes.bfloat16)
    c["c_iota_col"] = np.arange(128, dtype=np.float32).reshape(128, 1)
    c["c_eps"] = np.full((128, 1), EPS_LN, dtype=np.float32)
    OFFS = list(range(-10, 11)) + list(range(-127, -117)) + list(range(118, 128))
    dg = np.zeros((41, 128, 128), dtype=np.float32)
    p = np.arange(128)
    for i, o in enumerate(OFFS):
        n = p + o
        valid = (n >= 0) & (n < 128)
        dg[i, p[valid], n[valid]] = 1.0
    c["c_diag"] = np.ascontiguousarray(np.transpose(dg, (1, 0, 2)).reshape(128, 41 * 128)).astype(ml_dtypes.bfloat16)
    th = (1.0 / 10000.0 ** (2.0 * np.arange(16) / 32)).astype(np.float64)
    t = np.arange(T, dtype=np.float64)
    irow = np.tile(np.tile(np.arange(16), 8), 1)
    ang64 = (t[None, :] - 64) * th[irow][:, None]
    c["g_csq64hi_c"] = np.cos(ang64).astype(np.float32)
    c["g_csq64hi_s"] = np.sin(ang64).astype(np.float32)
    SCALE = (64 + 32) ** -0.5
    angb = (2 * t[None, :] - 63) * th[irow][:, None]
    c["g_csband_c"] = (SCALE * np.cos(angb)).astype(np.float32)
    c["g_csband_s"] = (SCALE * np.sin(angb)).astype(np.float32)
    t2 = np.arange(128, dtype=np.float64)
    ang2 = t2[None, :] * th[irow][:, None]
    c["g_csq2_c"] = (SCALE * np.cos(ang2)).astype(np.float32)
    c["g_csq2_s"] = (SCALE * np.sin(ang2)).astype(np.float32)
    angw = np.arange(T, dtype=np.float64)[None, :] * th[:16][:, None]
    c["g_cswbar"] = np.concatenate([np.cos(angw), np.sin(angw)], axis=1).astype(np.float32)
    m = np.arange(32, dtype=np.float64)
    angm = m[:, None, None] * th[None, None, :16].reshape(1, 1, 16)
    angm = np.broadcast_to(angm, (32, 2, 16)).reshape(1024)
    c["g_csrot"] = np.broadcast_to(
        np.concatenate([np.cos(angm), np.sin(angm)]).astype(np.float32)[None, :], (128, 2048)).copy()
    ang64n = (t[:, None] - 64) * th[None, :]
    c["g_csq64n"] = (SCALE * np.concatenate(
        [np.cos(ang64n), np.sin(ang64n)], axis=1)).astype(np.float32)
    c["c_iota_s"] = np.broadcast_to(np.arange(T, dtype=np.float32), (128, T)).copy()
    c["c_icol_i"] = np.arange(128, dtype=np.int32).reshape(128, 1)
    c["c_iota_sm"] = np.broadcast_to(np.arange(256, dtype=np.float32), (128, 256)).copy()
    c["c_zeros2k"] = np.zeros((128, 2048), ml_dtypes.bfloat16)
    c["c_t94"] = (np.arange(128) <= 94).astype(np.float32).reshape(128, 1)
    c["c_t63"] = (np.arange(128) <= 63).astype(np.float32).reshape(128, 1)
    pz = np.arange(128)[:, None]; sz = np.arange(128)[None, :]
    zone = ((pz <= 63) & (sz <= 95)) | ((pz >= 64) & (pz <= 94) & (sz > pz) & (sz <= 95))
    c["c_zone"] = zone.astype(np.float32).astype(ml_dtypes.bfloat16)
    return c


def prep_weights(inp):
    w = {}
    sW1 = inp["L0_ffn_sW1"]
    w["w_sW1"] = np.ascontiguousarray(sW1.reshape(ND, 128, 1024))
    sW2 = inp["L0_ffn_sW2"]
    w["w_sW2"] = np.ascontiguousarray(sW2.reshape(ND, 128, 1024))
    w["w_eW1"] = np.ascontiguousarray(inp["L0_ffn_eW1"].reshape(15, ND, 128, 1024))
    w["w_eW2"] = np.ascontiguousarray(inp["L0_ffn_eW2"].reshape(15, ND, 128, 1024))
    w["w_routerW"] = np.ascontiguousarray(inp["L0_ffn_router_W"].reshape(ND, 128, 15))
    w["w_routerB"] = np.broadcast_to(inp["L0_ffn_router_b"][None, :], (128, 15)).copy()
    w["w_sb1c"] = np.ascontiguousarray(inp["L0_ffn_sb1"].reshape(ND, 128).T)
    w["w_sb2r"] = inp["L0_ffn_sb2"].reshape(1, D).copy()
    w["w_eb1c"] = np.ascontiguousarray(np.transpose(inp["L0_ffn_eb1"].reshape(15, ND, 128), (0, 2, 1)))
    w["w_eb2"] = inp["L0_ffn_eb2"].copy()
    Win = inp["L0_attn_W_in"]; Wout = inp["L0_attn_W_out"]
    w["w_l0a_WinT"] = np.ascontiguousarray(Win.T.reshape(ND, 128, 1024))
    w["w_l0a_Win"] = np.ascontiguousarray(Win.reshape(ND, 128, 1024))
    w["w_l0a_WoutT"] = np.ascontiguousarray(Wout.T.reshape(ND, 128, 1024))
    w["w_l0a_Wc"] = np.ascontiguousarray(inp["L0_attn_Wc"])
    w["w_l0a_binc"] = np.ascontiguousarray(inp["L0_attn_b_in"].reshape(ND, 128).T)
    w["w_l0a_binsl"] = np.ascontiguousarray(inp["L0_attn_b_in"].reshape(ND, 128, 1))
    w["w_l0a_bout"] = inp["L0_attn_b_out"].reshape(1, D).copy()
    w["w_l0a_A_rep"] = np.broadcast_to(inp["L0_attn_A"].reshape(1, 672), (128, 672)).copy()
    w["w_l0a_base_cols"] = np.ascontiguousarray(np.transpose(inp["L0_attn_base"], (2, 0, 1)).reshape(128, 168))
    w["w_l0a_alphas"] = inp["L0_attn_alphas"].reshape(1, 8).copy()
    w["w_l0a_Vcols"] = np.ascontiguousarray(np.transpose(inp["L0_attn_V"], (2, 0, 1)).reshape(128, 32))
    w["w_l1f_W1T"] = np.ascontiguousarray(inp["L1_ffn_W1"].T.reshape(ND, 128, 4096))
    w["w_l1f_W2T"] = np.ascontiguousarray(inp["L1_ffn_W2"].T.reshape(32, 128, 1024))
    w["w_l1f_routerWT"] = np.ascontiguousarray(inp["L1_ffn_router_W"].T.reshape(ND, 128, 512))
    w["w_l1f_routerBr"] = inp["L1_ffn_router_b"].reshape(1, 512).copy()
    w["w_l1f_rawUT"] = np.ascontiguousarray(inp["L1_ffn_raw_U"].T.reshape(ND, 128, 512))
    w["w_l1f_rawU_rows"] = np.ascontiguousarray(inp["L1_ffn_raw_U"].reshape(4, 128, 1024))
    w["w_l1f_rawV_rows"] = np.ascontiguousarray(inp["L1_ffn_raw_V"].reshape(4, 128, 1024))
    w["w_l1f_lngc"] = np.ascontiguousarray(inp["L1_ffn_ln_g"].reshape(ND, 128).T)
    w["w_l1f_lnbc"] = np.ascontiguousarray(inp["L1_ffn_ln_b"].reshape(ND, 128).T)
    w["w_l1f_gammac"] = np.ascontiguousarray(inp["L1_ffn_gamma"].reshape(ND, 128).T)
    w["w_l1a_dkv"] = np.ascontiguousarray(inp["L1_attn_W_dkv"].reshape(ND, 128, 256))
    w["w_l1a_uk"] = np.ascontiguousarray(inp["L1_attn_W_uk"].reshape(2, 128, 512))
    w["w_l1a_uv"] = np.ascontiguousarray(inp["L1_attn_W_uv"].reshape(2, 128, 512))
    w["w_l1a_dq"] = np.ascontiguousarray(inp["L1_attn_W_dq"].reshape(ND, 128, 512))
    w["w_l1a_uq"] = np.ascontiguousarray(inp["L1_attn_W_uq"].reshape(4, 128, 512))
    perm = np.concatenate([np.arange(8)[:, None] * 32 + np.arange(16)[None, :],
                           np.arange(8)[:, None] * 32 + 16 + np.arange(16)[None, :]]).reshape(2, 8, 16)
    perm = np.concatenate([perm[0].reshape(-1), perm[1].reshape(-1)])
    w["w_l1a_qrR"] = np.ascontiguousarray(inp["L1_attn_W_qr"][:, perm].reshape(4, 128, 256))
    w["w_l1a_qr"] = np.ascontiguousarray(inp["L1_attn_W_qr"].reshape(4, 128, 256))
    w["w_l1a_kr"] = np.ascontiguousarray(inp["L1_attn_W_kr"].reshape(ND, 128, 32))
    w["w_l1a_wout"] = np.ascontiguousarray(inp["L1_attn_W_out"].reshape(4, 128, 1024))
    w["w_l1a_iwq"] = np.ascontiguousarray(inp["L1_attn_idx_wq"].reshape(ND, 128, 256))
    w["w_l1a_iwk"] = np.ascontiguousarray(inp["L1_attn_idx_wk"].reshape(ND, 128, 64))
    w["w_l1a_iw"] = inp["L1_attn_idx_w"].reshape(1, 4).copy()
    return w


_PROG_CACHE = {}

def _get_prog(blocks):
    key = tuple(blocks)
    if key not in _PROG_CACHE:
        _PROG_CACHE[key] = build_program(blocks)
    return _PROG_CACHE[key]


def kernel(**inputs):
    blocks = ALL_BLOCKS
    k, wdecl, taps = _get_prog(blocks)
    consts = make_consts()
    w = prep_weights(inputs)
    x = inputs["L0_attn_x"]
    in_maps = []
    for core in range(8):
        m = dict(consts)
        m.update(w)
        m["x_in"] = np.ascontiguousarray(x[core * NB:(core + 1) * NB])
        in_maps.append(m)
    res = run_bass_kernel_spmd(k.nc, in_maps, core_ids=list(range(8)))
    out = np.concatenate([r["y_out"] for r in res.results], axis=0)
    return out


# version 02 (final)
# speedup vs baseline: 1.2813x; 1.2813x over previous
import os
import numpy as np
from contextlib import ExitStack

import concourse.bass as bass
import concourse.mybir as mybir
from concourse.bass_utils import run_bass_kernel_spmd
from concourse.bass import IndirectOffsetOnAxis

import ml_dtypes

FP32 = mybir.dt.float32
BF16 = mybir.dt.bfloat16
I16 = mybir.dt.int16
I32 = mybir.dt.int32
U32 = mybir.dt.uint32
AF = mybir.ActivationFunctionType
ALU = mybir.AluOpType
AX = mybir.AxisListType.X

NB = int(os.environ.get("KNB", "2"))
T = 2048
D = 1024
NT = T // 128
ND = D // 128
EPS_LN = 1e-5

ALL_BLOCKS = ("l0a", "l0f", "l1a", "l1f")


class KB:
    def __init__(self):
        import os
        self.nc = bass.Bass("TRN2", target_bir_lowering=False, debug=bool(os.environ.get("KDBG")))
        self.ctx = ExitStack()
        self.sems = {}
        self.dram = {}

    def dram_in(self, name, shape, dt=FP32):
        ap = self.nc.dram_tensor(name, list(shape), dt, kind="ExternalInput").ap()
        self.dram[name] = ap
        return ap

    def dram_out(self, name, shape, dt=FP32):
        ap = self.nc.dram_tensor(name, list(shape), dt, kind="ExternalOutput").ap()
        self.dram[name] = ap
        return ap

    def dram_tmp(self, name, shape, dt=FP32):
        ap = self.nc.dram_tensor(name, list(shape), dt).ap()
        self.dram[name] = ap
        return ap

    def sb(self, name, shape, dt=FP32):
        return self.ctx.enter_context(self.nc.sbuf_tensor(name, list(shape), dt))

    def ps(self, name, shape, dt=FP32):
        return self.ctx.enter_context(self.nc.psum_tensor(name, list(shape), dt))

    def sem(self, name):
        h = self.ctx.enter_context(self.nc.semaphore(name))
        self.sems[name] = [h, 0]
        return h

    def inc(self, instr, name, amt=1):
        h, c = self.sems[name]
        instr.then_inc(h, amt)
        self.sems[name][1] = c + amt
        return self.sems[name][1]

    def wait(self, eng, name, v=None):
        h, c = self.sems[name]
        eng.wait_ge(h, c if v is None else v)

    _cur_scope = None
    _scope_seq = 0

    def scope(self, name):
        if self._cur_scope is not None:
            n0, sid = self._cur_scope
            self.nc.leave_named_scope(n0, sid, False)
            self._cur_scope = None
        if name:
            KB._scope_seq += 1
            name = "%s~%d" % (name, KB._scope_seq)
            sid, _ = self.nc.enter_named_scope(name, False)
            self._cur_scope = (name, sid)



def build_program(blocks=ALL_BLOCKS, dbg_taps=()):
    k = KB()
    nc = k.nc
    taps = {}

    x_in = k.dram_in("x_in", [NB, T, D], FP32)
    y_out = k.dram_out("y_out", [NB, T, D], FP32)

    wdecl = []
    def win(name, shape, dt=FP32):
        wdecl.append((name, shape, dt))
        return k.dram_in(name, shape, dt)

    c_ident = win("c_ident", [128, 128], FP32)
    c_identb = win("c_identb", [128, 128], BF16)
    c_ones_b = win("c_ones_b", [128, 512], BF16)
    c_iota_col = win("c_iota_col", [128, 1], FP32)
    c_eps = win("c_eps", [128, 1], FP32)

    if "l0f" in blocks or True:
        w_sW1 = win("w_sW1", [ND, 128, 1024])
        w_sW2 = win("w_sW2", [ND, 128, 1024])
        w_eW1 = win("w_eW1", [15, ND, 128, 1024])
        w_eW2 = win("w_eW2", [15, ND, 128, 1024])
        w_routerW = win("w_routerW", [ND, 128, 15])
        w_routerB = win("w_routerB", [128, 15])
        w_sb1c = win("w_sb1c", [128, ND])
        w_sb2r = win("w_sb2r", [1, D])
        w_eb1c = win("w_eb1c", [15, 128, ND])
        w_eb2 = win("w_eb2", [15, 1024])

    w_l0a_WinT = win("w_l0a_WinT", [ND, 128, 1024])
    w_l0a_Win = win("w_l0a_Win", [ND, 128, 1024])
    w_l0a_WoutT = win("w_l0a_WoutT", [ND, 128, 1024])
    w_l0a_Wc = win("w_l0a_Wc", [8, 128, 4])
    w_l0a_binc = win("w_l0a_binc", [128, ND])
    w_l0a_binsl = win("w_l0a_binsl", [ND, 128, 1])
    w_l0a_bout = win("w_l0a_bout", [1, D])
    w_l0a_A_rep = win("w_l0a_A_rep", [128, 672])
    w_l0a_base_cols = win("w_l0a_base_cols", [128, 168])
    w_l0a_alphas = win("w_l0a_alphas", [1, 8])
    w_l0a_Vcols = win("w_l0a_Vcols", [128, 32])
    c_diag = win("c_diag", [128, 41 * 128], BF16)
    o_dram = k.dram_tmp("o_dram", [8, 128, T], BF16)
    h1_dram = k.dram_tmp("h1_dram", [32, 128, T], BF16)

    w_l1f_W1T = win("w_l1f_W1T", [ND, 128, 4096])
    w_l1f_W2T = win("w_l1f_W2T", [32, 128, 1024])
    w_l1f_routerWT = win("w_l1f_routerWT", [ND, 128, 512])
    w_l1f_routerBr = win("w_l1f_routerBr", [1, 512])
    w_l1f_rawUT = win("w_l1f_rawUT", [ND, 128, 512])
    w_l1f_rawU_rows = win("w_l1f_rawU_rows", [4, 128, 1024])
    w_l1f_rawV_rows = win("w_l1f_rawV_rows", [4, 128, 1024])
    w_l1f_lngc = win("w_l1f_lngc", [128, ND])
    w_l1f_lnbc = win("w_l1f_lnbc", [128, ND])
    w_l1f_gammac = win("w_l1f_gammac", [128, ND])

    w_l1a_dkv = win("w_l1a_dkv", [ND, 128, 256])
    w_l1a_uk = win("w_l1a_uk", [2, 128, 512])
    w_l1a_uv = win("w_l1a_uv", [2, 128, 512])
    w_l1a_dq = win("w_l1a_dq", [ND, 128, 512])
    w_l1a_uq = win("w_l1a_uq", [4, 128, 512])
    w_l1a_qrR = win("w_l1a_qrR", [4, 128, 256])
    w_l1a_qr = win("w_l1a_qr", [4, 128, 256])
    w_l1a_kr = win("w_l1a_kr", [ND, 128, 32])
    w_l1a_wout = win("w_l1a_wout", [4, 128, 1024])
    w_l1a_iwq = win("w_l1a_iwq", [ND, 128, 256])
    w_l1a_iwk = win("w_l1a_iwk", [ND, 128, 64])
    w_l1a_iw = win("w_l1a_iw", [1, 4])
    g_csband_c = win("g_csband_c", [128, T])
    g_csband_s = win("g_csband_s", [128, T])
    g_csq2_c = win("g_csq2_c", [128, 128])
    g_csq2_s = win("g_csq2_s", [128, 128])
    g_cswbar = win("g_cswbar", [16, 2 * T])
    g_csrot = win("g_csrot", [128, 2048])
    c_iota_s = win("c_iota_s", [128, T])
    mu_dram = k.dram_tmp("mu_dram", [T, 32], FP32)
    srd_dram = k.dram_tmp("srd_dram", [128 * T + 4096, 8], BF16)
    qb_dram = k.dram_tmp("qb_dram", [8, 128, T], BF16)
    kc_dram = k.dram_tmp("kc_dram", [8, 64, T], BF16)
    qc_dram = k.dram_tmp("qc_dram", [4, 128, T], BF16)
    qh_dram = k.dram_tmp("qh_dram", [NT, 128, 256], BF16)

    for s in ("dq", "dw", "pe", "dve", "act", "pool", "dst",
              "dsc0", "dsc1", "dsb0", "dsb1", "dob", "dh0", "dh1", "di0", "di1"):
        k.sem(s)

    xT = k.sb("xT", [128, ND * T], FP32)
    ylnT = k.sb("ylnT", [128, ND * T], BF16)
    ident = k.sb("ident", [128, 128], FP32)
    identb = k.sb("identb", [128, 128], BF16)
    ones_b = k.sb("ones_b", [128, 512], BF16)
    iota_col = k.sb("iota_col", [128, 1], FP32)
    eps_col = k.sb("eps_col", [128, 1], FP32)

    nat_tmp = k.sb("nat_tmp", [128, D], FP32)
    PSA = k.ps("PSA", [128, 4096], FP32)
    ps_tp = PSA[:, 3584:4096]

    ln = None

    routerW_sb = k.sb("routerW_sb", [128, ND * 16], BF16)
    routerB_sb = k.sb("routerB_sb", [128, 15], FP32)
    sb1c_sb = k.sb("sb1c_sb", [128, ND], FP32)
    sb2r_sb = k.sb("sb2r_sb", [1, D], BF16)
    eb1c_sb = k.sb("eb1c_sb", [128, 15 * ND], FP32)
    eb2_sb = k.sb("eb2_sb", [15, 1024], BF16)

    POOL_BF = k.sb("POOL_BF", [128, 33792], BF16)
    POOL_F = k.sb("POOL_F", [128, 8192], FP32)

    ew_sb = POOL_BF[:, 0:16384]
    s1_bf = POOL_BF[:, 16384:20480]
    he_bf = s1_bf
    cwb_bf = POOL_BF[:, 25088:27136]
    ps_mm = PSA[:, 2560:3584]
    ps_big = PSA[:, 0:2048]
    lg_nat = k.sb("lg_nat", [128, 16], FP32)
    prob = k.sb("prob", [128, 16], FP32)
    r1 = k.sb("r1", [128, 16], FP32)
    rmax = k.sb("rmax", [128, 1], FP32)
    rsum = k.sb("rsum", [128, 1], FP32)
    rrec = k.sb("rrec", [128, 1], FP32)
    m8 = k.sb("m8_l0f", [128, 8], FP32)
    cw_nat = k.sb("cw_nat", [128, NT * 16], FP32)
    cwT_bf = POOL_BF[0:16, 20992:23040]
    cw_row = POOL_BF[0:1, 23040:25088]

    a_WinT = POOL_BF[:, 0:8192]
    a_WoutT = POOL_BF[:, 8192:16384]
    a_xpT_bf = POOL_BF[:, 16384:18432]
    a_xh_nat = POOL_BF[:, 18432:20480]
    a_T = POOL_BF[:, 20480:22016]
    a_diag = POOL_BF[:, 22016:24704]
    a_otc = POOL_BF[:, 24704:25216]
    a_S = POOL_BF[:, 25216:27264]
    a_cbsb = POOL_BF[:, 27264:29312]
    a_tmp = POOL_BF[:, 29312:31360]
    a_crow = POOL_BF[0:1, 31360:33408]
    a_winnat = POOL_BF[:, 31360:32384]
    a_wch = POOL_BF[:, 32384:32416]
    a_binsl = POOL_BF[:, 32416:32424]
    a_wceff = POOL_BF[:, 33408:33664]
    a_cbrow = POOL_BF[0:1, 33664:33696]
    a_obuf = POOL_BF[:, 16384:20480]
    a_dgc = POOL_BF[:, 8192:13440]

    ln0 = dict(
        ps_sx=PSA[0:1, 0:T],
        ps_sq=PSA[0:1, T:2 * T],
        ps_bc_a=PSA[:, 0:T],
        ps_bc_b=PSA[:, T:2 * T],
        row_m=POOL_F[0:1, 2048:4096],
        row_r=POOL_F[0:1, 4096:6144],
        row_t=POOL_F[0:1, 6144:8192],
        t1=POOL_F[:, 0:2048],
    )
    ln0["xbf"] = POOL_BF[:, 0:2048]
    ln0["sq"] = POOL_BF[:, 2048:4096]
    ln0["row_r_bf"] = POOL_BF[0:1, 4096:6144]
    ln0["row_m_bf"] = POOL_BF[0:1, 6144:8192]
    ln = ln0

    def dma(eng, out, in_, sname="dq"):
        ins = eng.dma_start(out=out, in_=in_)
        return k.inc(ins, sname, 16)

    def edge(ins, sname, *engs):
        k.inc(ins, sname)
        for e in engs:
            k.wait(e, sname)

    def V(op, *a, **kw):
        k.wait(nc.vector, "dve")
        ins = getattr(nc.vector, op)(*a, **kw)
        k.inc(ins, "dve")
        return ins

    def A(op, *a, **kw):
        k.wait(nc.scalar, "act")
        ins = getattr(nc.scalar, op)(*a, **kw)
        k.inc(ins, "act")
        return ins

    def barrier():
        for eng in (nc.tensor, nc.vector, nc.scalar, nc.gpsimd, nc.sync):
            for sn in list(k.sems):
                k.wait(eng, sn)

    dma(nc.sync, ident[:], c_ident[:, :])
    dma(nc.sync, identb[:], c_identb[:, :])
    dma(nc.sync, ones_b[:], c_ones_b[:, :])
    dma(nc.sync, iota_col[:], c_iota_col[:, :])
    dma(nc.sync, eps_col[:], c_eps[:, :])
    for c in range(ND):
        dma(nc.gpsimd, routerW_sb[:, c * 16:c * 16 + 15], w_routerW[c], "dw")
    dma(nc.sync, routerB_sb[:], w_routerB[:, :])
    dma(nc.sync, sb1c_sb[:], w_sb1c[:, :])
    dma(nc.gpsimd, sb2r_sb[:], w_sb2r[:, :], "dw")
    for e in range(15):
        dma(nc.sync, eb1c_sb[:, e * ND:(e + 1) * ND], w_eb1c[e], "dq")
    dma(nc.gpsimd, eb2_sb[:], w_eb2[:, :], "dw")
    W_LOADED = (k.sems["dq"][1], k.sems["dw"][1])

    for eng in (nc.vector, nc.scalar, nc.tensor, nc.gpsimd, nc.sync):
        k.wait(eng, "dq", W_LOADED[0])
        k.wait(eng, "dw", W_LOADED[1])


    def transpose_in(b):
        for tt in range(NT):
            k.wait(nc.sync, "pe")
            k.wait(nc.sync, "dq")
            dma(nc.sync, nat_tmp[:], x_in[b, tt * 128:(tt + 1) * 128, :])
            k.wait(nc.tensor, "dq")
            for c in range(ND):
                ins = nc.tensor.transpose(ps_tp[:, (c % 4) * 128:(c % 4) * 128 + 128],
                                          nat_tmp[:, c * 128:(c + 1) * 128], ident[:])
                if c % 4 == 3:
                    edge(ins, "pe", nc.vector)
                    for cc in range(c - 3, c + 1):
                        ins2 = V("tensor_copy", 
                            xT[:, cc * T + tt * 128: cc * T + tt * 128 + 128],
                            ps_tp[:, (cc % 4) * 128:(cc % 4) * 128 + 128])
                    k.wait(nc.tensor, "dve")

    def transpose_out(b):
        for tt in range(NT):
            k.wait(nc.tensor, "dq")
            k.wait(nc.vector, "dq")
            k.wait(nc.sync, "dq")
            for c in range(ND):
                ins = nc.tensor.matmul(ps_tp[:, (c % 4) * 128:(c % 4) * 128 + 128],
                                       xT[:, c * T + tt * 128: c * T + tt * 128 + 128],
                                       ident[:], start=True, stop=True)
                if c % 4 == 3:
                    edge(ins, "pe", nc.vector)
                    for cc in range(c - 3, c + 1):
                        ins2 = V("tensor_copy", 
                            nat_tmp[:, cc * 128:(cc + 1) * 128],
                            ps_tp[:, (cc % 4) * 128:(cc % 4) * 128 + 128])
                    k.wait(nc.tensor, "dve")
            k.wait(nc.sync, "dve")
            dma(nc.sync, y_out[b, tt * 128:(tt + 1) * 128, :], nat_tmp[:])

    def layer_norm():
        for c in range(ND):
            k.wait(nc.vector, "pe")
            ins = V("tensor_copy", ln["xbf"][:], xT[:, c * T:(c + 1) * T])
            k.wait(nc.scalar, "dve"); k.wait(nc.tensor, "dve")
            ins = A("activation", ln["sq"][:], ln["xbf"][:], AF.Square)
            k.wait(nc.tensor, "act")
            for tc in range(T // 512):
                nc.tensor.matmul(ln["ps_sx"][0:1, tc * 512:(tc + 1) * 512],
                                 ones_b[:, 0:1],
                                 ln["xbf"][:, tc * 512:(tc + 1) * 512],
                                 start=(c == 0), stop=(c == ND - 1))
                ins = nc.tensor.matmul(ln["ps_sq"][0:1, tc * 512:(tc + 1) * 512],
                                       ones_b[:, 0:1],
                                       ln["sq"][:, tc * 512:(tc + 1) * 512],
                                       start=(c == 0), stop=(c == ND - 1))
            k.inc(ins, "pe")
        k.wait(nc.vector, "pe")
        V("tensor_scalar", ln["row_m"][0:1, :], ln["ps_sx"][0:1, :], 1.0 / D, None, op0=ALU.mult)
        V("tensor_scalar", ln["row_r"][0:1, :], ln["ps_sq"][0:1, :], 1.0 / D, None, op0=ALU.mult)
        V("tensor_tensor", ln["row_t"][0:1, :], ln["row_m"][0:1, :], ln["row_m"][0:1, :], ALU.mult)
        ins = V("tensor_sub", ln["row_r"][0:1, :], ln["row_r"][0:1, :], ln["row_t"][0:1, :])
        k.wait(nc.scalar, "dve")
        ins = A("activation", ln["row_t"][0:1, :], ln["row_r"][0:1, :], AF.Sqrt, bias=eps_col[0:1, 0:1])
        k.wait(nc.vector, "act")
        V("reciprocal", ln["row_r"][0:1, :], ln["row_t"][0:1, :])
        V("tensor_mul", ln["row_m"][0:1, :], ln["row_m"][0:1, :], ln["row_r"][0:1, :])
        V("tensor_scalar", ln["row_m"][0:1, :], ln["row_m"][0:1, :], -1.0, None, op0=ALU.mult)
        V("tensor_copy", ln["row_r_bf"][0:1, :], ln["row_r"][0:1, :])
        ins = V("tensor_copy", ln["row_m_bf"][0:1, :], ln["row_m"][0:1, :])
        k.wait(nc.tensor, "dve")
        for tc in range(T // 512):
            nc.tensor.matmul(ln["ps_bc_a"][:, tc * 512:(tc + 1) * 512], ones_b[0:1, 0:128],
                             ln["row_r_bf"][0:1, tc * 512:(tc + 1) * 512], start=True, stop=True)
            ins = nc.tensor.matmul(ln["ps_bc_b"][:, tc * 512:(tc + 1) * 512], ones_b[0:1, 0:128],
                                   ln["row_m_bf"][0:1, tc * 512:(tc + 1) * 512], start=True, stop=True)
        edge(ins, "pe", nc.vector)
        for c in range(ND):
            V("tensor_tensor", ln["t1"][:], xT[:, c * T:(c + 1) * T], ln["ps_bc_a"][:], ALU.mult)
            ins = V("tensor_tensor", ylnT[:, c * T:(c + 1) * T], ln["t1"][:], ln["ps_bc_b"][:], ALU.add)
        k.wait(nc.tensor, "dve")

    def l0_ffn():
        for tt in range(NT):
            for c in range(ND):
                ins = nc.tensor.matmul(ps_mm[:, 0:15],
                                       ylnT[:, c * T + tt * 128: c * T + tt * 128 + 128],
                                       routerW_sb[:, c * 16: c * 16 + 15],
                                       start=(c == 0), stop=(c == ND - 1))
            edge(ins, "pe", nc.vector)
            ins = V("tensor_tensor", lg_nat[:, 0:15], ps_mm[:, 0:15], routerB_sb[:, 0:15], ALU.add)
            k.wait(nc.tensor, "dve")
            ins = V("reduce_max", rmax[:, 0:1], lg_nat[:, 0:15], axis=AX, negate=True)
            k.wait(nc.scalar, "dve")
            ins = A("activation", prob[:, 0:15], lg_nat[:, 0:15], AF.Exp,
                                       bias=rmax[:, 0:1], scale=1.0, accum_out=rsum[:, 0:1])
            k.wait(nc.vector, "act")
            V("reciprocal", rrec[:, 0:1], rsum[:, 0:1])
            ins = V("tensor_scalar", prob[:, 0:15], prob[:, 0:15], rrec[:, 0:1], None, op0=ALU.mult)
            ins = V("max", out=m8[:, 0:8], in_=prob[:, 0:15])
            
            V("tensor_scalar", r1[:, 0:15], prob[:, 0:15], m8[:, 2:3], None, op0=ALU.is_ge)
            ins = V("tensor_tensor", cw_nat[:, tt * 16: tt * 16 + 15], r1[:, 0:15], prob[:, 0:15], ALU.mult)
            if tt == NT - 1:
                k.wait(nc.tensor, "dve")
        for tt in range(NT):
            ins = nc.tensor.transpose(ps_tp[0:15, 0:128],
                                      cw_nat[:, tt * 16: tt * 16 + 15], ident[:])
            edge(ins, "pe", nc.vector)
            ins = V("tensor_copy", cwT_bf[0:15, tt * 128:(tt + 1) * 128], ps_tp[0:15, 0:128])
            k.wait(nc.tensor, "dve")

        k.scope("F0_shared")
        k.wait(nc.gpsimd, "pe")
        for c in range(ND):
            dma(nc.gpsimd, ew_sb[:, c * 1024:(c + 1) * 1024], w_sW1[c], "dw")
            dma(nc.gpsimd, ew_sb[:, (ND + c) * 1024:(ND + c + 1) * 1024], w_sW2[c], "dw")
        k.wait(nc.tensor, "dw")
        for tc in range(T // 512):
            t0 = tc * 512
            for ht in range(ND):
                pp = ps_mm[:, (ht % 2) * 512:(ht % 2) * 512 + 512]
                k.wait(nc.tensor, "act")
                for c in range(ND):
                    ins = nc.tensor.matmul(pp,
                                           ew_sb[:, c * 1024 + ht * 128: c * 1024 + ht * 128 + 128],
                                           ylnT[:, c * T + t0: c * T + t0 + 512],
                                           start=(c == 0), stop=(c == ND - 1))
                edge(ins, "pe", nc.scalar)
                ins = A("activation", s1_bf[:, ht * 512:(ht + 1) * 512], pp, AF.Relu,
                                           bias=sb1c_sb[:, ht:ht + 1], scale=1.0)
            k.wait(nc.tensor, "act")
            for half in range(2):
                for dt2 in range(4):
                    dt = half * 4 + dt2
                    pb = ps_big[:, dt2 * 512:(dt2 + 1) * 512]
                    mms = []
                    for hc in range(ND):
                        mms.append((ew_sb[:, (ND + hc) * 1024 + dt * 128: (ND + hc) * 1024 + dt * 128 + 128],
                                    s1_bf[:, hc * 512:(hc + 1) * 512]))
                    mms.append((sb2r_sb[0:1, dt * 128:(dt + 1) * 128], ones_b[0:1, 0:512]))
                    mms.append((eb2_sb[0:15, dt * 128:(dt + 1) * 128], cwT_bf[0:15, t0:t0 + 512]))
                    for i, (lh, rh) in enumerate(mms):
                        ins = nc.tensor.matmul(pb, lh, rh, start=(i == 0), stop=(i == len(mms) - 1))
                    k.inc(ins, "pe")
                k.wait(nc.vector, "pe")
                for dt2 in range(4):
                    dt = half * 4 + dt2
                    pb = ps_big[:, dt2 * 512:(dt2 + 1) * 512]
                    sl = slice(dt * T + t0, dt * T + t0 + 512)
                    ins = V("tensor_tensor", xT[:, sl], xT[:, sl], pb, ALU.add)
                k.wait(nc.tensor, "dve")

        last_act_slot = {}
        for e in range(15):
            k.scope("F0_e%02d" % e)
            k.wait(nc.gpsimd, "pe")
            k.wait(nc.gpsimd, "dve")
            dma(nc.gpsimd, cw_row[0:1, :], cwT_bf[e:e + 1, :], "dw")
            for c in range(ND):
                dma(nc.gpsimd, ew_sb[:, c * 1024:(c + 1) * 1024], w_eW1[e, c], "dw")
            dw_w1 = k.sems["dw"][1]
            k.wait(nc.gpsimd, "dw", dw_w1)
            for c in range(ND):
                dma(nc.gpsimd, ew_sb[:, (ND + c) * 1024:(ND + c + 1) * 1024], w_eW2[e, c], "dw")
            dw_w2 = k.sems["dw"][1]
            k.wait(nc.tensor, "dw", dw_w1)
            k.wait(nc.tensor, "act"); k.wait(nc.tensor, "dve")
            cwb_acts = []
            for q in range(4):
                pp = ps_mm[:, (q % 2) * 512:(q % 2) * 512 + 512]
                if q >= 2:
                    k.wait(nc.tensor, "act", cwb_acts[q - 2])
                ins = nc.tensor.matmul(pp, ones_b[0:1, 0:128], cw_row[0:1, q * 512:(q + 1) * 512],
                                       start=True, stop=True)
                edge(ins, "pe", nc.scalar)
                A("activation", cwb_bf[:, q * 512:(q + 1) * 512], pp, AF.Copy)
                cwb_acts.append(k.sems["act"][1])
            last_act_slot = {0: cwb_acts[2], 1: cwb_acts[3]}
            dve56 = None
            for tc in range(T // 512):
                t0 = tc * 512
                for ht in range(ND):
                    sl = (ht % 2)
                    pp = ps_mm[:, sl * 512:sl * 512 + 512]
                    k.wait(nc.tensor, "act", last_act_slot[sl])
                    if tc > 0 and ht == 0 and dve56 is not None:
                        k.wait(nc.tensor, "dve", dve56)
                    for c in range(ND):
                        ins = nc.tensor.matmul(pp,
                                               ew_sb[:, c * 1024 + ht * 128: c * 1024 + ht * 128 + 128],
                                               ylnT[:, c * T + t0: c * T + t0 + 512],
                                               start=(c == 0), stop=(c == ND - 1))
                    edge(ins, "pe", nc.scalar)
                    A("activation", s1_bf[:, ht * 512:(ht + 1) * 512], pp, AF.Relu,
                      bias=eb1c_sb[:, e * ND + ht: e * ND + ht + 1], scale=1.0)
                    last_act_slot[sl] = k.sems["act"][1]
                    k.wait(nc.vector, "act", last_act_slot[sl])
                    V("tensor_tensor", s1_bf[:, ht * 512:(ht + 1) * 512],
                      s1_bf[:, ht * 512:(ht + 1) * 512], cwb_bf[:, t0:t0 + 512], ALU.mult)
                if e == 0 and tc == 0 and b == 0:
                    dump("s1e0", s1_bf[:, 0:4096], [128, 4096], BF16)
                    dump("cwbe0", cwb_bf[:, 0:2048], [128, 2048], BF16)
                BMAP = [5, 6, 0, 1, 2, 3, 4, 7]
                k.wait(nc.tensor, "dve")
                k.wait(nc.tensor, "act")
                if tc == 0:
                    k.wait(nc.tensor, "dw", dw_w2)
                for di in range(8):
                    dt = di
                    bk = BMAP[di]
                    pb = PSA[:, bk * 512:(bk + 1) * 512]
                    for hc in range(ND):
                        ins = nc.tensor.matmul(pb,
                                               ew_sb[:, (ND + hc) * 1024 + dt * 128:(ND + hc) * 1024 + dt * 128 + 128],
                                               s1_bf[:, hc * 512:(hc + 1) * 512],
                                               start=(hc == 0), stop=(hc == ND - 1))
                    k.inc(ins, "pe")
                    if di == 1:
                        pe56 = k.sems["pe"][1]
                k.wait(nc.vector, "pe", pe56)
                for di in range(8):
                    if di == 2:
                        dve56 = k.sems["dve"][1]
                        k.wait(nc.vector, "pe")
                    dt = di
                    bk = BMAP[di]
                    pb = PSA[:, bk * 512:(bk + 1) * 512]
                    sl = slice(dt * T + t0, dt * T + t0 + 512)
                    V("tensor_tensor", xT[:, sl], xT[:, sl], pb, ALU.add)
                if dve56 is None:
                    dve56 = k.sems["dve"][1]
        k.wait(nc.tensor, "dve")
        return

    def dump(name, sb_ap, shape, dt=FP32):
        if name not in dbg_taps:
            return
        d = k.dram_out("tap_" + name, shape, dt)
        taps[name] = d
        for eng in (nc.sync,):
            k.wait(eng, "dve"); k.wait(eng, "act"); k.wait(eng, "pe"); k.wait(eng, "dq")
        dma(nc.sync, d[:, :] if len(shape) == 2 else d[:], sb_ap, "dq")
        k.wait(nc.vector, "dq"); k.wait(nc.tensor, "dq"); k.wait(nc.scalar, "dq")


    KS = [3, 3, 7, 7, 11, 11, 21, 21]
    OFFS = list(range(-10, 11)) + list(range(-127, -117)) + list(range(118, 128))

    def l0_attn():
        barrier()
        for c in range(ND):
            dma(nc.gpsimd, a_WinT[:, c * 1024:(c + 1) * 1024], w_l0a_WinT[c], "dw")
        dma(nc.sync, a_dgc[:], c_diag[:, :])
        dma(nc.sync, POOL_F[:, 2048:2720], w_l0a_A_rep[:, :])
        dma(nc.sync, POOL_F[:, 2720:2888], w_l0a_base_cols[:, :])
        dma(nc.sync, POOL_F[0:1, 2888:2896], w_l0a_alphas[:, :])
        dma(nc.sync, POOL_F[:, 2896:2928], w_l0a_Vcols[:, :])
        dma(nc.gpsimd, a_binsl[:, 0:8], w_l0a_binc[:, :], "dw")
        dma(nc.sync, POOL_F[:, 2928:2936], w_l0a_binc[:, :])
        A_rep = POOL_F[:, 2048:2720]
        base_cols = POOL_F[:, 2720:2888]
        alp_row = POOL_F[0:1, 2888:2896]
        Vcols = POOL_F[:, 2896:2928]
        binc_f = POOL_F[:, 2928:2936]
        c_nat = POOL_F[:, 2936:3448]
        alpb = POOL_F[:, 3448:3464]
        alp_bf = POOL_BF[0:1, 33696:33712]
        cT = POOL_F[0:32, 3584:5632]
        barrier()

        ps_wc = PSA[:, 2560:2816]
        ps_cb = PSA[0:1, 2816:2848]
        for h in range(8):
            k.wait(nc.gpsimd, "pe")
            dma(nc.gpsimd, a_winnat[:, 0:1024], w_l0a_Win[h], "dw")
            dma(nc.gpsimd, a_wch[:, h * 4:(h + 1) * 4], w_l0a_Wc[h], "dw")
            k.wait(nc.tensor, "dw")
            for dc in range(ND):
                ins = nc.tensor.matmul(ps_wc[:, dc * 32 + h * 4: dc * 32 + h * 4 + 4],
                                       a_winnat[:, dc * 128:(dc + 1) * 128],
                                       a_wch[:, h * 4:(h + 1) * 4], start=True, stop=True)
            ins = nc.tensor.matmul(ps_cb[0:1, h * 4:(h + 1) * 4],
                                   a_binsl[:, h:h + 1],
                                   a_wch[:, h * 4:(h + 1) * 4], start=True, stop=True)
            k.inc(ins, "pe")
        k.wait(nc.vector, "pe")
        V("tensor_copy", a_wceff[:, 0:256], ps_wc[:, 0:256])
        ins = V("tensor_copy", a_cbrow[0:1, 0:32], ps_cb[0:1, 0:32])
        k.wait(nc.tensor, "dve")

        ins = A("activation", alpb[0:1, 0:8], alp_row[0:1, 0:8], AF.Sigmoid)
        k.wait(nc.vector, "act")
        V("tensor_scalar", alpb[0:1, 8:16], alpb[0:1, 0:8], -1.0, 1.0, op0=ALU.mult, op1=ALU.add)
        ins = V("tensor_copy", alp_bf[0:1, 0:16], alpb[0:1, 0:16])
        k.wait(nc.tensor, "dve")
        ps_al = PSA[:, 2848:2864]
        ins = nc.tensor.matmul(ps_al[:, 0:16], ones_b[0:1, 0:128], alp_bf[0:1, 0:16], start=True, stop=True)
        edge(ins, "pe", nc.vector)
        ins = V("tensor_copy", alpb[:, 0:16], ps_al[:, 0:16])
        k.wait(nc.tensor, "dve")
        v_all = POOL_F[:, 5632:5800]
        for h in range(8):
            ins = V("tensor_scalar", v_all[:, h * 21:(h + 1) * 21],
                    base_cols[:, h * 21:(h + 1) * 21], alpb[:, 8 + h:9 + h], None, op0=ALU.mult)
        k.wait(nc.tensor, "dve")

        ps_c = PSA[:, 3072:3104]
        for tt in range(NT):
            for c in range(ND):
                nc.tensor.matmul(ps_c[:, 0:32],
                                 ylnT[:, c * T + tt * 128: c * T + tt * 128 + 128],
                                 a_wceff[:, c * 32:(c + 1) * 32],
                                 start=(c == 0), stop=False)
            ins = nc.tensor.matmul(ps_c[:, 0:32], ones_b[0:1, 0:128], a_cbrow[0:1, 0:32],
                                   start=False, stop=True)
            edge(ins, "pe", nc.vector)
            ins = V("tensor_copy", c_nat[:, tt * 32:(tt + 1) * 32], ps_c[:, 0:32])
            k.wait(nc.tensor, "dve")
        al_ap = bass.AP(alpb.tensor, alpb.offset, [[alpb.ap[0][0], 128], [0, NT], [1, 8], [0, 4]])
        ins = V("tensor_tensor", c_nat[:, 0:512], c_nat[:, 0:512], al_ap, ALU.mult)
        k.wait(nc.tensor, "dve")
        for tt in range(NT):
            ins = nc.tensor.transpose(ps_tp[0:32, 0:128], c_nat[:, tt * 32:(tt + 1) * 32], ident[:])
            edge(ins, "pe", nc.vector)
            ins = V("tensor_copy", cT[0:32, tt * 128:(tt + 1) * 128], ps_tp[0:32, 0:128])
            k.wait(nc.tensor, "dve")

        for h in range(8):
            kh = KS[h]; pad = kh // 2
            for c in range(ND):
                for tq in range(4):
                    ins = nc.tensor.matmul(PSA[:, tq * 512:(tq + 1) * 512],
                                           a_WinT[:, c * 1024 + h * 128: c * 1024 + h * 128 + 128],
                                           ylnT[:, c * T + tq * 512: c * T + tq * 512 + 512],
                                           start=(c == 0), stop=(c == ND - 1))
            edge(ins, "pe", nc.scalar, nc.vector)
            xpT_f = POOL_F[:, 0:2048]
            for tq in range(4):
                V("tensor_scalar", xpT_f[:, tq * 512:(tq + 1) * 512], PSA[:, tq * 512:(tq + 1) * 512],
                  binc_f[:, h:h + 1], None, op0=ALU.add)
            ins = V("tensor_copy", a_xpT_bf[:, 0:2048], xpT_f[:, 0:2048])
            k.wait(nc.tensor, "dve")
            for i in range(NT):
                ins = nc.tensor.transpose(ps_tp[:, (i % 4) * 128:(i % 4) * 128 + 128],
                                          xpT_f[:, i * 128:(i + 1) * 128], ident[:])
                if i % 4 == 3:
                    edge(ins, "pe", nc.vector)
                    for ii in range(i - 3, i + 1):
                        ins2 = V("tensor_copy", a_xh_nat[:, ii * 128:(ii + 1) * 128],
                                 ps_tp[:, (ii % 4) * 128:(ii % 4) * 128 + 128])
                    k.wait(nc.tensor, "dve")
            for r in range(4):
                for kind in range(3):
                    sl = a_T[:, (r * 3 + kind) * 128:(r * 3 + kind + 1) * 128]
                    taps = []
                    for kk in range(kh):
                        o = pad - kk + (0 if kind == 1 else (-128 if kind == 0 else 128))
                        if -127 <= o <= 127 and o in OFFS:
                            taps.append((kk, OFFS.index(o)))
                    first = True
                    for kk, oi in taps:
                        acol = A_rep[:, (h * 4 + r) * 21 + kk: (h * 4 + r) * 21 + kk + 1]
                        dg = a_dgc[:, oi * 128:(oi + 1) * 128]
                        if first:
                            V("tensor_scalar", sl, dg, acol, None, op0=ALU.mult)
                            first = False
                        else:
                            V("scalar_tensor_tensor", sl, dg, acol, sl, op0=ALU.mult, op1=ALU.add)
                    if first:
                        V("memset", sl, 0.0)
            zi = OFFS.index(0)
            for kk in range(kh):
                ins = V("tensor_scalar", a_diag[:, kk * 128:(kk + 1) * 128],
                        a_dgc[:, zi * 128:(zi + 1) * 128],
                        v_all[:, h * 21 + kk: h * 21 + kk + 1], None, op0=ALU.mult)
            k.wait(nc.tensor, "dve")

            for tq in range(4):
                t0 = tq * 512
                ps_o = PSA[:, 2048:2560]
                ps_S = PSA[:, 2560:3072]
                ps_cbb = PSA[:, 3072:3584]
                order = [pad] + [kk for kk in range(kh) if kk != pad]
                for idx_k, kk in enumerate(order):
                    src0 = t0 + kk - pad
                    lo = max(0, -src0); hi = min(512, 2048 - src0)
                    ins = nc.tensor.matmul(ps_o[:, lo:hi],
                                           a_diag[:, kk * 128:(kk + 1) * 128],
                                           a_xpT_bf[:, src0 + lo: src0 + hi],
                                           start=(idx_k == 0), stop=(idx_k == len(order) - 1))
                k.inc(ins, "pe")
                for nt2 in range(4):
                    i = tq * 4 + nt2
                    for r in range(4):
                        mms = []
                        if i > 0:
                            mms.append((a_xh_nat[:, (i - 1) * 128: i * 128], a_T[:, (r * 3 + 0) * 128:(r * 3 + 1) * 128]))
                        mms.append((a_xh_nat[:, i * 128:(i + 1) * 128], a_T[:, (r * 3 + 1) * 128:(r * 3 + 2) * 128]))
                        if i < NT - 1:
                            mms.append((a_xh_nat[:, (i + 1) * 128:(i + 2) * 128], a_T[:, (r * 3 + 2) * 128:(r * 3 + 3) * 128]))
                        for mi, (lh, rh) in enumerate(mms):
                            ins = nc.tensor.matmul(ps_S[:, r * 128:(r + 1) * 128], lh, rh,
                                                   start=(mi == 0), stop=(mi == len(mms) - 1))
                    edge(ins, "pe", nc.vector)
                    ins = V("tensor_copy", a_S[:, nt2 * 512:(nt2 + 1) * 512], ps_S[:, 0:512])
                    k.wait(nc.tensor, "dve")
                for r in range(4):
                    k.wait(nc.gpsimd, "pe"); k.wait(nc.gpsimd, "dve"); k.wait(nc.gpsimd, "act")
                    dma(nc.gpsimd, a_crow[0:1, 0:2048], cT[h * 4 + r: h * 4 + r + 1, 0:2048], "dw")
                    k.wait(nc.tensor, "dw")
                    ins = nc.tensor.matmul(ps_cbb[:, 0:512], ones_b[0:1, 0:128],
                                           a_crow[0:1, t0:t0 + 512], start=True, stop=True)
                    edge(ins, "pe", nc.scalar)
                    ins = A("activation", a_cbsb[:, r * 512:(r + 1) * 512], ps_cbb[:, 0:512], AF.Copy)
                k.wait(nc.vector, "act")
                cb_ap = bass.AP(a_cbsb.tensor, a_cbsb.offset,
                                [[a_cbsb.ap[0][0], 128], [128, 4], [512, 4], [1, 128]])
                ins = V("tensor_tensor", a_tmp[:, 0:2048], a_S[:, 0:2048], cb_ap, ALU.mult)
                k.wait(nc.tensor, "dve")
                for nt2 in range(4):
                    for r in range(4):
                        ins = V("scalar_tensor_tensor", ps_o[:, nt2 * 128:(nt2 + 1) * 128],
                                a_tmp[:, nt2 * 512 + r * 128: nt2 * 512 + r * 128 + 128],
                                Vcols[:, h * 4 + r: h * 4 + r + 1],
                                ps_o[:, nt2 * 128:(nt2 + 1) * 128], op0=ALU.mult, op1=ALU.add)
                k.wait(nc.scalar, "dve")
                k.wait(nc.scalar, "dq")
                ins = A("activation", a_otc[:, 0:512], ps_o[:, 0:512], AF.Copy)
                k.wait(nc.sync, "act"); k.wait(nc.tensor, "act")
                k.wait(nc.sync, "dq")
                dma(nc.sync, o_dram[h][:, t0:t0 + 512], a_otc[:, 0:512])
        barrier()
        for c in range(ND):
            dma(nc.gpsimd, a_WoutT[:, c * 1024:(c + 1) * 1024], w_l0a_WoutT[c], "dw")
        dma(nc.gpsimd, POOL_BF[0:1, 31360:32384], w_l0a_bout[:, :], "dw")
        bout_bf = POOL_BF[0:1, 31360:32384]
        barrier()
        for tq in range(4):
            t0 = tq * 512
            k.wait(nc.sync, "pe")
            for hc in range(ND):
                dma(nc.sync, a_obuf[:, hc * 512:(hc + 1) * 512], o_dram[hc][:, t0:t0 + 512])
            k.wait(nc.tensor, "dq")
            for half in range(2):
                for dt2 in range(4):
                    dt = half * 4 + dt2
                    pb = ps_big[:, dt2 * 512:(dt2 + 1) * 512]
                    for hc in range(ND):
                        nc.tensor.matmul(pb,
                                         a_WoutT[:, hc * 1024 + dt * 128: hc * 1024 + dt * 128 + 128],
                                         a_obuf[:, hc * 512:(hc + 1) * 512],
                                         start=(hc == 0), stop=False)
                    ins = nc.tensor.matmul(pb, bout_bf[0:1, dt * 128:(dt + 1) * 128],
                                           ones_b[0:1, 0:512], start=False, stop=True)
                    k.inc(ins, "pe")
                k.wait(nc.vector, "pe")
                for dt2 in range(4):
                    dt = half * 4 + dt2
                    pb = ps_big[:, dt2 * 512:(dt2 + 1) * 512]
                    sl = slice(dt * T + t0, dt * T + t0 + 512)
                    ins = V("tensor_tensor", xT[:, sl], xT[:, sl], pb, ALU.add)
                k.wait(nc.tensor, "dve")
        barrier()


    def l1_ffn():
        barrier()
        f_W1c = POOL_BF[:, 0:1024]
        f_W2c = POOL_BF[:, 1024:2048]
        f_h1slot = POOL_BF[:, 2048:2560]
        f_h1in = POOL_BF[:, 2560:3584]
        f_wT = POOL_BF[:, 3584:11776]
        f_yl2 = POOL_BF[:, 11776:12800]
        f_rBrow = POOL_BF[0:1, 24704:25216]
        f_rUc = POOL_BF[:, 12416:16512]
        f_rWc = POOL_BF[:, 16512:20608]
        f_rVc = POOL_BF[:, 20608:24704]
        f_urows = POOL_F[:, 4096:5120]
        g_alpha = POOL_F[:, 0:512]
        g_tmp = POOL_F[:, 512:1024]
        g_w = POOL_F[:, 1024:1536]
        g_m8 = POOL_F[:, 1536:1544]
        g_tau = POOL_F[:, 1544:1545]
        g_S = POOL_F[:, 1545:1546]
        g_zf = POOL_F[:, 1546:1547]
        g_nrm = POOL_F[:, 1552:1560]
        g_inv = POOL_F[:, 1560:2072]
        g_lngc = POOL_F[:, 2072:2080]
        g_lnbc = POOL_F[:, 2080:2088]
        g_gmc = POOL_F[:, 2088:2096]
        inv_row = POOL_F[0:4, 2112:2240]
        inv_row1 = POOL_F[0:1, 2240:2752]
        inv_rowb = POOL_BF[0:1, 26240:26752]

        for c in range(ND):
            dma(nc.gpsimd, f_rUc[:, c * 512:(c + 1) * 512], w_l1f_rawUT[c], "dw")
            dma(nc.gpsimd, f_rWc[:, c * 512:(c + 1) * 512], w_l1f_routerWT[c], "dw")
        for ec in range(4):
            dma(nc.gpsimd, f_rVc[:, ec * 1024:(ec + 1) * 1024], w_l1f_rawV_rows[ec], "dw")
        dma(nc.gpsimd, f_rBrow[0:1, :], w_l1f_routerBr[:, :], "dw")
        dma(nc.sync, g_lngc[:, 0:8], w_l1f_lngc[:, :])
        dma(nc.sync, g_lnbc[:, 0:8], w_l1f_lnbc[:, :])
        dma(nc.sync, g_gmc[:, 0:8], w_l1f_gammac[:, :])
        barrier()

        for which in range(2):
            src = w_l1f_rawU_rows if which == 0 else w_l1f_rawV_rows
            for ec in range(4):
                k.wait(nc.sync, "dve"); k.wait(nc.sync, "act")
                dma(nc.sync, f_urows[:, 0:1024], src[ec])
                k.wait(nc.scalar, "dq")
                ins = A("activation", POOL_F[:, 5120:6144], f_urows[:, 0:1024], AF.Square)
                k.wait(nc.vector, "act")
                ins = V("reduce_sum", g_nrm[:, which * 4 + ec: which * 4 + ec + 1],
                        POOL_F[:, 5120:6144], axis=AX)
        k.wait(nc.scalar, "dve")
        ins = A("activation", g_nrm[:, 0:8], g_nrm[:, 0:8], AF.Sqrt)
        k.wait(nc.vector, "act")
        V("tensor_scalar", g_nrm[:, 0:8], g_nrm[:, 0:8], 1e-6, None, op0=ALU.max)
        V("tensor_tensor", g_nrm[:, 0:4], g_nrm[:, 0:4], g_nrm[:, 4:8], ALU.mult)
        ins = V("reciprocal", g_nrm[:, 0:4], g_nrm[:, 0:4])
        k.wait(nc.tensor, "dve")
        ins = nc.tensor.transpose(ps_tp[0:4, 0:128], g_nrm[:, 0:4], ident[:])
        edge(ins, "pe", nc.vector)
        ins = V("tensor_copy", inv_row[0:4, 0:128], ps_tp[0:4, 0:128])
        k.wait(nc.sync, "dve")
        for ec in range(4):
            dma(nc.sync, inv_row1[0:1, ec * 128:(ec + 1) * 128], inv_row[ec:ec + 1, 0:128])
        k.wait(nc.vector, "dq")
        ins = V("tensor_copy", inv_rowb[0:1, 0:512], inv_row1[0:1, 0:512])
        k.wait(nc.tensor, "dve")
        ins = nc.tensor.matmul(PSA[:, 3072:3584], ones_b[0:1, 0:128], inv_rowb[0:1, 0:512], start=True, stop=True)
        edge(ins, "pe", nc.vector)
        ins = V("tensor_copy", g_inv[:, 0:512], PSA[:, 3072:3584])
        k.wait(nc.tensor, "dve")

        for c in range(ND):
            k.wait(nc.scalar, "pe")
            ins = A("activation", ln["sq"], ylnT[:, c * T:(c + 1) * T], AF.Square)
            k.wait(nc.tensor, "act")
            for tc in range(T // 512):
                nc.tensor.matmul(ln["ps_sx"][0:1, tc * 512:(tc + 1) * 512], ones_b[:, 0:1],
                                 ylnT[:, c * T + tc * 512: c * T + tc * 512 + 512],
                                 start=(c == 0), stop=(c == ND - 1))
                ins = nc.tensor.matmul(ln["ps_sq"][0:1, tc * 512:(tc + 1) * 512], ones_b[:, 0:1],
                                       ln["sq"][:, tc * 512:(tc + 1) * 512],
                                       start=(c == 0), stop=(c == ND - 1))
            k.inc(ins, "pe")
        k.wait(nc.vector, "pe")
        V("tensor_scalar", ln["row_m"][0:1, :], ln["ps_sx"][0:1, :], 1.0 / D, None, op0=ALU.mult)
        V("tensor_scalar", ln["row_r"][0:1, :], ln["ps_sq"][0:1, :], 1.0 / D, None, op0=ALU.mult)
        V("tensor_tensor", ln["row_t"][0:1, :], ln["row_m"][0:1, :], ln["row_m"][0:1, :], ALU.mult)
        ins = V("tensor_sub", ln["row_r"][0:1, :], ln["row_r"][0:1, :], ln["row_t"][0:1, :])
        k.wait(nc.scalar, "dve")
        ins = A("activation", ln["row_t"][0:1, :], ln["row_r"][0:1, :], AF.Sqrt, bias=eps_col[0:1, 0:1])
        k.wait(nc.vector, "act")
        V("reciprocal", ln["row_r"][0:1, :], ln["row_t"][0:1, :])
        V("tensor_scalar", ln["row_m"][0:1, :], ln["row_m"][0:1, :], -1.0, None, op0=ALU.mult)
        V("tensor_copy", ln["row_r_bf"][0:1, :], ln["row_r"][0:1, :])
        ins = V("tensor_copy", ln["row_m_bf"][0:1, :], ln["row_m"][0:1, :])
        k.wait(nc.tensor, "dve")

        ps_r = PSA[:, 2048:2560]
        ps_h = PSA[:, 2560:3072]
        ps_bc2 = PSA[:, 3072:3328]
        for tt in range(NT):
            t0 = tt * 128
            nc.tensor.matmul(ps_bc2[:, 0:128], ones_b[0:1, 0:128],
                             ln["row_m_bf"][0:1, t0:t0 + 128], start=True, stop=True)
            ins = nc.tensor.matmul(ps_bc2[:, 128:256], ones_b[0:1, 0:128],
                                   ln["row_r_bf"][0:1, t0:t0 + 128], start=True, stop=True)
            edge(ins, "pe", nc.vector)
            for c in range(ND):
                V("tensor_tensor", g_tmp[:, 0:128], ylnT[:, c * T + t0: c * T + t0 + 128],
                  ps_bc2[:, 0:128], ALU.add)
                V("tensor_tensor", g_tmp[:, 128:256], g_tmp[:, 0:128],
                  ps_bc2[:, 128:256], ALU.mult)
                ins = V("tensor_scalar", f_yl2[:, c * 128:(c + 1) * 128], g_tmp[:, 128:256],
                        g_lngc[:, c:c + 1], g_lnbc[:, c:c + 1], op0=ALU.mult, op1=ALU.add)
            k.wait(nc.tensor, "dve")
            for c in range(ND):
                nc.tensor.matmul(ps_r[:, 0:512], f_yl2[:, c * 128:(c + 1) * 128],
                                 f_rWc[:, c * 512:(c + 1) * 512],
                                 start=(c == 0), stop=False)
            ins = nc.tensor.matmul(ps_r[:, 0:512], ones_b[0:1, 0:128], f_rBrow[0:1, 0:512],
                                   start=False, stop=True)
            edge(ins, "pe", nc.vector)
            for c in range(ND):
                ins = nc.tensor.matmul(ps_h[:, 0:512],
                                       ylnT[:, c * T + t0: c * T + t0 + 128],
                                       f_rUc[:, c * 512:(c + 1) * 512],
                                       start=(c == 0), stop=(c == ND - 1))
            edge(ins, "pe", nc.vector)
            V("tensor_scalar", g_alpha[:, 0:512], ps_r[:, 0:512], 10.0, -10.0, op0=ALU.min, op1=ALU.max)
            ins = V("tensor_scalar", g_tmp[:, 0:512], g_alpha[:, 0:512], 0.0, None, op0=ALU.max)
            k.wait(nc.scalar, "dve")
            ins = A("activation", g_w[:, 0:512], g_alpha[:, 0:512], AF.Abs)
            ins = A("activation", g_w[:, 0:512], g_w[:, 0:512], AF.Exp, scale=-1.0)
            ins = A("activation", g_w[:, 0:512], g_w[:, 0:512], AF.Ln, bias=1.0)
            k.wait(nc.vector, "act")
            ins = V("tensor_tensor", g_alpha[:, 0:512], g_tmp[:, 0:512], g_w[:, 0:512], ALU.add)
            ins = V("max", out=g_m8[:, 0:8], in_=g_alpha[:, 0:512])
            V("reduce_sum", g_S[:, 0:1], g_m8[:, 0:8], axis=AX)
            k.wait(nc.scalar, "dve")
            ins = A("activation", g_zf[:, 0:1], g_S[:, 0:1], AF.Tanh)
            k.wait(nc.vector, "act")
            V("tensor_scalar", g_tmp[:, 0:1], g_S[:, 0:1], 1e-6, None, op0=ALU.add)
            V("reciprocal", g_tmp[:, 0:1], g_tmp[:, 0:1])
            V("tensor_mul", g_zf[:, 0:1], g_zf[:, 0:1], g_tmp[:, 0:1])
            V("tensor_scalar", g_tmp[:, 0:512], g_alpha[:, 0:512], g_m8[:, 7:8], None, op0=ALU.is_ge)
            V("tensor_tensor", g_w[:, 0:512], g_tmp[:, 0:512], g_alpha[:, 0:512], ALU.mult)
            V("tensor_scalar", g_w[:, 0:512], g_w[:, 0:512], g_zf[:, 0:1], None, op0=ALU.mult)
            V("tensor_tensor", g_w[:, 0:512], g_w[:, 0:512], ps_h[:, 0:512], ALU.mult)
            V("tensor_tensor", g_w[:, 0:512], g_w[:, 0:512], g_inv[:, 0:512], ALU.mult)
            k.wait(nc.tensor, "dve")
            for ec in range(4):
                ins = nc.tensor.transpose(ps_tp[:, (ec % 4) * 128:(ec % 4) * 128 + 128],
                                          g_w[:, ec * 128:(ec + 1) * 128], ident[:])
                edge(ins, "pe", nc.vector)
                ins = V("tensor_copy", f_wT[:, ec * T + t0: ec * T + t0 + 128],
                        ps_tp[:, (ec % 4) * 128:(ec % 4) * 128 + 128])
                k.wait(nc.tensor, "dve")
        barrier()

        for dt in range(ND):
            for tq in range(4):
                pb = PSA[:, 2048:2560]
                for ec in range(4):
                    ins = nc.tensor.matmul(pb,
                                           f_rVc[:, ec * 1024 + dt * 128: ec * 1024 + dt * 128 + 128],
                                           f_wT[:, ec * T + tq * 512: ec * T + tq * 512 + 512],
                                           start=(ec == 0), stop=(ec == 3))
                edge(ins, "pe", nc.vector)
                sl = slice(dt * T + tq * 512, dt * T + tq * 512 + 512)
                ins = V("scalar_tensor_tensor", xT[:, sl], pb, g_gmc[:, dt:dt + 1], xT[:, sl],
                        op0=ALU.mult, op1=ALU.add)
                k.wait(nc.tensor, "dve")
        barrier()

        w1_slots = [f_W1c, f_W2c]
        h1_slots = [f_h1slot[:, 0:512], f_h1in[:, 0:512]]
        pe_after_ht = {}
        act_slot = {}
        dh_slot = {}
        for c in range(ND):
            dma(nc.gpsimd, w1_slots[0][:, c * 128:(c + 1) * 128],
                w_l1f_W1T[c][:, 0:128], "dw")
        dw_ht = {0: k.sems["dw"][1]}
        for ht in range(32):
            if ht + 1 < 32:
                if ht >= 1:
                    k.wait(nc.gpsimd, "pe", pe_after_ht[ht - 1])
                k.wait(nc.gpsimd, "dw", dw_ht[ht])
                for c in range(ND):
                    dma(nc.gpsimd, w1_slots[(ht + 1) % 2][:, c * 128:(c + 1) * 128],
                        w_l1f_W1T[c][:, (ht + 1) * 128:(ht + 2) * 128], "dw")
                dw_ht[ht + 1] = k.sems["dw"][1]
            k.wait(nc.tensor, "dw", dw_ht[ht])
            for tq in range(4):
                sl = tq % 2
                pb = PSA[:, 2048 + sl * 512: 2048 + sl * 512 + 512]
                if act_slot.get(sl):
                    k.wait(nc.tensor, "act", act_slot[sl])
                for c in range(ND):
                    ins = nc.tensor.matmul(pb,
                                           w1_slots[ht % 2][:, c * 128:(c + 1) * 128],
                                           ylnT[:, c * T + tq * 512: c * T + tq * 512 + 512],
                                           start=(c == 0), stop=(c == ND - 1))
                edge(ins, "pe", nc.scalar)
                hs = (ht * 4 + tq) % 2
                if dh_slot.get(hs):
                    k.wait(nc.scalar, "dh%d" % hs, dh_slot[hs])
                A("activation", h1_slots[hs], pb, AF.Gelu)
                act_slot[sl] = k.sems["act"][1]
                k.wait(nc.sync, "act", act_slot[sl])
                dma(nc.sync, h1_dram[ht][:, tq * 512:(tq + 1) * 512], h1_slots[hs], "dh%d" % hs)
                dh_slot[hs] = k.sems["dh%d" % hs][1]
            pe_after_ht[ht] = k.sems["pe"][1]
        barrier()
        w2_slots = [f_W2c, f_W1c]
        pe_hc = {}
        di_slot = {}
        dw_hc = {}
        for tq in range(4):
            t0 = tq * 512
            for hc in range(32):
                g = tq * 32 + hc
                hbuf = f_h1in[:, (g % 2) * 512:(g % 2) * 512 + 512]
                if pe_hc.get(g - 2):
                    k.wait(nc.sync, "pe", pe_hc[g - 2])
                    k.wait(nc.gpsimd, "pe", pe_hc[g - 2])
                if g > 0:
                    k.wait(nc.gpsimd, "dw", dw_hc[g - 1])
                dma(nc.sync, hbuf, h1_dram[hc][:, t0:t0 + 512], "di%d" % (g % 2))
                di_slot[g] = k.sems["di%d" % (g % 2)][1]
                dma(nc.gpsimd, w2_slots[g % 2][:, 0:1024], w_l1f_W2T[hc], "dw")
                dw_hc[g] = k.sems["dw"][1]
                k.wait(nc.tensor, "di%d" % (g % 2), di_slot[g])
                k.wait(nc.tensor, "dw", dw_hc[g])
                if hc == 0:
                    k.wait(nc.tensor, "dve")
                for dt in range(ND):
                    ins = nc.tensor.matmul(PSA[:, dt * 512:(dt + 1) * 512],
                                           w2_slots[g % 2][:, dt * 128:(dt + 1) * 128], hbuf,
                                           start=(hc == 0), stop=(hc == 31))
                k.inc(ins, "pe")
                pe_hc[g] = k.sems["pe"][1]
            k.wait(nc.vector, "pe")
            for dt in range(ND):
                sl = slice(dt * T + t0, dt * T + t0 + 512)
                ins = V("tensor_tensor", xT[:, sl], xT[:, sl], PSA[:, dt * 512:(dt + 1) * 512], ALU.add)
            k.wait(nc.tensor, "dve")
        barrier()




    s_m8 = k.sb("s_m8", [128, 8], FP32)
    s_v32 = k.sb("s_v32", [128, 32], FP32)
    s_iu = k.sb("s_iu", [128, 32], U32)
    s_ii = k.sb("s_ii", [128, 32], I32)
    s_ig = k.sb("s_ig", [128, 32], I32)
    s_isc = k.sb("s_isc", [128, 32], I32)
    s_icol = k.sb("s_icol", [128, 1], I32)
    s_tc = k.sb("s_tc", [128, 2], I32)
    s_iosm = k.sb("s_iosm", [128, 256], FP32)
    s_th = k.sb("s_th", [128, 1], FP32)
    s_iwb = k.sb("s_iwb", [128, 4], FP32)
    s_iwrow = k.sb("s_iwrow", [1, 4], BF16)
    s_z = k.sb("s_z", [128, 8], FP32)
    s_mx = k.sb("s_mx", [128, 16], FP32)
    s_cm = s_v32
    c_icol_i = win("c_icol_i", [128, 1], I32)
    c_iota_sm = win("c_iota_sm", [128, 256], FP32)
    g_csq64hi_c = win("g_csq64hi_c", [128, T])
    g_csq64hi_s = win("g_csq64hi_s", [128, T])
    c_zeros2k = win("c_zeros2k", [128, 2048], BF16)
    c_t94 = win("c_t94", [128, 1], FP32)
    c_t63 = win("c_t63", [128, 1], FP32)
    c_zone = win("c_zone", [128, 128], BF16)
    g_csq64n = win("g_csq64n", [T, 32], FP32)
    qi_dram = k.dram_tmp("qi_dram", [2, 128, T], BF16)
    ol1a_dram = k.dram_tmp("ol1a_dram", [8, 64, T], BF16)
    sc_dram = k.dram_tmp("sc_dram", [8, 128, 2048], BF16)

    def l1_attn(b):
        barrier()
        SCALE = float((64 + 32) ** -0.5)
        vnat = POOL_BF[:, 0:8192]
        qbar_tt = POOL_BF[:, 8192:8576]
        q2p_tt = POOL_BF[:, 8576:8960]
        kcb = POOL_BF[0:64, 12288:14336]
        wbar = POOL_BF[:, 16384:18432]
        kiT = POOL_BF[0:64, 18688:20736]
        srdq = [POOL_BF[:, 22784:24832], POOL_BF[:, 24832:26880]]
        maskb = POOL_BF[:, 26880:28928]
        stripb = POOL_BF[:, 28928:29184]
        zoneb = POOL_BF[:, 29184:29312]
        asc = POOL_BF[:, 29952:30208]
        slabT = POOL_BF[:, 30208:32256]
        qctt = POOL_BF[0:64, 32256:33280]
        qitt = POOL_BF[0:64, 33280:33792]
        qhtt = POOL_BF[:, 20736:20992]
        obh = POOL_BF[:, 20992:21248]
        fI = POOL_F[:, 0:2048]
        f_relu = POOL_F[:, 2048:4096]
        f_sc = POOL_F[:, 4096:4352]

        k.scope("A_wstage")
        W_ = {}
        off = 0
        for nm, src, n in (("dkv", w_l1a_dkv, ND), ("uk", w_l1a_uk, 2), ("uv", w_l1a_uv, 2),
                           ("dq", w_l1a_dq, ND), ("uq", w_l1a_uq, 4), ("qrR", w_l1a_qrR, 4),
                           ("qr", w_l1a_qr, 4), ("kr", w_l1a_kr, ND), ("iwq", w_l1a_iwq, ND),
                           ("iwk", w_l1a_iwk, ND)):
            per = src.shape[2]
            W_[nm] = (off, per)
            for c in range(n):
                dma(nc.gpsimd, POOL_BF[:, off + c * per: off + (c + 1) * per], src[c], "dw")
            off += n * per
        for c in range(2):
            dma(nc.gpsimd, POOL_BF[:, 29440 + c * 512: 29440 + (c + 1) * 512], w_l1a_uv[c], "dw")
        dma(nc.gpsimd, s_iwrow[0:1, :], w_l1a_iw[:, :], "dw")
        dma(nc.sync, s_icol[:], c_icol_i[:, :])
        dma(nc.sync, s_iosm[:], c_iota_sm[:, :])
        dma(nc.sync, maskb[:, 0:2048], c_zeros2k[:, :])
        k.wait(nc.sync, "dq")
        for zi in range(8):
            zap = bass.AP(srd_dram.tensor, zi * 262144, [[2048, 128], [1, 2048]])
            dma(nc.sync, zap, maskb[:, 0:2048])
        barrier()

        def wsl(nm, c, o, width=128):
            o0, per = W_[nm]
            return POOL_BF[:, o0 + c * per + o: o0 + c * per + o + width]

        cqT = POOL_BF[:, 15104:23296]
        ckvT = POOL_BF[:, 23296:27392]
        ebuf2 = POOL_BF[:, 27392:29440]
        uv2 = POOL_BF[:, 29440:30464]

        def wstat_to(dst_fn, nm, nk, rhs, ochunks, scale=None):
            for oc in range(ochunks):
                for tq in range(4):
                    for c in range(nk):
                        ins = nc.tensor.matmul(PSA[:, tq * 512:(tq + 1) * 512],
                                               wsl(nm, c, oc * 128),
                                               rhs[:, c * T + tq * 512: c * T + tq * 512 + 512],
                                               start=(c == 0), stop=(c == nk - 1))
                edge(ins, "pe", nc.vector)
                k.wait(nc.vector, "dq")
                for tq in range(4):
                    if scale is None:
                        ins = V("tensor_copy", ebuf2[:, tq * 512:(tq + 1) * 512],
                                PSA[:, tq * 512:(tq + 1) * 512])
                    else:
                        ins = V("tensor_scalar", ebuf2[:, tq * 512:(tq + 1) * 512],
                                PSA[:, tq * 512:(tq + 1) * 512], scale, None, op0=ALU.mult)
                k.wait(nc.tensor, "dve")
                dst_fn(oc)

        k.scope("A_proj")
        for oc in range(2):
            for tq in range(4):
                for c in range(ND):
                    ins = nc.tensor.matmul(PSA[:, tq * 512:(tq + 1) * 512],
                                           wsl("dkv", c, oc * 128),
                                           ylnT[:, c * T + tq * 512: c * T + tq * 512 + 512],
                                           start=(c == 0), stop=(c == ND - 1))
            edge(ins, "pe", nc.vector)
            for tq in range(4):
                ins = V("tensor_copy", ckvT[:, oc * T + tq * 512: oc * T + tq * 512 + 512],
                        PSA[:, tq * 512:(tq + 1) * 512])
            k.wait(nc.tensor, "dve")
        def kc_out(oc):
            k.wait(nc.sync, "dve")
            dma(nc.sync, kc_dram[2 * oc], ebuf2[0:64, 0:2048])
            dma(nc.sync, kc_dram[2 * oc + 1], POOL_BF[64:128, 27392:29440])
        wstat_to(kc_out, "uk", 2, ckvT, 4)
        for oc in range(4):
            for tq in range(4):
                for c in range(ND):
                    ins = nc.tensor.matmul(PSA[:, tq * 512:(tq + 1) * 512],
                                           wsl("dq", c, oc * 128),
                                           ylnT[:, c * T + tq * 512: c * T + tq * 512 + 512],
                                           start=(c == 0), stop=(c == ND - 1))
            edge(ins, "pe", nc.vector)
            for tq in range(4):
                ins = V("tensor_copy", cqT[:, oc * T + tq * 512: oc * T + tq * 512 + 512],
                        PSA[:, tq * 512:(tq + 1) * 512])
            k.wait(nc.tensor, "dve")
        for st in range(NT):
            for c in range(2):
                ins = nc.tensor.matmul(PSA[:, 2048:2560],
                                       ckvT[:, c * T + st * 128: c * T + st * 128 + 128],
                                       uv2[:, c * 512:(c + 1) * 512],
                                       start=(c == 0), stop=(c == 1))
            edge(ins, "pe", nc.vector)
            ins = V("tensor_copy", vnat[:, st * 512:(st + 1) * 512], PSA[:, 2048:2560])
            k.wait(nc.tensor, "dve")
        def qc_out(oc):
            k.wait(nc.sync, "dve")
            dma(nc.sync, qc_dram[oc], ebuf2[:, 0:2048])
        wstat_to(qc_out, "uq", 4, cqT, 4, scale=SCALE)
        qrT = [POOL_BF[:, 23296:25344], POOL_BF[:, 25344:27392]]
        for oc in range(2):
            for tq in range(4):
                for c in range(4):
                    ins = nc.tensor.matmul(PSA[:, tq * 512:(tq + 1) * 512],
                                           wsl("qrR", c, oc * 128),
                                           cqT[:, c * T + tq * 512: c * T + tq * 512 + 512],
                                           start=(c == 0), stop=(c == 3))
            edge(ins, "pe", nc.vector)
            for tq in range(4):
                ins = V("tensor_copy", qrT[oc][:, tq * 512:(tq + 1) * 512],
                        PSA[:, tq * 512:(tq + 1) * 512])
            k.wait(nc.tensor, "dve")
        for tt in range(NT):
            t0 = tt * 128
            for c in range(4):
                ins = nc.tensor.matmul(PSA[:, 2048:2304],
                                       cqT[:, c * T + t0: c * T + t0 + 128],
                                       POOL_BF[:, W_["qr"][0] + c * 256: W_["qr"][0] + (c + 1) * 256],
                                       start=(c == 0), stop=(c == 3))
            edge(ins, "pe", nc.vector)
            V("tensor_scalar", POOL_F[:, 4608:4864], PSA[:, 2048:2304], 0.0, None, op0=ALU.max)
            ins = V("tensor_copy", POOL_F[:, 4864:5120], PSA[:, 2048:2304])
            k.wait(nc.scalar, "dve"); k.wait(nc.tensor, "dve")
            ins = A("activation", POOL_F[:, 5120:5376], POOL_F[:, 4864:5120], AF.Abs)
            ins = A("activation", POOL_F[:, 5120:5376], POOL_F[:, 5120:5376], AF.Exp, scale=-1.0)
            ins = A("activation", POOL_F[:, 5120:5376], POOL_F[:, 5120:5376], AF.Ln, bias=1.0)
            k.wait(nc.vector, "act"); k.wait(nc.vector, "dq")
            ins = V("tensor_tensor", ebuf2[:, 0:256], POOL_F[:, 4608:4864], POOL_F[:, 5120:5376], ALU.add)
            k.wait(nc.sync, "dve")
            dma(nc.sync, qh_dram[tt], ebuf2[:, 0:256])
        barrier()
        for st in range(NT):
            s0 = st * 128
            for c in range(ND):
                ins = nc.tensor.matmul(PSA[:, 2048:2080],
                                       ylnT[:, c * T + s0: c * T + s0 + 128],
                                       wsl("kr", c, 0, 32),
                                       start=(c == 0), stop=(c == ND - 1))
            edge(ins, "pe", nc.vector)
            V("tensor_scalar", POOL_F[:, 0:32], PSA[:, 2048:2080], 0.0, None, op0=ALU.max)
            ins = V("tensor_copy", POOL_F[:, 32:64], PSA[:, 2048:2080])
            k.wait(nc.scalar, "dve"); k.wait(nc.tensor, "dve")
            ins = A("activation", POOL_F[:, 64:96], POOL_F[:, 32:64], AF.Abs)
            ins = A("activation", POOL_F[:, 64:96], POOL_F[:, 64:96], AF.Exp, scale=-1.0)
            ins = A("activation", POOL_F[:, 64:96], POOL_F[:, 64:96], AF.Ln, bias=1.0)
            k.wait(nc.vector, "act"); k.wait(nc.vector, "dq")
            ins = V("tensor_tensor", POOL_F[:, 96:128], POOL_F[:, 0:32], POOL_F[:, 64:96], ALU.add)
            k.wait(nc.sync, "dve")
            dma(nc.sync, mu_dram[s0:s0 + 128, :], POOL_F[:, 96:128])

        barrier()
        what = [POOL_F[0:16, 4096:6144], POOL_F[0:16, 6144:8192]]
        for comp in range(2):
            for tq in range(4):
                for c in range(ND):
                    ins = nc.tensor.matmul(PSA[0:16, 2048 + tq * 512: 2048 + tq * 512 + 512],
                                           wsl("kr", c, comp * 16, 16),
                                           ylnT[:, c * T + tq * 512: c * T + tq * 512 + 512],
                                           start=(c == 0), stop=(c == ND - 1))
            edge(ins, "pe", nc.vector)
            V("tensor_scalar", POOL_F[0:16, 0:2048], PSA[0:16, 2048:4096], 0.0, None, op0=ALU.max)
            ins = V("tensor_copy", POOL_F[0:16, 2048:4096], PSA[0:16, 2048:4096])
            k.wait(nc.scalar, "dve"); k.wait(nc.tensor, "dve")
            ins = A("activation", POOL_F[0:16, 2048:4096], POOL_F[0:16, 2048:4096], AF.Abs)
            ins = A("activation", POOL_F[0:16, 2048:4096], POOL_F[0:16, 2048:4096], AF.Exp, scale=-1.0)
            ins = A("activation", POOL_F[0:16, 2048:4096], POOL_F[0:16, 2048:4096], AF.Ln, bias=1.0)
            k.wait(nc.vector, "act")
            ins = V("tensor_tensor", what[comp], POOL_F[0:16, 0:2048], POOL_F[0:16, 2048:4096], ALU.add)
            k.wait(nc.tensor, "dve")
        k.scope("A_rot")
        barrier()
        gstage = POOL_F[0:16, 2048:4096]
        wtmp = POOL_F[0:16, 0:2048]
        for comp in range(2):
            k.wait(nc.sync, "dve"); k.wait(nc.sync, "act"); k.wait(nc.sync, "dq")
            k.wait(nc.vector, "dw")
            dma(nc.sync, gstage, g_cswbar[0:16, 0:T] if comp == 0 else g_cswbar[0:16, T:2 * T])
            k.wait(nc.vector, "dq")
            V("tensor_tensor", wtmp, what[0], gstage, ALU.mult)
            k.wait(nc.sync, "dve")
            dma(nc.sync, gstage, g_cswbar[0:16, T:2 * T] if comp == 0 else g_cswbar[0:16, 0:T])
            k.wait(nc.vector, "dq")
            V("tensor_tensor", gstage, what[1], gstage, ALU.mult)
            if comp == 0:
                ins = V("tensor_sub", wtmp, wtmp, gstage)
            else:
                ins = V("tensor_add", wtmp, wtmp, gstage)
            k.wait(nc.gpsimd, "dve")
            for a4 in range(3):
                dma(nc.gpsimd, POOL_BF[a4 * 32 + comp * 16:a4 * 32 + comp * 16 + 16, 16384:18432], wtmp, "dw")

        barrier()
        for oc in range(2):
            for tq in range(4):
                for c in range(ND):
                    ins = nc.tensor.matmul(PSA[:, tq * 512:(tq + 1) * 512],
                                           wsl("iwq", c, oc * 128),
                                           ylnT[:, c * T + tq * 512: c * T + tq * 512 + 512],
                                           start=(c == 0), stop=(c == ND - 1))
            edge(ins, "pe", nc.vector)
            k.wait(nc.vector, "dq")
            for tq in range(4):
                ins = V("tensor_copy", ebuf2[:, tq * 512:(tq + 1) * 512],
                        PSA[:, tq * 512:(tq + 1) * 512])
            k.wait(nc.tensor, "dve"); k.wait(nc.sync, "dve")
            dma(nc.sync, qi_dram[oc], ebuf2[:, 0:2048])
        kiT = POOL_BF[0:64, 18688:20736]
        for tq in range(4):
            for c in range(ND):
                ins = nc.tensor.matmul(PSA[0:64, tq * 512:(tq + 1) * 512],
                                       wsl("iwk", c, 0, 64),
                                       ylnT[:, c * T + tq * 512: c * T + tq * 512 + 512],
                                       start=(c == 0), stop=(c == ND - 1))
        edge(ins, "pe", nc.vector)
        for tq in range(4):
            ins = V("tensor_copy", kiT[:, tq * 512:(tq + 1) * 512], PSA[0:64, tq * 512:(tq + 1) * 512])
        k.wait(nc.tensor, "dve")

        barrier()
        qsp = [POOL_F[:, 0:2048], POOL_F[:, 2048:4096]]
        for comp in range(2):
            src = POOL_BF[:, 23296 + comp * 2048: 23296 + (comp + 1) * 2048]
            V("tensor_scalar", qsp[comp], src, 0.0, None, op0=ALU.max)
            ins = V("tensor_copy", POOL_F[:, 4096:6144], src)
            k.wait(nc.scalar, "dve")
            ins = A("activation", POOL_F[:, 4096:6144], POOL_F[:, 4096:6144], AF.Abs)
            ins = A("activation", POOL_F[:, 4096:6144], POOL_F[:, 4096:6144], AF.Exp, scale=-1.0)
            ins = A("activation", POOL_F[:, 4096:6144], POOL_F[:, 4096:6144], AF.Ln, bias=1.0)
            k.wait(nc.vector, "act")
            ins = V("tensor_tensor", qsp[comp], qsp[comp], POOL_F[:, 4096:6144], ALU.add)
        k.wait(nc.sync, "dve")
        gc = POOL_F[:, 4096:6144]
        gs = POOL_F[:, 6144:8192]
        rot_out = [POOL_BF[:, 20736:22784], POOL_BF[:, 22784:24832]]
        rot_tmp = POOL_BF[:, 24832:26880]
        k.wait(nc.sync, "dve"); k.wait(nc.sync, "dq")
        dma(nc.sync, gc, g_csband_c[:, :])
        dma(nc.sync, gs, g_csband_s[:, :])
        k.wait(nc.vector, "dq")
        V("tensor_tensor", rot_out[0], qsp[0], gc, ALU.mult)
        V("tensor_tensor", rot_tmp, qsp[1], gs, ALU.mult)
        V("tensor_sub", rot_out[0], rot_out[0], rot_tmp)
        V("tensor_tensor", rot_out[1], qsp[0], gs, ALU.mult)
        V("tensor_tensor", rot_tmp, qsp[1], gc, ALU.mult)
        ins = V("tensor_add", rot_out[1], rot_out[1], rot_tmp)
        k.wait(nc.sync, "dve")
        dma(nc.sync, qb_dram[2], rot_out[0])
        dma(nc.sync, qb_dram[3], rot_out[1])
        k.wait(nc.vector, "dq")
        k.wait(nc.sync, "dve")
        k.wait(nc.sync, "dq")
        dma(nc.sync, gc[:, 0:128], g_csq2_c[:, :])
        dma(nc.sync, gs[:, 0:128], g_csq2_s[:, :])
        k.wait(nc.vector, "dq")
        V("tensor_tensor", rot_out[0][:, 0:128], qsp[0][:, 0:128], gc[:, 0:128], ALU.mult)
        V("tensor_tensor", rot_tmp[:, 0:128], qsp[1][:, 0:128], gs[:, 0:128], ALU.mult)
        V("tensor_sub", rot_out[0][:, 0:128], rot_out[0][:, 0:128], rot_tmp[:, 0:128])
        V("tensor_tensor", rot_out[1][:, 0:128], qsp[0][:, 0:128], gs[:, 0:128], ALU.mult)
        V("tensor_tensor", rot_tmp[:, 0:128], qsp[1][:, 0:128], gc[:, 0:128], ALU.mult)
        ins = V("tensor_add", rot_out[1][:, 0:128], rot_out[1][:, 0:128], rot_tmp[:, 0:128])
        k.wait(nc.sync, "dve")
        dma(nc.sync, qb_dram[4][:, 0:128], rot_out[0][:, 0:128])
        dma(nc.sync, qb_dram[5][:, 0:128], rot_out[1][:, 0:128])
        k.wait(nc.tensor, "dw"); k.wait(nc.tensor, "dq")
        ins = nc.tensor.matmul(PSA[:, 2048:2052], ones_b[0:1, 0:128], s_iwrow[0:1, 0:4], start=True, stop=True)
        edge(ins, "pe", nc.vector)
        ins = V("tensor_copy", s_iwb[:], PSA[:, 2048:2052])
        k.wait(nc.sync, "dve"); k.wait(nc.sync, "dq")
        dma(nc.sync, POOL_F[:, 5120:6144], g_csrot[:, 0:1024])
        dma(nc.sync, POOL_F[:, 6144:7168], g_csrot[:, 1024:2048])
        dma(nc.sync, zoneb[:, 0:128], c_zone[:, :])
        grc = POOL_F[:, 5120:6144]
        grs = POOL_F[:, 6144:7168]
        k.wait(nc.vector, "dq")
        V("tensor_scalar", s_tc[:, 0:1], s_icol[:], 2048, None, op0=ALU.mult)
        ins = V("tensor_scalar", s_tc[:, 1:2], s_icol[:], 32 - 2048, None, op0=ALU.mult)
        barrier()

        mug = POOL_F[:, 7168:8192]
        NBIG = 128 * 2048
        TT_DBG = 8

        def FA(col, *dims):
            a0 = POOL_F[:, col:col + 1]
            return bass.AP(a0.tensor, a0.offset, [[a0.ap[0][0], 128]] + [list(d) for d in dims])
        for tt in range(NT):
            t0 = tt * 128
            Wk = 128 * (tt + 1)
            nq = (Wk + 511) // 512
            k.scope("A%02d_stage" % tt)
            k.wait(nc.sync, "pe"); k.wait(nc.sync, "dve"); k.wait(nc.sync, "dq")
            for h8 in range(8):
                qdst = POOL_BF[0:64, 32256 + h8 * 128:32256 + (h8 + 1) * 128] if h8 < 4 \
                    else POOL_BF[64:128, 32256 + (h8 - 4) * 128:32256 + (h8 - 3) * 128]
                dma(nc.sync, qdst,
                    qc_dram[h8 // 2][(h8 % 2) * 64:(h8 % 2) * 64 + 64, t0:t0 + 128])
            for hi in range(4):
                dma(nc.sync, POOL_BF[0:64, 33280 + hi * 128:33280 + (hi + 1) * 128],
                    qi_dram[hi // 2][(hi % 2) * 64:(hi % 2) * 64 + 64, t0:t0 + 128])
            dma(nc.sync, qhtt[:, 0:256], qh_dram[tt])
            for h in range(8):
                blk, ab = h // 3, h % 3
                for comp in range(2):
                    dma(nc.sync, POOL_BF[ab * 32 + comp * 16:ab * 32 + comp * 16 + 16,
                                         8192 + blk * 128:8192 + (blk + 1) * 128],
                        qb_dram[2 + comp][h * 16:(h + 1) * 16, t0:t0 + 128])
            if tt == 0:
                for h in range(8):
                    blk, ab = h // 3, h % 3
                    for comp in range(2):
                        dma(nc.sync, POOL_BF[ab * 32 + comp * 16:ab * 32 + comp * 16 + 16,
                                             8576 + blk * 128:8576 + (blk + 1) * 128],
                            qb_dram[4 + comp][h * 16:(h + 1) * 16, 0:128])
            k.scope("A%02d_isc" % tt)
            k.wait(nc.tensor, "dq")
            for hi in range(4):
                lh = POOL_BF[0:64, 33280 + hi * 128:33280 + (hi + 1) * 128]
                for ch in range(nq):
                    w = min(512, Wk - ch * 512)
                    ins = nc.tensor.matmul(PSA[:, ch * 512:ch * 512 + w], lh, kiT[:, ch * 512:ch * 512 + w],
                                           start=True, stop=True)
                edge(ins, "pe", nc.scalar)
                k.wait(nc.scalar, "dve")
                for ch in range(nq):
                    w = min(512, Wk - ch * 512)
                    ins = A("activation", f_relu[:, ch * 512:ch * 512 + w], PSA[:, ch * 512:ch * 512 + w], AF.Relu)
                k.wait(nc.vector, "act"); k.wait(nc.tensor, "act")
                if hi == 0:
                    ins = V("tensor_scalar", fI[:, 0:Wk], f_relu[:, 0:Wk], s_iwb[:, 0:1], None, op0=ALU.mult)
                else:
                    ins = V("scalar_tensor_tensor", fI[:, 0:Wk], f_relu[:, 0:Wk], s_iwb[:, hi:hi + 1],
                            fI[:, 0:Wk], op0=ALU.mult, op1=ALU.add)
            ins = V("tensor_scalar", fI[:, t0 + 64:Wk], fI[:, t0 + 64:Wk], -2e30, None, op0=ALU.add)
            B0 = max(0, t0 - 64)
            bw = (t0 + 64) - B0
            V("tensor_scalar", s_th[:], iota_col[:], float(t0 - 63 - B0), None, op0=ALU.add)
            V("tensor_scalar", f_relu[:, 0:bw], s_iosm[:, 0:bw], s_th[:, 0:1], None, op0=ALU.is_ge)
            ins = V("scalar_tensor_tensor", fI[:, B0:B0 + bw], f_relu[:, 0:bw], -2e30, fI[:, B0:B0 + bw],
                    op0=ALU.mult, op1=ALU.add)
            if b == 0 and tt == TT_DBG:
                dump("selI", fI[:, 0:2048], [128, 2048])
            k.scope("A%02d_sel" % tt)
            for r in range(4):
                V("max", out=s_m8[:, 0:8], in_=fI[:, 0:Wk])
                V("max_index", out=s_iu[:, r * 8:(r + 1) * 8], in_max=s_m8[:, 0:8], in_values=fI[:, 0:Wk])
                V("tensor_copy", s_v32[:, r * 8:(r + 1) * 8], s_m8[:, 0:8])
                V("match_replace", out=fI[:, 0:Wk], in_to_replace=s_m8[:, 0:8], in_values=fI[:, 0:Wk],
                  imm_value=-3e30)
            V("tensor_copy", s_ii[:], s_iu[:])
            V("tensor_scalar", s_ig[:], s_ii[:], 0, 2047, op0=ALU.max, op1=ALU.min)
            V("tensor_scalar", f_sc[:, 0:32], s_v32[:], -1e29, None, op0=ALU.is_le)
            V("tensor_copy", s_isc[:], f_sc[:, 0:32])
            V("tensor_copy", s_ii[:], s_iosm[:, 0:32])
            V("tensor_scalar", s_ii[:], s_ii[:], NBIG, None, op0=ALU.add)
            V("tensor_tensor", s_ii[:], s_ii[:], s_ig[:], ALU.subtract)
            V("tensor_tensor", s_ii[:], s_ii[:], s_tc[:, 1:2].to_broadcast([128, 32]), ALU.add)
            V("tensor_tensor", s_isc[:], s_isc[:], s_ii[:], ALU.mult)
            V("tensor_tensor", s_isc[:], s_isc[:], s_ig[:], ALU.add)
            ins = V("tensor_tensor", s_isc[:], s_isc[:], s_tc[:, 0:1].to_broadcast([128, 32]), ALU.add)
            if b == 0 and tt == TT_DBG:
                dump("selidx", s_iu[:], [128, 32], U32)
                dump("selv", s_v32[:], [128, 32])

            k.scope("A%02d_maskb" % tt)
            V("tensor_scalar", f_relu[:, 0:Wk], fI[:, 0:Wk], -2.5e30, None, op0=ALU.is_le)
            tA = POOL_F[:, 4096:4352]; tB = POOL_F[:, 4352:4608]
            tC = POOL_F[:, 4608:4864]
            if tt >= 1:
                B0m = t0 - 128
                if B0m > 0:
                    V("tensor_scalar", maskb[:, 0:B0m], f_relu[:, 0:B0m], 2e30, -2e30,
                      op0=ALU.mult, op1=ALU.add)
                V("tensor_scalar", s_th[:], iota_col[:], 65.0, None, op0=ALU.add)
                V("tensor_scalar", tA[:, 0:256], s_iosm[:, 0:256], s_th[:, 0:1], None, op0=ALU.is_ge)
                V("tensor_scalar", s_th[:], iota_col[:], 128.0, None, op0=ALU.add)
                V("tensor_scalar", tB[:, 0:256], s_iosm[:, 0:256], s_th[:, 0:1], None, op0=ALU.is_le)
                V("tensor_tensor", stripb[:, 0:256], tA[:, 0:256], tB[:, 0:256], ALU.mult)
                V("tensor_tensor", tC[:, 0:256], f_relu[:, B0m:B0m + 256], tA[:, 0:256], ALU.mult)
                V("tensor_sub", tC[:, 0:256], f_relu[:, B0m:B0m + 256], tC[:, 0:256])
                V("tensor_tensor", tC[:, 0:256], tC[:, 0:256], stripb[:, 0:256], ALU.add)
                V("tensor_scalar", maskb[:, B0m:B0m + 256], tC[:, 0:256], 2e30, -2e30,
                  op0=ALU.mult, op1=ALU.add)
            else:
                V("tensor_scalar", s_th[:], iota_col[:], -63.0, None, op0=ALU.add)
                V("tensor_scalar", tA[:, 0:128], s_iosm[:, 0:128], s_th[:, 0:1], None, op0=ALU.is_ge)
                V("tensor_scalar", tB[:, 0:128], s_iosm[:, 0:128], iota_col[:, 0:1], None, op0=ALU.is_le)
                V("tensor_tensor", stripb[:, 0:128], tA[:, 0:128], tB[:, 0:128], ALU.mult)
                V("tensor_tensor", tC[:, 0:128], f_relu[:, 0:128], tA[:, 0:128], ALU.mult)
                V("tensor_sub", tC[:, 0:128], f_relu[:, 0:128], tC[:, 0:128])
                V("tensor_tensor", tC[:, 0:128], tC[:, 0:128], stripb[:, 0:128], ALU.add)
                V("tensor_tensor", tC[:, 0:128], tC[:, 0:128], zoneb[:, 0:128], ALU.add)
                V("tensor_scalar", tC[:, 0:128], tC[:, 0:128], 1.0, None, op0=ALU.min)
                V("tensor_scalar", maskb[:, 0:128], tC[:, 0:128], 2e30, -2e30,
                  op0=ALU.mult, op1=ALU.add)
            if b == 0 and tt == TT_DBG:
                dump("maskb", maskb[:, 0:2048], [128, 2048], BF16)

            k.scope("A%02d_gat" % tt)
            k.wait(nc.gpsimd, "dve")
            for m in range(32):
                ins = nc.gpsimd.indirect_dma_start(
                    out=mug[:, m * 32:(m + 1) * 32], out_offset=None,
                    in_=mu_dram[:, :],
                    in_offset=IndirectOffsetOnAxis(s_ig[:, m:m + 1], 0))
                k.inc(ins, "dw", 16)
            if b == 0 and tt == TT_DBG:
                k.wait(nc.sync, "dw")
                dump("mug", mug[:, 0:1024], [128, 1024])

            if tt >= 1 and os.environ.get("KBANDMOVE", "1") == "1":
                k.scope("A%02d_band" % tt)
                ps_band = PSA[:, 2048:2304]
                for h in range(8):
                    blk, ab = h // 3, h % 3
                    k.wait(nc.tensor, "dve")
                    nc.tensor.matmul(ps_band[:, 0:128],
                                     qbar_tt[ab * 32:(ab + 1) * 32, blk * 128:blk * 128 + 128],
                                     wbar[ab * 32:(ab + 1) * 32, t0 - 128:t0], start=True, stop=True)
                    ins = nc.tensor.matmul(ps_band[:, 128:256],
                                           qbar_tt[ab * 32:(ab + 1) * 32, blk * 128:blk * 128 + 128],
                                           wbar[ab * 32:(ab + 1) * 32, t0:t0 + 128], start=True, stop=True)
                    edge(ins, "pe", nc.vector)
                    V("tensor_tensor", bdm_all[:, h * 256:(h + 1) * 256],
                      ps_band[:, 0:256], stripb[:, 0:256], ALU.mult)

            k.wait(nc.vector, "dw")
            GT = 4096
            V("tensor_tensor", POOL_F[:, 2048:3072], mug[:, 0:1024], grc[:, 0:1024], ALU.mult)
            V("tensor_tensor", POOL_F[:, 3072:4096], mug[:, 0:1024], grs[:, 0:1024], ALU.mult)
            V("tensor_tensor", FA(GT, [32, 32], [1, 16]), FA(2048, [32, 32], [1, 16]),
              FA(3072 + 16, [32, 32], [1, 16]), ALU.subtract)
            V("tensor_tensor", FA(GT + 16, [32, 32], [1, 16]), FA(3072, [32, 32], [1, 16]),
              FA(2048 + 16, [32, 32], [1, 16]), ALU.add)

            k.wait(nc.sync, "dve"); k.wait(nc.sync, "dq")
            dma(nc.sync, POOL_F[:, 512:544], g_csq64n[t0:t0 + 128, :])
            k.wait(nc.vector, "dq")
            cosb = FA(512, [0, 8], [1, 16]); sinb = FA(528, [0, 8], [1, 16])
            qh1 = bass.AP(qhtt.tensor, qhtt.offset, [[qhtt.ap[0][0], 128], [32, 8], [1, 16]])
            qh2 = bass.AP(qhtt.tensor, qhtt.offset + 16, [[qhtt.ap[0][0], 128], [32, 8], [1, 16]])
            V("tensor_tensor", FA(256, [32, 8], [1, 16]), qh1, cosb, ALU.mult)
            V("tensor_tensor", FA(0, [32, 8], [1, 16]), qh2, sinb, ALU.mult)
            V("tensor_tensor", FA(0, [32, 8], [1, 16]), FA(256, [32, 8], [1, 16]),
              FA(0, [32, 8], [1, 16]), ALU.subtract)
            V("tensor_tensor", FA(256, [32, 8], [1, 16]), qh1, sinb, ALU.mult)
            V("tensor_tensor", FA(16, [32, 8], [1, 16]), qh2, cosb, ALU.mult)
            V("tensor_tensor", FA(16, [32, 8], [1, 16]), FA(256, [32, 8], [1, 16]),
              FA(16, [32, 8], [1, 16]), ALU.add)

            for h in range(8):
                V("tensor_tensor", POOL_F[:, 2048:3072], POOL_F[:, GT:GT + 1024],
                  FA(h * 32, [0, 32], [1, 32]), ALU.mult)
                V("tensor_tensor", FA(3072, [16, 32], [1, 16]), FA(2048, [32, 32], [1, 16]),
                  FA(2048 + 16, [32, 32], [1, 16]), ALU.add)
                V("tensor_tensor", FA(3584, [8, 32], [1, 8]), FA(3072, [16, 32], [1, 8]),
                  FA(3072 + 8, [16, 32], [1, 8]), ALU.add)
                V("tensor_tensor", FA(3840, [4, 32], [1, 4]), FA(3584, [8, 32], [1, 4]),
                  FA(3584 + 4, [8, 32], [1, 4]), ALU.add)
                V("tensor_tensor", FA(3968, [2, 32], [1, 2]), FA(3840, [4, 32], [1, 2]),
                  FA(3840 + 2, [4, 32], [1, 2]), ALU.add)
                V("tensor_tensor", FA(544 + h, [8, 32]), FA(3968, [2, 32]),
                  FA(3968 + 1, [2, 32]), ALU.add)
            ins = V("tensor_copy", asc[:, 0:256], POOL_F[:, 544:800])
            if b == 0 and tt == TT_DBG:
                dump("pack", POOL_F[:, 544:800], [128, 256])

            k.scope("A%02d_scat" % tt)
            k.wait(nc.gpsimd, "dve"); k.wait(nc.gpsimd, "dq")
            for m in range(32):
                ins = nc.gpsimd.indirect_dma_start(
                    out=srd_dram[:, :],
                    out_offset=IndirectOffsetOnAxis(s_isc[:, m:m + 1], 0),
                    in_=asc[:, m * 8:(m + 1) * 8], in_offset=None)
                k.inc(ins, "dw", 16)

            bdm_all = POOL_BF[:, 14336:16384]
            srdq_full = POOL_BF[:, 22784:26880]
            sc_bf = [POOL_BF[:, 32768:33280], POOL_BF[:, 33280:33792]]
            PST = 33792

            if tt == 0:
                k.wait(nc.sync, "dw"); k.wait(nc.sync, "dve")
                srd_ap0 = bass.AP(srd_dram.tensor, 0, [[16384, 128], [1, 1024]])
                dma(nc.sync, srdq_full[:, 0:1024], srd_ap0)
                for h in range(8):
                    blk, ab = h // 3, h % 3
                    base = 0 if h < 4 else 64
                    kc_h = POOL_BF[base:base + 64, 12288:12416]
                    k.wait(nc.sync, "dve"); k.wait(nc.sync, "pe")
                    dma(nc.sync, kc_h, kc_dram[h][:, 0:128])
                    k.wait(nc.tensor, "dq"); k.wait(nc.tensor, "dve")
                    qc_h = POOL_BF[0:64, 32256 + h * 128:32256 + (h + 1) * 128] if h < 4 \
                        else POOL_BF[64:128, 32256 + (h - 4) * 128:32256 + (h - 3) * 128]
                    nc.tensor.matmul(PSA[:, 0:128], qc_h, kc_h, start=True, stop=True)
                    ps_band = PSA[:, 2048:2304]
                    ps_r2 = PSA[:, 2560:2688]
                    nc.tensor.matmul(ps_band[:, 0:128],
                                     qbar_tt[ab * 32:(ab + 1) * 32, blk * 128:blk * 128 + 128],
                                     wbar[ab * 32:(ab + 1) * 32, 0:128], start=True, stop=True)
                    ins = nc.tensor.matmul(ps_r2[:, 0:128],
                                           q2p_tt[ab * 32:(ab + 1) * 32, blk * 128:blk * 128 + 128],
                                           wbar[ab * 32:(ab + 1) * 32, 0:128], start=True, stop=True)
                    edge(ins, "pe", nc.vector)
                    k.wait(nc.vector, "dq")
                    sA = bass.AP(srdq_full.tensor, srdq_full.offset + h, [[PST, 128], [8, 128]])
                    V("tensor_tensor", fI[:, 0:128], PSA[:, 0:128], sA, ALU.add)
                    bdm = POOL_F[:, 4096:4352]
                    V("tensor_tensor", bdm[:, 0:128], ps_band[:, 0:128], stripb[:, 0:128], ALU.mult)
                    V("tensor_tensor", fI[:, 0:128], fI[:, 0:128], bdm[:, 0:128], ALU.add)
                    r1 = POOL_F[:, 4352:4480]
                    V("tensor_tensor", r1[:, 0:128], ps_r2[:, 0:128], sA, ALU.subtract)
                    V("tensor_sub", r1[:, 0:128], r1[:, 0:128], bdm[:, 0:128])
                    V("tensor_tensor", r1[:, 0:128], r1[:, 0:128], zoneb[:, 0:128], ALU.mult)
                    V("tensor_tensor", fI[:, 0:128], fI[:, 0:128], r1[:, 0:128], ALU.add)
                    V("tensor_tensor", fI[:, 0:128], fI[:, 0:128], maskb[:, 0:128], ALU.add)
                    V("reduce_max", out=s_mx[:, 0:1], in_=fI[:, 0:128], axis=AX)
                    if b == 0 and tt == TT_DBG and h == 0:
                        dump("slab", fI[:, 0:2048], [128, 2048])
                    V("tensor_scalar", s_mx[:, 1:2], s_mx[:, 0:1], -1.0, None, op0=ALU.mult)
                    k.wait(nc.scalar, "dve")
                    A("activation", fI[:, 0:128], fI[:, 0:128], AF.Exp, scale=1.0,
                      bias=s_mx[:, 1:2], accum_out=s_mx[:, 2:3])
                    k.wait(nc.vector, "act")
                    V("reciprocal", s_mx[:, 3:4], s_mx[:, 2:3])
                    V("tensor_scalar", fI[:, 0:128], fI[:, 0:128], s_mx[:, 3:4], None, op0=ALU.mult)
                    k.wait(nc.tensor, "dve")
                    ins = nc.tensor.transpose(PSA[:, 3584:3712], fI[:, 0:128], ident[:])
                    edge(ins, "pe", nc.vector)
                    V("tensor_copy", slabT[:, 0:128], PSA[:, 3584:3712])
                    k.wait(nc.tensor, "dve")
                    ps_o = PSA[0:64, 3072:3200]
                    ins = nc.tensor.matmul(ps_o, vnat[:, h * 64:h * 64 + 64],
                                           slabT[:, 0:128], start=True, stop=True)
                    edge(ins, "pe", nc.vector)
                    k.wait(nc.vector, "dob")
                    ins = V("tensor_copy", obh[0:64, 0:128], ps_o)
                    k.wait(nc.sync, "dve")
                    dma(nc.sync, ol1a_dram[h][0:64, t0:t0 + 128], obh[0:64, 0:128], "dob")
            else:
                if os.environ.get("KBANDMOVE", "1") != "1":
                    k.scope("A%02d_band" % tt)
                    ps_band = PSA[:, 2048:2304]
                    for h in range(8):
                        blk, ab = h // 3, h % 3
                        k.wait(nc.tensor, "dve")
                        nc.tensor.matmul(ps_band[:, 0:128],
                                         qbar_tt[ab * 32:(ab + 1) * 32, blk * 128:blk * 128 + 128],
                                         wbar[ab * 32:(ab + 1) * 32, t0 - 128:t0], start=True, stop=True)
                        ins = nc.tensor.matmul(ps_band[:, 128:256],
                                               qbar_tt[ab * 32:(ab + 1) * 32, blk * 128:blk * 128 + 128],
                                               wbar[ab * 32:(ab + 1) * 32, t0:t0 + 128], start=True, stop=True)
                        edge(ins, "pe", nc.vector)
                        V("tensor_tensor", bdm_all[:, h * 256:(h + 1) * 256],
                          ps_band[:, 0:256], stripb[:, 0:256], ALU.mult)
                k.scope("A%02d_pA" % tt)
                k.wait(nc.sync, "dw")
                k.wait(nc.sync, "dsb0"); k.wait(nc.sync, "dsb1")
                for ch in range(nq):
                    c0 = ch * 512
                    cw = min(512, Wk - c0)
                    k.wait(nc.sync, "dve"); k.wait(nc.sync, "pe"); k.wait(nc.sync, "dq")
                    srd_apc = bass.AP(srd_dram.tensor, c0 * 8, [[16384, 128], [1, cw * 8]])
                    dma(nc.sync, srdq_full[:, 0:cw * 8], srd_apc)
                    for h in range(8):
                        base = 0 if h < 4 else 64
                        dma(nc.sync, POOL_BF[base:base + 64, 12288 + (h % 4) * 512:12288 + (h % 4) * 512 + cw],
                            kc_dram[h][:, c0:c0 + cw])
                    k.wait(nc.tensor, "dq"); k.wait(nc.tensor, "dve")
                    for h in range(8):
                        qc_h = POOL_BF[0:64, 32256 + h * 128:32256 + (h + 1) * 128] if h < 4 \
                            else POOL_BF[64:128, 32256 + (h - 4) * 128:32256 + (h - 3) * 128]
                        kc_h = POOL_BF[0:64, 12288 + h * 512:12288 + h * 512 + cw] if h < 4 \
                            else POOL_BF[64:128, 12288 + (h - 4) * 512:12288 + (h - 4) * 512 + cw]
                        ins = nc.tensor.matmul(PSA[:, h * 512:h * 512 + cw], qc_h, kc_h,
                                               start=True, stop=True)
                    edge(ins, "pe", nc.vector)
                    k.wait(nc.vector, "dq")
                    d0 = t0 - 128
                    for h in range(8):
                        sA = bass.AP(srdq_full.tensor, srdq_full.offset + h, [[PST, 128], [8, cw]])
                        V("tensor_tensor", fI[:, 0:cw], PSA[:, h * 512:h * 512 + cw], sA, ALU.add)
                        o1 = max(d0, c0); o2 = min(t0 + 128, c0 + cw)
                        if o1 < o2:
                            V("tensor_tensor", fI[:, o1 - c0:o2 - c0], fI[:, o1 - c0:o2 - c0],
                              bdm_all[:, h * 256 + (o1 - d0):h * 256 + (o2 - d0)], ALU.add)
                        k.wait(nc.vector, "dsc%d" % (h % 2))
                        V("tensor_tensor", sc_bf[h % 2][:, 0:cw], fI[:, 0:cw],
                          maskb[:, c0:c0 + cw], ALU.add)
                        V("reduce_max", out=s_cm[:, h * 4 + ch:h * 4 + ch + 1],
                          in_=sc_bf[h % 2][:, 0:cw], axis=AX)
                        k.wait(nc.sync, "dve")
                        dma(nc.sync, sc_dram[h][:, c0:c0 + cw], sc_bf[h % 2][:, 0:cw], "dsc%d" % (h % 2))
                k.scope("A%02d_pB" % tt)
                KPB = os.environ.get("KPB", "1") == "1"
                dsb_v = {}
                for h in range(8):
                    if h == 0:
                        k.wait(nc.sync, "dve"); k.wait(nc.sync, "act")
                        k.wait(nc.sync, "dsc0"); k.wait(nc.sync, "dsc1")
                        dma(nc.sync, srdq[0][:, 0:Wk], sc_dram[0][:, 0:Wk], "dsb0")
                        dsb_v[0] = k.sems["dsb0"][1]
                    if h + 1 < 8 and KPB:
                        k.wait(nc.sync, "act")
                        dma(nc.sync, srdq[(h + 1) % 2][:, 0:Wk], sc_dram[h + 1][:, 0:Wk],
                            "dsb%d" % ((h + 1) % 2))
                        dsb_v[h + 1] = k.sems["dsb%d" % ((h + 1) % 2)][1]
                    if h > 0 and not KPB:
                        k.wait(nc.sync, "act"); k.wait(nc.sync, "dve")
                        dma(nc.sync, srdq[h % 2][:, 0:Wk], sc_dram[h][:, 0:Wk],
                            "dsb%d" % (h % 2))
                        dsb_v[h] = k.sems["dsb%d" % (h % 2)][1]
                    sch = srdq[h % 2]
                    if b == 0 and tt == TT_DBG and h == 0:
                        k.wait(nc.sync, "dsb0")
                        dump("slab", sch[:, 0:2048], [128, 2048], BF16)
                    V("reduce_max", out=s_mx[:, 0:1], in_=s_cm[:, h * 4:h * 4 + nq], axis=AX)
                    V("tensor_scalar", s_mx[:, 1:2], s_mx[:, 0:1], -1.0, None, op0=ALU.mult)
                    k.wait(nc.scalar, "dve")
                    k.wait(nc.scalar, "dsb%d" % (h % 2), dsb_v[h])
                    A("activation", fI[:, 0:Wk], sch[:, 0:Wk], AF.Exp, scale=1.0,
                      bias=s_mx[:, 1:2], accum_out=s_mx[:, 2:3])
                    k.wait(nc.vector, "act")
                    V("reciprocal", s_mx[:, 3:4], s_mx[:, 2:3])
                    V("tensor_scalar", fI[:, 0:Wk], fI[:, 0:Wk], s_mx[:, 3:4], None, op0=ALU.mult)
                    k.wait(nc.tensor, "dve")
                    NSLOT = int(os.environ.get("KNSLOT", "3"))
                    TPCOLS = [3584, 2304, 2560, 3712]
                    tp_dve = {}
                    for j in range(tt + 1):
                        slot = j % NSLOT
                        pcol = TPCOLS[slot]
                        if slot in tp_dve:
                            k.wait(nc.tensor, "dve", tp_dve[slot])
                        ins = nc.tensor.transpose(PSA[:, pcol:pcol + 128],
                                                  fI[:, j * 128:(j + 1) * 128], ident[:])
                        edge(ins, "pe", nc.vector)
                        V("tensor_copy", slabT[:, j * 128:(j + 1) * 128], PSA[:, pcol:pcol + 128])
                        tp_dve[slot] = k.sems["dve"][1]
                    k.wait(nc.tensor, "dve")
                    ps_o = PSA[0:64, 3072:3200]
                    for j in range(tt + 1):
                        ins = nc.tensor.matmul(ps_o, vnat[:, j * 512 + h * 64:j * 512 + h * 64 + 64],
                                               slabT[:, j * 128:(j + 1) * 128],
                                               start=(j == 0), stop=(j == tt))
                    edge(ins, "pe", nc.vector)
                    k.wait(nc.vector, "dob")
                    ins = V("tensor_copy", obh[0:64, 0:128], ps_o)
                    k.wait(nc.sync, "dve")
                    dma(nc.sync, ol1a_dram[h][0:64, t0:t0 + 128], obh[0:64, 0:128], "dob")
                    if b == 0 and tt == TT_DBG and h == 0:
                        dump("oh", obh[0:64, 0:128], [64, 128], BF16)

        k.scope("A_wout")
        barrier()
        wout_sb = POOL_BF[:, 12288:16384]
        for c4 in range(4):
            dma(nc.gpsimd, wout_sb[:, c4 * 1024:(c4 + 1) * 1024], w_l1a_wout[c4], "dw")
        obuf = POOL_BF[:, 18688:20736]
        k.wait(nc.tensor, "dw")
        for tq5 in range(4):
            k.wait(nc.sync, "pe"); k.wait(nc.sync, "dve")
            for c4 in range(4):
                dma(nc.sync, obuf[0:64, c4 * 512:(c4 + 1) * 512],
                    ol1a_dram[2 * c4][0:64, tq5 * 512:(tq5 + 1) * 512])
                dma(nc.sync, obuf[64:128, c4 * 512:(c4 + 1) * 512],
                    ol1a_dram[2 * c4 + 1][0:64, tq5 * 512:(tq5 + 1) * 512])
            k.wait(nc.tensor, "dq"); k.wait(nc.tensor, "dve")
            for dt in range(ND):
                for c4 in range(4):
                    ins = nc.tensor.matmul(PSA[:, dt * 512:(dt + 1) * 512],
                                           wout_sb[:, c4 * 1024 + dt * 128:c4 * 1024 + (dt + 1) * 128],
                                           obuf[:, c4 * 512:(c4 + 1) * 512],
                                           start=(c4 == 0), stop=(c4 == 3))
            edge(ins, "pe", nc.vector)
            for dt in range(ND):
                V("tensor_tensor", xT[:, dt * T + tq5 * 512:dt * T + (tq5 + 1) * 512],
                  xT[:, dt * T + tq5 * 512:dt * T + (tq5 + 1) * 512],
                  PSA[:, dt * 512:(dt + 1) * 512], ALU.add)
            k.wait(nc.tensor, "dve")
        barrier()
    for b in range(NB):
        k.scope("tin%d" % b)
        transpose_in(b)
        if b == 0:
            dump("xT", xT[:], [128, ND * T])
        if "l0a" in blocks:
            k.scope("ln_a0")
            layer_norm()
            k.scope("L0A")
            l0_attn()
        if "l0f" in blocks:
            k.scope("ln_f0")
            layer_norm()
            k.scope("F0_router")
            if b == 0:
                dump("ylnT", ylnT[:], [128, ND * T], BF16)
            l0_ffn()
            if b == 0:
                dump("cw", cw_nat[:], [128, NT * 16])
                dump("xT2", xT[:], [128, ND * T])
        if "l1a" in blocks:
            k.scope("ln_a1")
            layer_norm()
            l1_attn(b)
        if "l1f" in blocks:
            k.scope("ln_f1")
            layer_norm()
            k.scope("L1F")
            l1_ffn()
        k.scope("tout%d" % b)
        transpose_out(b)
    k.scope(None)

    k.wait(nc.sync, "dq")

    return k, wdecl, taps



def make_consts():
    c = {}
    c["c_ident"] = np.eye(128, dtype=np.float32)
    c["c_identb"] = np.eye(128, dtype=np.float32).astype(ml_dtypes.bfloat16)
    c["c_ones_b"] = np.ones((128, 512), dtype=np.float32).astype(ml_dtypes.bfloat16)
    c["c_iota_col"] = np.arange(128, dtype=np.float32).reshape(128, 1)
    c["c_eps"] = np.full((128, 1), EPS_LN, dtype=np.float32)
    OFFS = list(range(-10, 11)) + list(range(-127, -117)) + list(range(118, 128))
    dg = np.zeros((41, 128, 128), dtype=np.float32)
    p = np.arange(128)
    for i, o in enumerate(OFFS):
        n = p + o
        valid = (n >= 0) & (n < 128)
        dg[i, p[valid], n[valid]] = 1.0
    c["c_diag"] = np.ascontiguousarray(np.transpose(dg, (1, 0, 2)).reshape(128, 41 * 128)).astype(ml_dtypes.bfloat16)
    th = (1.0 / 10000.0 ** (2.0 * np.arange(16) / 32)).astype(np.float64)
    t = np.arange(T, dtype=np.float64)
    irow = np.tile(np.tile(np.arange(16), 8), 1)
    ang64 = (t[None, :] - 64) * th[irow][:, None]
    c["g_csq64hi_c"] = np.cos(ang64).astype(np.float32)
    c["g_csq64hi_s"] = np.sin(ang64).astype(np.float32)
    SCALE = (64 + 32) ** -0.5
    angb = (2 * t[None, :] - 63) * th[irow][:, None]
    c["g_csband_c"] = (SCALE * np.cos(angb)).astype(np.float32)
    c["g_csband_s"] = (SCALE * np.sin(angb)).astype(np.float32)
    t2 = np.arange(128, dtype=np.float64)
    ang2 = t2[None, :] * th[irow][:, None]
    c["g_csq2_c"] = (SCALE * np.cos(ang2)).astype(np.float32)
    c["g_csq2_s"] = (SCALE * np.sin(ang2)).astype(np.float32)
    angw = np.arange(T, dtype=np.float64)[None, :] * th[:16][:, None]
    c["g_cswbar"] = np.concatenate([np.cos(angw), np.sin(angw)], axis=1).astype(np.float32)
    m = np.arange(32, dtype=np.float64)
    angm = m[:, None, None] * th[None, None, :16].reshape(1, 1, 16)
    angm = np.broadcast_to(angm, (32, 2, 16)).reshape(1024)
    c["g_csrot"] = np.broadcast_to(
        np.concatenate([np.cos(angm), np.sin(angm)]).astype(np.float32)[None, :], (128, 2048)).copy()
    ang64n = (t[:, None] - 64) * th[None, :]
    c["g_csq64n"] = (SCALE * np.concatenate(
        [np.cos(ang64n), np.sin(ang64n)], axis=1)).astype(np.float32)
    c["c_iota_s"] = np.broadcast_to(np.arange(T, dtype=np.float32), (128, T)).copy()
    c["c_icol_i"] = np.arange(128, dtype=np.int32).reshape(128, 1)
    c["c_iota_sm"] = np.broadcast_to(np.arange(256, dtype=np.float32), (128, 256)).copy()
    c["c_zeros2k"] = np.zeros((128, 2048), ml_dtypes.bfloat16)
    c["c_t94"] = (np.arange(128) <= 94).astype(np.float32).reshape(128, 1)
    c["c_t63"] = (np.arange(128) <= 63).astype(np.float32).reshape(128, 1)
    pz = np.arange(128)[:, None]; sz = np.arange(128)[None, :]
    zone = ((pz <= 63) & (sz <= 95)) | ((pz >= 64) & (pz <= 94) & (sz > pz) & (sz <= 95))
    c["c_zone"] = zone.astype(np.float32).astype(ml_dtypes.bfloat16)
    return c


def prep_weights(inp):
    w = {}
    sW1 = inp["L0_ffn_sW1"]
    w["w_sW1"] = np.ascontiguousarray(sW1.reshape(ND, 128, 1024))
    sW2 = inp["L0_ffn_sW2"]
    w["w_sW2"] = np.ascontiguousarray(sW2.reshape(ND, 128, 1024))
    w["w_eW1"] = np.ascontiguousarray(inp["L0_ffn_eW1"].reshape(15, ND, 128, 1024))
    w["w_eW2"] = np.ascontiguousarray(inp["L0_ffn_eW2"].reshape(15, ND, 128, 1024))
    w["w_routerW"] = np.ascontiguousarray(inp["L0_ffn_router_W"].reshape(ND, 128, 15))
    w["w_routerB"] = np.broadcast_to(inp["L0_ffn_router_b"][None, :], (128, 15)).copy()
    w["w_sb1c"] = np.ascontiguousarray(inp["L0_ffn_sb1"].reshape(ND, 128).T)
    w["w_sb2r"] = inp["L0_ffn_sb2"].reshape(1, D).copy()
    w["w_eb1c"] = np.ascontiguousarray(np.transpose(inp["L0_ffn_eb1"].reshape(15, ND, 128), (0, 2, 1)))
    w["w_eb2"] = inp["L0_ffn_eb2"].copy()
    Win = inp["L0_attn_W_in"]; Wout = inp["L0_attn_W_out"]
    w["w_l0a_WinT"] = np.ascontiguousarray(Win.T.reshape(ND, 128, 1024))
    w["w_l0a_Win"] = np.ascontiguousarray(Win.reshape(ND, 128, 1024))
    w["w_l0a_WoutT"] = np.ascontiguousarray(Wout.T.reshape(ND, 128, 1024))
    w["w_l0a_Wc"] = np.ascontiguousarray(inp["L0_attn_Wc"])
    w["w_l0a_binc"] = np.ascontiguousarray(inp["L0_attn_b_in"].reshape(ND, 128).T)
    w["w_l0a_binsl"] = np.ascontiguousarray(inp["L0_attn_b_in"].reshape(ND, 128, 1))
    w["w_l0a_bout"] = inp["L0_attn_b_out"].reshape(1, D).copy()
    w["w_l0a_A_rep"] = np.broadcast_to(inp["L0_attn_A"].reshape(1, 672), (128, 672)).copy()
    w["w_l0a_base_cols"] = np.ascontiguousarray(np.transpose(inp["L0_attn_base"], (2, 0, 1)).reshape(128, 168))
    w["w_l0a_alphas"] = inp["L0_attn_alphas"].reshape(1, 8).copy()
    w["w_l0a_Vcols"] = np.ascontiguousarray(np.transpose(inp["L0_attn_V"], (2, 0, 1)).reshape(128, 32))
    w["w_l1f_W1T"] = np.ascontiguousarray(inp["L1_ffn_W1"].T.reshape(ND, 128, 4096))
    w["w_l1f_W2T"] = np.ascontiguousarray(inp["L1_ffn_W2"].T.reshape(32, 128, 1024))
    w["w_l1f_routerWT"] = np.ascontiguousarray(inp["L1_ffn_router_W"].T.reshape(ND, 128, 512))
    w["w_l1f_routerBr"] = inp["L1_ffn_router_b"].reshape(1, 512).copy()
    w["w_l1f_rawUT"] = np.ascontiguousarray(inp["L1_ffn_raw_U"].T.reshape(ND, 128, 512))
    w["w_l1f_rawU_rows"] = np.ascontiguousarray(inp["L1_ffn_raw_U"].reshape(4, 128, 1024))
    w["w_l1f_rawV_rows"] = np.ascontiguousarray(inp["L1_ffn_raw_V"].reshape(4, 128, 1024))
    w["w_l1f_lngc"] = np.ascontiguousarray(inp["L1_ffn_ln_g"].reshape(ND, 128).T)
    w["w_l1f_lnbc"] = np.ascontiguousarray(inp["L1_ffn_ln_b"].reshape(ND, 128).T)
    w["w_l1f_gammac"] = np.ascontiguousarray(inp["L1_ffn_gamma"].reshape(ND, 128).T)
    w["w_l1a_dkv"] = np.ascontiguousarray(inp["L1_attn_W_dkv"].reshape(ND, 128, 256))
    w["w_l1a_uk"] = np.ascontiguousarray(inp["L1_attn_W_uk"].reshape(2, 128, 512))
    w["w_l1a_uv"] = np.ascontiguousarray(inp["L1_attn_W_uv"].reshape(2, 128, 512))
    w["w_l1a_dq"] = np.ascontiguousarray(inp["L1_attn_W_dq"].reshape(ND, 128, 512))
    w["w_l1a_uq"] = np.ascontiguousarray(inp["L1_attn_W_uq"].reshape(4, 128, 512))
    perm = np.concatenate([np.arange(8)[:, None] * 32 + np.arange(16)[None, :],
                           np.arange(8)[:, None] * 32 + 16 + np.arange(16)[None, :]]).reshape(2, 8, 16)
    perm = np.concatenate([perm[0].reshape(-1), perm[1].reshape(-1)])
    w["w_l1a_qrR"] = np.ascontiguousarray(inp["L1_attn_W_qr"][:, perm].reshape(4, 128, 256))
    w["w_l1a_qr"] = np.ascontiguousarray(inp["L1_attn_W_qr"].reshape(4, 128, 256))
    w["w_l1a_kr"] = np.ascontiguousarray(inp["L1_attn_W_kr"].reshape(ND, 128, 32))
    w["w_l1a_wout"] = np.ascontiguousarray(inp["L1_attn_W_out"].reshape(4, 128, 1024))
    w["w_l1a_iwq"] = np.ascontiguousarray(inp["L1_attn_idx_wq"].reshape(ND, 128, 256))
    w["w_l1a_iwk"] = np.ascontiguousarray(inp["L1_attn_idx_wk"].reshape(ND, 128, 64))
    w["w_l1a_iw"] = inp["L1_attn_idx_w"].reshape(1, 4).copy()
    return w


_PROG_CACHE = {}

def _get_prog(blocks):
    key = tuple(blocks)
    if key not in _PROG_CACHE:
        _PROG_CACHE[key] = build_program(blocks)
    return _PROG_CACHE[key]


def kernel(**inputs):
    blocks = ALL_BLOCKS
    k, wdecl, taps = _get_prog(blocks)
    consts = make_consts()
    w = prep_weights(inputs)
    x = inputs["L0_attn_x"]
    in_maps = []
    for core in range(8):
        m = dict(consts)
        m.update(w)
        m["x_in"] = np.ascontiguousarray(x[core * NB:(core + 1) * NB])
        in_maps.append(m)
    res = run_bass_kernel_spmd(k.nc, in_maps, core_ids=list(range(8)))
    out = np.concatenate([r["y_out"] for r in res.results], axis=0)
    return out
